# version 06
# baseline (speedup 1.0000x reference)
import math
import numpy as np
import concourse.bass as bass
import concourse.mybir as mybir
from concourse.bass_utils import run_bass_kernel_spmd
from contextlib import ExitStack

F32 = mybir.dt.float32
F32R = mybir.dt.float32r
BF16 = mybir.dt.bfloat16
AF = mybir.ActivationFunctionType
ALU = mybir.AluOpType
AX = mybir.AxisListType

D, H, NH, DH, T, E, R = 2048, 5632, 16, 128, 1024, 8, 16
NDK = D // 128
NHC = H // 128
HALF = NHC // 2
NTC = T // 128
TS = 512
SM = 1.0 / math.sqrt(DH)
STAGE = 3


class _Sem:
    __slots__ = ("name", "n", "h")
    def __init__(self, name):
        self.name, self.n, self.h = name, 0, None


class _B:
    def __init__(self):
        self.ops = []
        self.sems = []
        self._lw = {}
        self.auto_chain = {}

    def sem(self, name):
        s = _Sem(name)
        self.sems.append(s)
        return s

    def op(self, eng, fn, waits=(), inc=None):
        waits = list(waits)
        if eng in self.auto_chain:
            s = self.auto_chain[eng]
            waits.append((s, s.n))
            if inc is None:
                inc = (s, 1)
        w = []
        for (s, v) in waits:
            if v <= 0:
                continue
            k = (eng, s.name)
            if self._lw.get(k, -1) >= v:
                continue
            self._lw[k] = v
            w.append((s, v))
        if inc is not None:
            inc[0].n += inc[1]
        self.ops.append((eng, fn, w, inc))
        return inc[0].n if inc else None

    def emit(self, name, eng):
        for (e, fn, w, inc) in self.ops:
            if e != name:
                continue
            for (s, v) in w:
                eng.wait_ge(s.h, v)
            inst = fn(eng)
            if inc is not None:
                inst.then_inc(inc[0].h, inc[1])


def _tl(src2d):
    return src2d.rearrange("(a p) c -> p a c", p=128)


def _t3(dst2d, c):
    return dst2d.rearrange("p (a c) -> p a c", c=c)


def build_nc():
    nc = bass.Bass("TRN2", target_bir_lowering=False)

    xs = nc.dram_tensor("xs", [D, T], F32, kind="ExternalInput").ap()
    wq_d = nc.dram_tensor("wq", [D, D], F32, kind="ExternalInput").ap()
    wk_d = nc.dram_tensor("wk", [D, D], F32, kind="ExternalInput").ap()
    wv_d = nc.dram_tensor("wv", [D, D], F32, kind="ExternalInput").ap()
    wo_d = nc.dram_tensor("wo", [D, D], F32, kind="ExternalInput").ap()
    bqd = nc.dram_tensor("bqd", [128, NDK], F32, kind="ExternalInput").ap()
    bkd = nc.dram_tensor("bkd", [128, NDK], F32, kind="ExternalInput").ap()
    bvd = nc.dram_tensor("bvd", [1, D], F32, kind="ExternalInput").ap()
    wupT = nc.dram_tensor("wupT", [D, H], F32, kind="ExternalInput").ap()
    wgateT = nc.dram_tensor("wgateT", [D, H], F32, kind="ExternalInput").ap()
    wdownT = nc.dram_tensor("wdownT", [H, D], F32, kind="ExternalInput").ap()
    upAT = nc.dram_tensor("upAT", [D, 128], F32, kind="ExternalInput").ap()
    gateAT = nc.dram_tensor("gateAT", [D, 128], F32, kind="ExternalInput").ap()
    upBT = nc.dram_tensor("upBT", [128, H], F32, kind="ExternalInput").ap()
    gateBT = nc.dram_tensor("gateBT", [128, H], F32, kind="ExternalInput").ap()
    dnAT = nc.dram_tensor("dnAT", [H, 128], F32, kind="ExternalInput").ap()
    dnBT = nc.dram_tensor("dnBT", [128, D], F32, kind="ExternalInput").ap()
    gwt = nc.dram_tensor("gwt", [128, 128], F32, kind="ExternalInput").ap()
    identf = nc.dram_tensor("identf", [128, 128], F32, kind="ExternalInput").ap()
    ones_r = nc.dram_tensor("ones_r", [1, 128], F32, kind="ExternalInput").ap()
    ones_c = nc.dram_tensor("ones_c", [128, 1], F32, kind="ExternalInput").ap()
    ecol_d = nc.dram_tensor("ecol_d", [128, 1], F32, kind="ExternalInput").ap()
    iota8_d = nc.dram_tensor("iota8_d", [128, 8], F32, kind="ExternalInput").ap()
    amask_d = nc.dram_tensor("amask_d", [128, 16 * 128], F32, kind="ExternalInput").ap()
    out = nc.dram_tensor("out", [D, T], F32, kind="ExternalOutput").ap()

    qdram = nc.dram_tensor("qdram", [D, T], BF16).ap()
    cin = nc.dram_tensor("cin", [2048, 2048], BF16).ap()
    cout = nc.dram_tensor("cout", [4096, 2048], BF16).ap()
    trd = nc.dram_tensor("trd", [32, 128], F32).ap()
    gin = [nc.dram_tensor(f"gin{i}", [1, 2], F32).ap() for i in range(2)]
    gout = [nc.dram_tensor(f"gout{i}", [2, 2], F32).ap() for i in range(2)]

    P = _B()
    dA = P.sem("dA")
    dO = P.sem("dO")
    wA = [P.sem("wA0"), P.sem("wA1")]
    wB = [P.sem("wB0"), P.sem("wB1")]
    wP = [P.sem("wP0"), P.sem("wP1")]
    wV = [P.sem("wV0"), P.sem("wV1")]
    dK = [P.sem("dK0"), P.sem("dK1")]
    dH = [P.sem("dH0"), P.sem("dH1")]
    dOUT = P.sem("dOUT")
    cc = [P.sem(f"cc{i}") for i in range(3)]
    pe, sc, dv = P.sem("pe"), P.sem("sc"), P.sem("dv")
    P.auto_chain = {"dv": dv, "sc": sc}

    with ExitStack() as stk:
        en = stk.enter_context
        resid = en(nc.sbuf_tensor("resid", [128, NDK * 1024], F32))
        act = en(nc.sbuf_tensor("act", [128, NDK * 1024], BF16))
        arena = en(nc.sbuf_tensor("arena", [128, HALF * 1024], BF16))
        wa = en(nc.sbuf_tensor("wa", [128, 4 * 2048], BF16))
        wb = en(nc.sbuf_tensor("wb", [128, 2 * 1408], BF16))
        wsm = en(nc.sbuf_tensor("wsm", [128, 6 * 128], BF16))
        lnc = en(nc.sbuf_tensor("lnc", [128, 2048], BF16))
        sf = en(nc.sbuf_tensor("sf", [128, 2560], F32))
        sb = en(nc.sbuf_tensor("sb", [128, 10240], BF16))
        rowz = en(nc.sbuf_tensor("rowz", [1, 4], F32))
        smalls = en(nc.sbuf_tensor("smalls", [128, 64], F32))
        pack = en(nc.sbuf_tensor("pack", [128, 32], F32))
        trsb = en(nc.sbuf_tensor("trsb", [32, 128], F32))
        consts = en(nc.sbuf_tensor("consts", [128, 48], F32))
        cbf = en(nc.sbuf_tensor("cbf", [128, 16], BF16))
        crow = en(nc.sbuf_tensor("crow", [1, 192], F32))
        gstage = en(nc.sbuf_tensor("gstage", [1, 8], F32))
        kstg = en(nc.sbuf_tensor("kstg", [128, 2 * 512], BF16))
        gwtile = en(nc.sbuf_tensor("gwtile", [128, 128], BF16))
        identsb = en(nc.sbuf_tensor("identsb", [128, 128], F32))
        ps = [en(nc.psum_tensor(f"ps{i}", [128, 512], F32)) for i in range(4)]
        psQ = [en(nc.psum_tensor(f"psQ{i}", [128, 512], F32)) for i in range(4)]
        for s in P.sems:
            s.h = en(nc.semaphore(s.name))
        block = en(nc.Block())

        def rtile(dc, ts=None):
            a = resid[:, dc * 1024:(dc + 1) * 1024]
            return a if ts is None else a[:, ts * TS:(ts + 1) * TS]

        def ytile(dc, ts=None):
            a = act[:, dc * 1024:(dc + 1) * 1024]
            return a if ts is None else a[:, ts * TS:(ts + 1) * TS]

        ecol = consts[:, 0:1]
        iotasb = consts[:, 8:16]
        bq_s = consts[:, 16:32]
        bk_s = consts[:, 32:48]
        ones_cb = cbf[:, 0:1]
        ones_rb = crow[:, 0:64].bitcast(BF16)
        m0A, m1A = smalls[:, 0:8], smalls[:, 8:16]
        e0A, e1A = smalls[:, 16:24], smalls[:, 24:32]
        eqs, tmp8, lmt = smalls[:, 32:40], smalls[:, 40:48], smalls[:, 48:56]
        s01 = smalls[:, 56:58]
        s01r = smalls[:, 58:60].bitcast(F32R)
        rzb = smalls[:, 60:62]

        row_mean = sb[0:1, 0:2048].bitcast(F32)
        row_var = sb[0:1, 2048:4096].bitcast(F32)
        rstd_hi = sb[0:1, 4096:5120]
        rstd_lo = sb[0:1, 5120:6144]
        bneg_hi = sb[0:1, 6144:7168]
        bneg_lo = sb[0:1, 7168:8192]
        w_hi = [sb[0:1, 0:1024], sb[0:1, 2048:3072]]
        w_lo = [sb[0:1, 1024:2048], sb[0:1, 3072:4096]]
        row_wf = [sb[0:1, 4096:6144].bitcast(F32), sb[0:1, 6144:8192].bitcast(F32)]
        row_idx = [sb[0:1, 8192:9216], sb[0:1, 9216:10240]]
        linv_f = sb[0:1, 4096:6144].bitcast(F32)
        linv_hi = sb[0:1, 6144:7168]
        linv_lo = sb[0:1, 7168:8192]
        row_bv = sb[0:1, 8192:10240]

        rstd_b = sf[:, 0:1024]
        bneg_b = sf[:, 1024:2048]
        bA = sf[:, 0:512]
        bB = sf[:, 512:1024]
        s1t = sf[:, 1024:1536]
        t2t = sf[:, 1536:2048]
        siluX = sf[:, 2048:2560]
        bv_b = sf[:, 0:2048]
        linv_b = sf[:, 0:1024]

        def rho(i, ts):
            o = (i * 2 + ts) * 512
            return sb[:, o:o + 512]

        def crowb(i, ts):
            o = 4096 + (i * 2 + ts) * 512
            return sb[:, o:o + 512]

        def Mm(i, ts):
            o = 6144 + (i * 2 + ts) * 512
            return sb[:, o:o + 512]

        def probsT(buf):
            return sb[:, buf * 1024:(buf + 1) * 1024]
        amask_sb = sb[:, 2048:4096]
        cht = [sb[:, 8192 + i * 512:8192 + (i + 1) * 512] for i in range(4)]

        def mixed(k, ts):
            o = k * 1024 + ts * 512
            return arena[:, o:o + 512]
        q_t = [arena[:, i * 1024:(i + 1) * 1024] for i in range(2)]
        k_t = [arena[:, 2048 + i * 2048:2048 + (i + 1) * 2048] for i in range(2)]
        v_t = [arena[:, 6144 + i * 2048:6144 + (i + 1) * 2048] for i in range(2)]

        def wtA(slot, buf):
            o = (slot * 2 + buf) * 2048
            return wa[:, o:o + 2048]

        def wtP(buf):
            return wa[:, buf * 2048:buf * 2048 + 2048]

        def wtB(buf):
            return wb[:, buf * 1408:buf * 1408 + 1408]

        def wvt(buf):
            return wb[:, buf * 1408:buf * 1408 + 512]

        def btl(slot, buf):
            o = (slot * 2 + buf) * 128
            return wsm[:, o:o + 128]

        upA_sb = lnc[:, 0:2048]
        dnB_sb = lnc[:, 0:2048]
        xsq = [lnc[:, 0:1024]] * 2
        xbf = [lnc[:, 1024:2048]] * 2

        ef = P.op
        S = {"y": 0}

        ef('sy', lambda e: e.dma_start(out=_t3(resid[:, :], T), in_=_tl(xs)), inc=(dA, 16))
        mx = dA.n
        ef('sy', lambda e: e.dma_start(out=identsb[:, :], in_=identf), inc=(dA, 16))
        ef('sy', lambda e: e.dma_start(out=consts[:, 0:1], in_=ecol_d), inc=(dA, 16))
        ef('sy', lambda e: e.dma_start(out=consts[:, 8:16], in_=iota8_d), inc=(dA, 16))
        ef('sy', lambda e: e.dma_start(out=consts[:, 16:32], in_=bqd), inc=(dA, 16))
        ef('sy', lambda e: e.dma_start(out=consts[:, 32:48], in_=bkd), inc=(dA, 16))
        mca = dA.n
        mx = dA.n
        ef('gp', lambda e: e.dma_start(out=ones_cb, in_=ones_c), inc=(dO, 16))
        ef('gp', lambda e: e.dma_start(out=ones_rb, in_=ones_r), inc=(dO, 16))
        ef('gp', lambda e: e.dma_start(out=gwtile[:, :], in_=gwt), inc=(dO, 16))
        mcg = dO.n

        def layer_norm(base_pe, base_dv, base_sc):
            mson = [0] * NDK
            for dc in range(NDK):
                b = dc % 2
                wpe = max(mson[dc - 1] if dc >= 1 else 0, base_pe)
                ef('dv', lambda e, dc=dc, b=b: e.tensor_tensor(xsq[b], rtile(dc), rtile(dc), ALU.mult),
                   waits=[(dA, mx), (pe, wpe), (dv, base_dv)], inc=(dv, 1))
                mdv = dv.n
                ef('sc', lambda e, dc=dc, b=b: e.copy(xbf[b], rtile(dc)),
                   waits=[(dA, mx), (pe, wpe), (dv, base_dv), (sc, base_sc)], inc=(sc, 1))
                msc = sc.n
                for h2 in range(2):
                    tgt = ps[h2]
                    ef('pe', lambda e, b=b, h2=h2, tgt=tgt, dc=dc: e.matmul(
                        tgt[0:1, :], ones_cb, xsq[b][:, h2 * 512:(h2 + 1) * 512],
                        start=(dc == 0), stop=(dc == NDK - 1)),
                       waits=[(dv, mdv), (dO, mcg), (dv, base_dv), (sc, base_sc)])
                    tgt2 = ps[2 + h2]
                    mson[dc] = ef('pe', lambda e, b=b, h2=h2, tgt2=tgt2, dc=dc: e.matmul(
                        tgt2[0:1, :], ones_cb, xbf[b][:, h2 * 512:(h2 + 1) * 512],
                        start=(dc == 0), stop=(dc == NDK - 1)),
                       waits=[(sc, msc)], inc=(pe, 1))
            m_stats = pe.n
            ef('sc', lambda e: e.activation(row_mean[:, 0:512], ps[2][0:1, :], AF.Copy, scale=1.0 / D),
               waits=[(pe, m_stats)], inc=(sc, 1))
            ef('sc', lambda e: e.activation(row_mean[:, 512:1024], ps[3][0:1, :], AF.Copy, scale=1.0 / D), inc=(sc, 1))
            ef('sc', lambda e: e.activation(row_var[:, 0:512], ps[0][0:1, :], AF.Copy, scale=1.0 / D), inc=(sc, 1))
            ef('sc', lambda e: e.activation(row_var[:, 512:1024], ps[1][0:1, :], AF.Copy, scale=1.0 / D), inc=(sc, 1))
            mrow = sc.n
            ef('dv', lambda e: e.tensor_tensor(ps[0][0:1, :], row_mean[:, 0:512], row_mean[:, 0:512], ALU.mult),
               waits=[(sc, mrow)], inc=(dv, 1))
            ef('dv', lambda e: e.tensor_tensor(ps[1][0:1, :], row_mean[:, 512:1024], row_mean[:, 512:1024], ALU.mult), inc=(dv, 1))
            ef('dv', lambda e: e.tensor_tensor(row_var[:, 0:512], row_var[:, 0:512], ps[0][0:1, :], ALU.subtract), inc=(dv, 1))
            ef('dv', lambda e: e.tensor_tensor(row_var[:, 512:1024], row_var[:, 512:1024], ps[1][0:1, :], ALU.subtract), inc=(dv, 1))
            mv = dv.n
            ef('dv', lambda e: e.tensor_single_scalar(row_var[:, :], row_var[:, :], 1e-5, ALU.add),
               waits=[(dv, mv)], inc=(dv, 1))
            ef('sc', lambda e: e.activation(row_var[:, :], row_var[:, :], AF.Sqrt),
               waits=[(dv, dv.n)], inc=(sc, 1))
            ef('dv', lambda e: e.reciprocal(row_var[:, :], row_var[:, :]), waits=[(sc, sc.n)], inc=(dv, 1))
            ef('dv', lambda e: e.scalar_tensor_tensor(row_mean[:, :], row_mean[:, :], -1.0,
                                                      row_var[:, :], ALU.mult, ALU.mult), inc=(dv, 1))
            ef('dv', lambda e: e.tensor_copy(rstd_hi, row_var[:, :]), inc=(dv, 1))
            ef('dv', lambda e: e.tensor_tensor(rstd_lo, row_var[:, :], rstd_hi, ALU.subtract), inc=(dv, 1))
            ef('dv', lambda e: e.tensor_copy(bneg_hi, row_mean[:, :]), inc=(dv, 1))
            ef('dv', lambda e: e.tensor_tensor(bneg_lo, row_mean[:, :], bneg_hi, ALU.subtract), inc=(dv, 1))
            mrs = dv.n
            for h2 in range(2):
                ef('pe', lambda e, h2=h2: e.matmul(ps[h2][:, :], ones_rb,
                                                   rstd_hi[:, h2 * 512:(h2 + 1) * 512],
                                                   start=True, stop=False),
                   waits=[(dv, mrs)])
                ef('pe', lambda e, h2=h2: e.matmul(ps[h2][:, :], ones_rb,
                                                   rstd_lo[:, h2 * 512:(h2 + 1) * 512],
                                                   start=False, stop=True), inc=(pe, 1))
                ef('pe', lambda e, h2=h2: e.matmul(ps[2 + h2][:, :], ones_rb,
                                                   bneg_hi[:, h2 * 512:(h2 + 1) * 512],
                                                   start=True, stop=False))
                ef('pe', lambda e, h2=h2: e.matmul(ps[2 + h2][:, :], ones_rb,
                                                   bneg_lo[:, h2 * 512:(h2 + 1) * 512],
                                                   start=False, stop=True), inc=(pe, 1))
            mbc = pe.n
            ef('sc', lambda e: e.copy(rstd_b[:, 0:512], ps[0][:, :]), waits=[(pe, mbc)], inc=(sc, 1))
            ef('sc', lambda e: e.copy(rstd_b[:, 512:1024], ps[1][:, :]), inc=(sc, 1))
            ef('sc', lambda e: e.copy(bneg_b[:, 0:512], ps[2][:, :]), inc=(sc, 1))
            ef('sc', lambda e: e.copy(bneg_b[:, 512:1024], ps[3][:, :]), inc=(sc, 1))
            mbce = sc.n
            mnorm = [0] * NDK
            for dc in range(NDK):
                pa, pb = ps[2 * (dc % 2)], ps[1 + 2 * (dc % 2)]
                wv_ = max(mnorm[dc - 2] if dc >= 2 else 0, 0)
                m = 0
                for h2 in range(2):
                    tgt = pa if h2 == 0 else pb
                    ef('dv', lambda e, dc=dc, h2=h2, tgt=tgt: e.tensor_tensor(
                        tgt[:, :], rtile(dc)[:, h2 * 512:(h2 + 1) * 512],
                        rstd_b[:, h2 * 512:(h2 + 1) * 512], ALU.mult),
                       waits=[(sc, mbce), (dv, wv_)])
                    m = ef('dv', lambda e, dc=dc, h2=h2, tgt=tgt: e.tensor_tensor(
                        ytile(dc)[:, h2 * 512:(h2 + 1) * 512], tgt[:, :],
                        bneg_b[:, h2 * 512:(h2 + 1) * 512], ALU.add), inc=(dv, 1))
                mnorm[dc] = m
            S["y"] = dv.n
            return dv.n

        def gating(call):
            my = S["y"]
            mls = [0] * NTC
            for tc in range(NTC):
                Lp = ps[tc % 2]
                wdv = mls[tc - 2] if tc >= 2 else 0
                for dk in range(NDK):
                    ef('pe', lambda e, tc=tc, dk=dk, Lp=Lp: e.matmul(
                        Lp[:, 0:8], ytile(dk)[:, tc * 128:(tc + 1) * 128],
                        gwtile[:, dk * 8:(dk + 1) * 8],
                        start=(dk == 0), stop=(dk == NDK - 1)),
                       waits=[(dv, max(my, wdv)), (dO, mcg)],
                       inc=(pe, 1) if dk == NDK - 1 else None)
                mL = pe.n
                ef('dv', lambda e, tc=tc, Lp=Lp: e.tensor_reduce(m0A[:, tc:tc + 1], Lp[:, 0:8], AX.X, ALU.max),
                   waits=[(pe, mL), (dA, mca)], inc=(dv, 1))
                ef('sc', lambda e, tc=tc: e.activation(e0A[:, tc:tc + 1], m0A[:, tc:tc + 1], AF.Exp),
                   waits=[(dv, dv.n)], inc=(sc, 1))
                ef('dv', lambda e, tc=tc, Lp=Lp: e.tensor_single_scalar(eqs, Lp[:, 0:8], m0A[:, tc:tc + 1], ALU.is_equal))
                ef('dv', lambda e: e.scalar_tensor_tensor(tmp8, eqs, -1000.0, iotasb, ALU.mult, ALU.add))
                ef('dv', lambda e, tc=tc: e.tensor_reduce(pack[:, tc:tc + 1], tmp8, AX.X, ALU.min))
                ef('dv', lambda e, tc=tc: e.tensor_single_scalar(pack[:, tc:tc + 1], pack[:, tc:tc + 1], 1000.0, ALU.add))
                ef('dv', lambda e, tc=tc: e.tensor_single_scalar(eqs, iotasb, pack[:, tc:tc + 1], ALU.is_equal))
                ef('dv', lambda e, Lp=Lp: e.scalar_tensor_tensor(lmt, eqs, -1e30, Lp[:, 0:8], ALU.mult, ALU.add),
                   inc=(dv, 1))
                mls[tc] = dv.n
                ef('dv', lambda e, tc=tc: e.tensor_reduce(m1A[:, tc:tc + 1], lmt, AX.X, ALU.max), inc=(dv, 1))
                ef('sc', lambda e, tc=tc: e.activation(e1A[:, tc:tc + 1], m1A[:, tc:tc + 1], AF.Exp),
                   waits=[(dv, dv.n)], inc=(sc, 1))
                ef('dv', lambda e, tc=tc: e.tensor_single_scalar(eqs, lmt, m1A[:, tc:tc + 1], ALU.is_equal))
                ef('dv', lambda e: e.scalar_tensor_tensor(tmp8, eqs, -1000.0, iotasb, ALU.mult, ALU.add))
                ef('dv', lambda e, tc=tc: e.tensor_reduce(pack[:, 8 + tc:9 + tc], tmp8, AX.X, ALU.min))
                ef('dv', lambda e, tc=tc: e.tensor_single_scalar(pack[:, 8 + tc:9 + tc], pack[:, 8 + tc:9 + tc],
                                                                 1000.0, ALU.add), inc=(dv, 1))
            msce = sc.n
            s01hl = smalls[:, 56:58].bitcast(BF16)
            ef('dv', lambda e: e.tensor_reduce(s01[:, 0:1], e0A, AX.X, ALU.add),
               waits=[(sc, msce)], inc=(dv, 1))
            ef('dv', lambda e: e.tensor_reduce(s01[:, 1:2], e1A, AX.X, ALU.add), inc=(dv, 1))
            ef('dv', lambda e: e.tensor_copy(s01hl[:, 0:2], s01), inc=(dv, 1))
            ef('dv', lambda e: e.tensor_tensor(s01hl[:, 2:4], s01, s01hl[:, 0:2], ALU.subtract), inc=(dv, 1))
            mS = dv.n
            ef('pe', lambda e: e.matmul(ps[2][0:1, 0:4], ones_cb, s01hl, start=True, stop=True),
               waits=[(dv, mS), (dO, mcg)], inc=(pe, 1))
            ef('sc', lambda e: e.copy(gstage[:, 0:4], ps[2][0:1, 0:4]), waits=[(pe, pe.n)], inc=(sc, 1))
            ef('dv', lambda e: e.tensor_tensor(gstage[:, 4:6], gstage[:, 0:2], gstage[:, 2:4], ALU.add),
               waits=[(sc, sc.n)], inc=(dv, 1))
            ef('sy', lambda e, call=call: e.dma_start(out=gin[call], in_=gstage[:, 4:6]),
               waits=[(dv, dv.n)], inc=(dA, 16))
            mgin = dA.n
            ef('gp', lambda e, call=call: e.collective_compute(
                "AllGather", ALU.bypass, replica_groups=[[0, 1], [2, 3], [4, 5], [6, 7]],
                ins=[gin[call].opt()], outs=[gout[call].opt()]),
               waits=[(dA, mgin)], inc=(cc[call], 1))
            ef('sy', lambda e, call=call: e.dma_start(out=gstage[:, 0:2], in_=gout[call][0:1, :]),
               waits=[(cc[call], 1)], inc=(dA, 16))
            ef('sy', lambda e, call=call: e.dma_start(out=gstage[:, 2:4], in_=gout[call][1:2, :]),
               inc=(dA, 16))
            mgz = dA.n
            ef('dv', lambda e: e.tensor_tensor(gstage[:, 6:8], gstage[:, 0:2], gstage[:, 2:4], ALU.add),
               waits=[(dA, mgz)], inc=(dv, 1))
            ef('dv', lambda e: e.reciprocal(gstage[:, 6:8], gstage[:, 6:8]), inc=(dv, 1))
            rzhl = rowz.bitcast(BF16)
            ef('dv', lambda e: e.tensor_copy(rzhl[:, 0:2], gstage[:, 6:8]), inc=(dv, 1))
            ef('dv', lambda e: e.tensor_tensor(rzhl[:, 2:4], gstage[:, 6:8], rzhl[:, 0:2], ALU.subtract),
               inc=(dv, 1))
            mrz = dv.n
            ef('pe', lambda e: e.matmul(ps[3][:, 0:2], ones_rb, rzhl[:, 0:2], start=True, stop=False),
               waits=[(dv, mrz)])
            ef('pe', lambda e: e.matmul(ps[3][:, 0:2], ones_rb, rzhl[:, 2:4], start=False, stop=True),
               inc=(pe, 1))
            ef('sc', lambda e: e.copy(rzb, ps[3][:, 0:2]), waits=[(pe, pe.n)], inc=(sc, 1))
            ef('dv', lambda e: e.tensor_single_scalar(pack[:, 16:24], e0A, rzb[:, 0:1], ALU.mult),
               waits=[(sc, sc.n)])
            ef('dv', lambda e: e.tensor_single_scalar(pack[:, 24:32], e1A, rzb[:, 1:2], ALU.mult),
               inc=(dv, 1))
            mpk = dv.n
            ef('pe', lambda e: e.transpose(ps[2][0:32, 0:128], pack[:, :], identsb[:, :]),
               waits=[(dv, mpk), (dA, mca)], inc=(pe, 1))
            ef('sc', lambda e: e.copy(trsb[:, :], ps[2][0:32, 0:128]), waits=[(pe, pe.n)], inc=(sc, 1))
            ef('sy', lambda e: e.dma_start(out=trd, in_=trsb[:, :]), waits=[(sc, sc.n)], inc=(dA, 16))
            mtr = dA.n
            ef('sy', lambda e: e.dma_start(out=row_wf[0],
                                           in_=trd[16:24, :].rearrange("a b -> (a b)").unsqueeze(0)),
               waits=[(dA, mtr)], inc=(dA, 16))
            ef('sy', lambda e: e.dma_start(out=row_wf[1],
                                           in_=trd[24:32, :].rearrange("a b -> (a b)").unsqueeze(0)),
               inc=(dA, 16))
            mwf = dA.n
            for wi in range(2):
                ef('dv', lambda e, wi=wi: e.tensor_copy(w_hi[wi], row_wf[wi]),
                   waits=[(dA, mwf)], inc=(dv, 1))
                ef('dv', lambda e, wi=wi: e.tensor_tensor(w_lo[wi], row_wf[wi], w_hi[wi], ALU.subtract),
                   inc=(dv, 1))
            mwhl = dv.n
            for ts in range(2):
                for wi in range(2):
                    tgt = ps[2 + wi]
                    ef('pe', lambda e, ts=ts, wi=wi, tgt=tgt: e.matmul(
                        tgt[:, :], ones_rb, w_hi[wi][:, ts * 512:(ts + 1) * 512],
                        start=True, stop=False),
                       waits=[(dv, mwhl), (sc, sc.n)])
                    ef('pe', lambda e, ts=ts, wi=wi, tgt=tgt: e.matmul(
                        tgt[:, :], ones_rb, w_lo[wi][:, ts * 512:(ts + 1) * 512],
                        start=False, stop=True), inc=(pe, 1))
                mcb = pe.n
                ef('sc', lambda e, ts=ts: e.copy(crowb(0, ts), ps[2][:, :]), waits=[(pe, mcb)], inc=(sc, 1))
                ef('sc', lambda e, ts=ts: e.copy(crowb(1, ts), ps[3][:, :]), inc=(sc, 1))
            ef('gp', lambda e: e.dma_start(out=row_idx[0],
                                           in_=trd[0:8, :].rearrange("a b -> (a b)").unsqueeze(0)),
               waits=[(dA, mtr)], inc=(dO, 16))
            ef('gp', lambda e: e.dma_start(out=row_idx[1],
                                           in_=trd[8:16, :].rearrange("a b -> (a b)").unsqueeze(0)),
               inc=(dO, 16))
            mrows = dO.n
            mmask_prev_dv = 0
            for ts in range(2):
                ef('pe', lambda e, ts=ts: e.matmul(ps[0][:, :], ones_rb,
                                                   row_idx[0][:, ts * 512:(ts + 1) * 512],
                                                   start=True, stop=True),
                   waits=[(dO, mrows), (dv, mmask_prev_dv)], inc=(pe, 1))
                ef('pe', lambda e, ts=ts: e.matmul(ps[1][:, :], ones_rb,
                                                   row_idx[1][:, ts * 512:(ts + 1) * 512],
                                                   start=True, stop=True), inc=(pe, 1))
                mb = pe.n
                ef('dv', lambda e, ts=ts: e.tensor_scalar(Mm(0, ts), ps[0][:, :], ecol, 2.0,
                                                          ALU.is_equal, ALU.mult),
                   waits=[(pe, mb)], inc=(dv, 1))
                ef('dv', lambda e, ts=ts: e.tensor_scalar(Mm(1, ts), ps[1][:, :], ecol, 2.0,
                                                          ALU.is_equal, ALU.mult), inc=(dv, 1))
                mmask_prev_dv = dv.n
            return dv.n, sc.n

        def ffn(call):
            base_pe, base_dv, base_sc = pe.n, dv.n, sc.n
            layer_norm(base_pe, base_dv, base_sc)
            mmask_dv, mmask_sc = gating(call)

            for w_i, Ad in enumerate((upAT, gateAT)):
                ef('gp', lambda e, Ad=Ad: e.dma_start(out=_t3(upA_sb, 128), in_=_tl(Ad)),
                   waits=[(pe, pe.n)], inc=(dO, 16))
                mA = dO.n
                for ts in range(2):
                    tgt = ps[2 * ts + (w_i % 2)]
                    for dk in range(NDK):
                        ef('pe', lambda e, ts=ts, dk=dk, tgt=tgt: e.matmul(
                            tgt[:, :], upA_sb[:, dk * 128:(dk + 1) * 128], ytile(dk, ts),
                            start=(dk == 0), stop=(dk == NDK - 1)),
                           waits=[(dO, mA), (dv, mmask_dv), (sc, mmask_sc)],
                           inc=(pe, 1) if dk == NDK - 1 else None)
                    mr = pe.n
                    ef('dv', lambda e, ts=ts, w_i=w_i, tgt=tgt: e.tensor_tensor(
                        rho(2 * w_i, ts), tgt[:, :], Mm(0, ts), ALU.mult),
                       waits=[(pe, mr)], inc=(dv, 1))
                    ef('dv', lambda e, ts=ts, tgt=tgt: e.tensor_tensor(
                        cht[0], tgt[:, :], Mm(1, ts), ALU.mult), inc=(dv, 1))
                    ef('dv', lambda e, ts=ts, w_i=w_i: e.tensor_tensor(
                        rho(2 * w_i + 1, ts), cht[0], rho(2 * w_i, ts), ALU.subtract), inc=(dv, 1))
            mrho = dv.n

            wload = [0] * NHC
            wfree = [0] * NHC

            def load_hc(hc):
                wpe = max(wfree[hc - 2] if hc >= 2 else 0, base_pe)
                sem = wA[hc % 2]
                ef('gp', lambda e, hc=hc: e.dma_start(
                    out=_t3(wtA(0, hc % 2), 128), in_=_tl(wupT[:, hc * 128:(hc + 1) * 128])),
                   waits=[(pe, wpe)], inc=(sem, 16))
                ef('gp', lambda e, hc=hc: e.dma_start(
                    out=_t3(wtA(1, hc % 2), 128), in_=_tl(wgateT[:, hc * 128:(hc + 1) * 128])),
                   inc=(sem, 16))
                ef('gp', lambda e, hc=hc: e.dma_start(
                    out=btl(0, hc % 2), in_=upBT[:, hc * 128:(hc + 1) * 128]), inc=(sem, 16))
                ef('gp', lambda e, hc=hc: e.dma_start(
                    out=btl(1, hc % 2), in_=gateBT[:, hc * 128:(hc + 1) * 128]), inc=(sem, 16))
                ef('gp', lambda e, hc=hc: e.dma_start(
                    out=_t3(btl(2, hc % 2), 128), in_=_tl(dnAT[hc * 128:(hc + 1) * 128, :])), inc=(sem, 16))
                wload[hc] = sem.n

            load_hc(0)
            load_hc(1)
            slA_free = [0, 0]
            slB_free = [0, 0]
            cht_free = [0]
            silu0_free = [0]
            silu1_free = [0]
            bA_free = [0]
            bB_free = [0]
            mdnb = [0]
            mrho2 = [0]
            bfree = [0] * NDK
            bfree2 = [0] * 32
            psfree_dv = [0]

            for half in range(2):
                for k in range(HALF):
                    hc = half * HALF + k
                    par = hc % 2
                    for ts in range(2):
                        pA, pB, pDu, pDg = ps[0], ps[1], ps[2], ps[3]
                        for dk in range(NDK):
                            ef('pe', lambda e, ts=ts, dk=dk, par=par: e.matmul(
                                pA[:, :], wtA(0, par)[:, dk * 128:(dk + 1) * 128], ytile(dk, ts),
                                start=(dk == 0), stop=False),
                               waits=([(wA[par], wload[hc]), (sc, slA_free[0]), (dv, mrho),
                                       (dv, psfree_dv[0])]
                                      if dk == 0 else []))
                        ef('pe', lambda e, ts=ts, par=par: e.matmul(
                            pA[:, :], btl(0, par), rho(0, ts), start=False, stop=True),
                           inc=(pe, 1))
                        mAup = pe.n
                        for dk in range(NDK):
                            ef('pe', lambda e, ts=ts, dk=dk, par=par: e.matmul(
                                pB[:, :], wtA(1, par)[:, dk * 128:(dk + 1) * 128], ytile(dk, ts),
                                start=(dk == 0), stop=False),
                               waits=([(sc, slA_free[1])] if dk == 0 else []))
                        ef('pe', lambda e, ts=ts, par=par: e.matmul(
                            pB[:, :], btl(1, par), rho(2, ts), start=False, stop=True),
                           inc=(pe, 1))
                        mAg = pe.n
                        ef('pe', lambda e, ts=ts, par=par: e.matmul(
                            pDu[:, :], btl(0, par), rho(1, ts), start=True, stop=True),
                           waits=[(dv, slB_free[0]), (dv, psfree_dv[0])], inc=(pe, 1))
                        mDu = pe.n
                        ef('pe', lambda e, ts=ts, par=par: e.matmul(
                            pDg[:, :], btl(1, par), rho(3, ts), start=True, stop=True),
                           waits=[(dv, slB_free[1]), (dv, psfree_dv[0])], inc=(pe, 1))
                        mDg = pe.n
                        ef('sc', lambda e: e.copy(bA, pA[:, :]),
                           waits=[(pe, mAup), (dv, bA_free[0])], inc=(sc, 1))
                        slA_free[0] = sc.n
                        ef('sc', lambda e: e.copy(bB, pB[:, :]),
                           waits=[(pe, mAg), (dv, bB_free[0])], inc=(sc, 1))
                        slA_free[1] = sc.n
                        ef('sc', lambda e: e.activation(siluX, bA, AF.Silu),
                           waits=[(dv, silu0_free[0])], inc=(sc, 1))
                        ms0 = sc.n
                        ef('dv', lambda e: e.tensor_tensor(cht[0], siluX, bB, ALU.mult),
                           waits=[(sc, ms0), (pe, cht_free[0])], inc=(dv, 1))
                        silu0_free[0] = dv.n
                        ef('dv', lambda e: e.tensor_tensor(s1t, bA, pDu[:, :], ALU.add),
                           waits=[(pe, mDu)], inc=(dv, 1))
                        slB_free[0] = dv.n
                        bA_free[0] = dv.n
                        ef('sc', lambda e: e.activation(siluX, s1t, AF.Silu),
                           waits=[(dv, dv.n)], inc=(sc, 1))
                        ms1 = sc.n
                        ef('dv', lambda e: e.tensor_tensor(t2t, bB, pDg[:, :], ALU.add),
                           waits=[(pe, mDg)], inc=(dv, 1))
                        slB_free[1] = dv.n
                        bB_free[0] = dv.n
                        ef('dv', lambda e: e.tensor_tensor(cht[1], siluX, t2t, ALU.mult),
                           waits=[(sc, ms1)], inc=(dv, 1))
                        silu0_free[0] = dv.n
                        silu1_free[0] = dv.n
                        ef('dv', lambda e, ts=ts: e.tensor_tensor(cht[2], cht[0], crowb(0, ts), ALU.mult))
                        ef('dv', lambda e, ts=ts: e.tensor_tensor(cht[3], cht[1], crowb(1, ts), ALU.mult))
                        ef('dv', lambda e, k=k, ts=ts: e.tensor_tensor(
                            mixed(k, ts), cht[2], cht[3], ALU.add), inc=(dv, 1))
                        mch = dv.n
                        ef('pe', lambda e, hc=hc, ts=ts, par=par: e.matmul(
                            psQ[ts][:, :], btl(2, par), cht[2],
                            start=(hc == 0), stop=(hc == NHC - 1)),
                           waits=[(dv, mch)])
                        ef('pe', lambda e, hc=hc, ts=ts, par=par: e.matmul(
                            psQ[2 + ts][:, :], btl(2, par), cht[3],
                            start=(hc == 0), stop=(hc == NHC - 1)), inc=(pe, 1))
                        cht_free[0] = pe.n
                        if ts == 1:
                            wfree[hc] = pe.n
                    if hc + 2 < NHC:
                        load_hc(hc + 2)

                mmixed = dv.n
                if half == 1:
                    ef('gp', lambda e: e.dma_start(out=dnB_sb, in_=dnBT),
                       waits=[(pe, pe.n)], inc=(dO, 16))
                    mdnb[0] = dO.n
                    for ts in range(2):
                        ef('dv', lambda e, ts=ts: e.tensor_tensor(cht[0], psQ[ts][:, :], Mm(0, ts), ALU.mult),
                           waits=[(pe, cht_free[0])], inc=(dv, 1))
                        ef('dv', lambda e, ts=ts: e.tensor_tensor(cht[1], psQ[2 + ts][:, :], Mm(1, ts), ALU.mult),
                           inc=(dv, 1))
                        ef('dv', lambda e, ts=ts: e.tensor_tensor(rho(0, ts), cht[0], cht[1], ALU.add),
                           inc=(dv, 1))
                    mrho2[0] = dv.n
                bload2 = [0] * 32
                bfree2_prev = list(bfree2) if half == 1 else [0] * 32

                def load_b(dc, pc, half=half):
                    idx = 2 * dc + pc
                    prev = bfree2[idx - 2] if idx >= 2 else (bfree2_prev[idx] if half == 1 else 0)
                    ef('gp', lambda e, dc=dc, pc=pc, half=half: e.dma_start(
                        out=_t3(wtB((2 * dc + pc) % 2), 128),
                        in_=_tl(wdownT[half * 2816 + pc * 1408:half * 2816 + (pc + 1) * 1408,
                                       dc * 128:(dc + 1) * 128])),
                       waits=[(pe, max(prev, base_pe)), (dv, base_dv)], inc=(wB[idx % 2], 16))
                    bload2[idx] = wB[idx % 2].n

                load_b(0, 0)
                load_b(0, 1)
                for dc in range(NDK):
                    pf = [ps[(dc % 2) * 2], ps[(dc % 2) * 2 + 1]]
                    mpc = [0, 0]
                    for k in range(HALF):
                        pc = k // 11
                        kk = k % 11
                        buf = (2 * dc + pc) % 2
                        for ts in range(2):
                            isl = (half == 0 and k == HALF - 1 and ts == 1)
                            w0 = []
                            if kk == 0 and ts == 0:
                                w0 = [(wB[(2 * dc + pc) % 2], bload2[2 * dc + pc]), (dv, mmixed),
                                      (dv, bfree[dc - 2] if dc >= 2 else 0),
                                      (sc, slA_free[0]), (sc, slA_free[1]),
                                      (dv, slB_free[0]), (dv, slB_free[1]),
                                      (dv, silu1_free[0])]
                            ef('pe', lambda e, ts=ts, k=k, kk=kk, buf=buf, pX=pf[ts],
                               stp=(half == 0 and k == HALF - 1): e.matmul(
                                pX[:, :], wtB(buf)[:, kk * 128:(kk + 1) * 128], mixed(k, ts),
                                start=(k == 0), stop=stp),
                               waits=w0,
                               inc=(pe, 1) if (kk == 10 and ts == 1) or isl else None)
                        if kk == 10 and ts == 1:
                            mpc[pc] = pe.n
                        if k == 10:
                            bfree2[2 * dc] = pe.n
                            ni = 2 * dc + 2
                            if ni < 32:
                                load_b(ni // 2, ni % 2)
                        if k == HALF - 1:
                            bfree2[2 * dc + 1] = pe.n
                            ni = 2 * dc + 3
                            if ni < 32:
                                load_b(ni // 2, ni % 2)
                    for ts in range(2):
                        if half == 1:
                            ef('pe', lambda e, dc=dc, ts=ts, pX=pf[ts]: e.matmul(
                                pX[:, :], dnB_sb[:, dc * 128:(dc + 1) * 128], rho(0, ts),
                                start=False, stop=True),
                               waits=[(dO, mdnb[0]), (dv, mrho2[0])], inc=(pe, 1))
                        mpb = pe.n
                        ef('dv', lambda e, dc=dc, ts=ts, pX=pf[ts]: e.scalar_tensor_tensor(
                            rtile(dc, ts), pX[:, :], 0.5, rtile(dc, ts), ALU.mult, ALU.add),
                           waits=[(pe, mpb)], inc=(dv, 1))
                    bfree[dc] = dv.n
                psfree_dv[0] = dv.n

        def attention():
            base_pe, base_dv, base_sc = pe.n, dv.n, sc.n
            layer_norm(base_pe, base_dv, base_sc)
            my = S["y"]
            ef('gp', lambda e: e.dma_start(out=row_bv, in_=bvd),
               waits=[(dv, my)], inc=(dO, 16))
            mbv = dO.n
            for h4 in range(4):
                ef('pe', lambda e, h4=h4: e.matmul(ps[h4][:, :], ones_rb,
                                                   row_bv[:, h4 * 512:(h4 + 1) * 512],
                                                   start=True, stop=True),
                   waits=[(dO, mbv), (dv, my)], inc=(pe, 1))
            mbc2 = pe.n
            for h4 in range(4):
                ef('sc', lambda e, h4=h4: e.copy(bv_b[:, h4 * 512:(h4 + 1) * 512], ps[h4][:, :]),
                   waits=[(pe, mbc2)], inc=(sc, 1))
            mbve = sc.n
            pload = [0] * (2 * NDK)
            pfree = [0] * (2 * NDK)
            pfree_sc = [0] * (2 * NDK)
            kdma = {0: 0, 1: 0}
            kview = cin.rearrange("a (b c) -> (a b) c", b=2)

            def load_p(i):
                wpe = max(pfree[i - 2] if i >= 2 else 0, base_pe)
                Wd = wq_d if i < NDK else wk_d
                oc = i % NDK
                ef('gp', lambda e, Wd=Wd, oc=oc, i=i: e.dma_start(
                    out=_t3(wtP(i % 2), 128), in_=_tl(Wd[:, oc * 128:(oc + 1) * 128])),
                   waits=[(pe, wpe)], inc=(wP[i % 2], 16))
                pload[i] = wP[i % 2].n

            load_p(0)
            load_p(1)
            for i in range(2 * NDK):
                oc = i % NDK
                isq = i < NDK
                for ts in range(2):
                    pX = ps[(i % 2) * 2 + ts]
                    for dk in range(NDK):
                        ef('pe', lambda e, i=i, ts=ts, dk=dk, pX=pX: e.matmul(
                            pX[:, :], wtP(i % 2)[:, dk * 128:(dk + 1) * 128], ytile(dk, ts),
                            start=(dk == 0), stop=(dk == NDK - 1)),
                           waits=([(wP[i % 2], pload[i]), (dv, my), (sc, mbve),
                                   (dv, pfree_sc[i - 2] if i >= 2 else 0)] if dk == 0 else []),
                           inc=(pe, 1) if dk == NDK - 1 else None)
                    mproj = pe.n
                    bias = bq_s[:, oc:oc + 1] if isq else bk_s[:, oc:oc + 1]
                    scl = SM if isq else 1.0
                    ef('dv', lambda e, ts=ts, pX=pX, bias=bias, scl=scl: e.tensor_scalar(
                        kstg[:, ts * 512:(ts + 1) * 512], pX[:, :], bias, scl,
                        ALU.add, ALU.mult),
                       waits=[(pe, mproj), (dK[ts], kdma[ts])], inc=(dv, 1))
                    dst = (qdram[oc * 128:(oc + 1) * 128, ts * 512:(ts + 1) * 512] if isq
                           else kview[oc * 128:(oc + 1) * 128, ts * 512:(ts + 1) * 512])
                    ef('sy', lambda e, dst=dst, ts=ts: e.dma_start(
                        out=dst, in_=kstg[:, ts * 512:(ts + 1) * 512]),
                       waits=[(dv, dv.n)], inc=(dK[ts], 16))
                    kdma[ts] = dK[ts].n
                pfree[i] = pe.n
                pfree_sc[i] = dv.n
                if i + 2 < 2 * NDK:
                    load_p(i + 2)
            vw_free = [base_pe, base_pe]
            vev_free = [base_dv] * 4
            for tcb in range(2):
                for dvb in range(4):
                    mvw = 0
                    for dk in range(NDK):
                        buf = dk % 2
                        ef('gp', lambda e, dvb=dvb, dk=dk, buf=buf: e.dma_start(
                            out=wvt(buf), in_=wv_d[dk * 128:(dk + 1) * 128, dvb * 512:(dvb + 1) * 512]),
                           waits=[(pe, vw_free[buf]), (dv, base_dv)], inc=(wV[buf], 16))
                        mvw = wV[buf].n
                        for j4 in range(4):
                            tc = tcb * 4 + j4
                            ef('pe', lambda e, tc=tc, dk=dk, j4=j4, buf=buf: e.matmul(
                                psQ[j4][:, :], ytile(dk)[:, tc * 128:(tc + 1) * 128], wvt(buf),
                                start=(dk == 0), stop=(dk == NDK - 1)),
                               waits=([(wV[buf], mvw), (dv, vev_free[j4]), (dv, my)] if True else []),
                               inc=(pe, 1) if j4 == 3 else None)
                        vw_free[buf] = pe.n
                    mvp = pe.n
                    for j4 in range(4):
                        tc = tcb * 4 + j4
                        sslot = j4 % 2
                        ef('dv', lambda e, j4=j4, dvb=dvb, sslot=sslot: e.tensor_tensor(
                            kstg[:, sslot * 512:sslot * 512 + 512], psQ[j4][:, :],
                            bv_b[:, dvb * 512:(dvb + 1) * 512], ALU.add),
                           waits=[(pe, mvp), (sc, mbve), (dK[sslot], kdma[sslot])], inc=(dv, 1))
                        vev_free[j4] = dv.n
                        ef('sy', lambda e, tc=tc, dvb=dvb, sslot=sslot: e.dma_start(
                            out=cin[1024 + tc * 128:1024 + (tc + 1) * 128, dvb * 512:(dvb + 1) * 512],
                            in_=kstg[:, sslot * 512:sslot * 512 + 512]),
                           waits=[(dv, dv.n)], inc=(dK[sslot], 16))
                        kdma[sslot] = dK[sslot].n
            mcin = None
            ef('gp', lambda e: e.collective_compute(
                "AllGather", ALU.bypass, replica_groups=[[0, 1], [2, 3], [4, 5], [6, 7]],
                ins=[cin.opt()], outs=[cout.opt()]),
               waits=[(dK[0], dK[0].n), (dK[1], dK[1].n)], inc=(cc[2], 1))
            ef('gp', lambda e: e.dma_start(out=amask_sb, in_=amask_d),
               waits=[(dv, mrho_dead_dv[0])], inc=(dO, 16))
            mam = dO.n
            kvout = cout.rearrange("a (b c) -> (a b) c", b=2)
            hload = [0] * NH
            hvl = [0, 0]
            hfree_q = [base_dv, base_dv]
            hfree_k = [base_pe, base_pe]
            hfree_v = [base_pe, base_pe]
            sco_free = [base_sc, base_sc]
            pb_free = [base_pe, base_pe]
            ctx_free = [base_dv]
            l_free = [base_sc]

            def load_head(h):
                buf = h % 2
                sem = dH[buf]
                ef('sy', lambda e, h=h, buf=buf: e.dma_start(
                    out=q_t[buf], in_=qdram[h * 128:(h + 1) * 128, :]),
                   waits=[(dv, hfree_q[buf])], inc=(sem, 16))
                for j in range(2):
                    src = kvout[4096 * j + h * 128:4096 * j + (h + 1) * 128, :]
                    dst = k_t[buf].rearrange("p (g c) -> p g c", c=128)[:, j::2, :]
                    ef('sy', lambda e, src=src, dst=dst: e.dma_start(
                        out=dst, in_=src.rearrange("p (g c) -> p g c", c=128)),
                       waits=[(cc[2], 1), (pe, hfree_k[buf])], inc=(sem, 16))
                vbuf = buf
                for j in range(2):
                    src = cout[2048 * j + 1024:2048 * j + 2048, h * 128:(h + 1) * 128]
                    dst = v_t[vbuf].rearrange("p (g w) -> p g w", w=128)[:, j::2, :]
                    ef('sy', lambda e, src=src, dst=dst: e.dma_start(
                        out=dst, in_=src.rearrange("(g p) w -> p g w", p=128)),
                       waits=[(cc[2], 1), (pe, hfree_v[vbuf])], inc=(sem, 16))
                hvl[vbuf] = sem.n
                hload[h] = sem.n

            load_head(0)
            load_head(1)
            for h in range(NH):
                buf = h % 2
                vbuf = h % 2
                hp = h % 2
                mpv = 0
                for kt in range(16):
                    s0 = 128 * (kt // 2)
                    pr = [ps[0], ps[1]] if kt % 2 == 0 else [ps[2], ps[3]]
                    segs = ([(0, s0, 512)] if s0 < 512 else []) + [(1, max(512, s0), 1024)]
                    for n_, (si, a, b_) in enumerate(segs):
                        ef('pe', lambda e, kt=kt, buf=buf, pr=pr, si=si, a=a, b_=b_: e.matmul(
                            pr[si][:, a - si * 512:b_ - si * 512],
                            k_t[buf][:, kt * 128:(kt + 1) * 128], q_t[buf][:, a:b_],
                            start=True, stop=True),
                           waits=([(dH[buf], hload[h]), (sc, sco_free[kt % 2]),
                                   (sc, l_free[0])] if n_ == 0 else []),
                           inc=(pe, 1) if n_ == len(segs) - 1 else None)
                    msco = pe.n
                    for (si, a, b_) in segs:
                        ef('sc', lambda e, kt=kt, si=si, a=a, b_=b_, pr=pr: e.activation(
                            probsT(kt % 2)[:, a:b_], pr[si][:, a - si * 512:b_ - si * 512], AF.Exp),
                           waits=[(pe, max(msco, pb_free[kt % 2])), (dO, mam)], inc=(sc, 1))
                    msce2 = sc.n
                    sco_free[kt % 2] = msce2
                    ef('dv', lambda e, kt=kt, s0=s0: e.tensor_tensor(
                        probsT(kt % 2)[:, s0:s0 + 128], probsT(kt % 2)[:, s0:s0 + 128],
                        amask_sb[:, kt * 128:(kt + 1) * 128], ALU.mult),
                       waits=[(sc, msce2)], inc=(dv, 1))
                    mmk = dv.n
                    for n_, (si, a, b_) in enumerate(segs):
                        stp = (kt == (15 if si == 1 else 7))
                        ef('pe', lambda e, h=h, kt=kt, si=si, a=a, b_=b_, vbuf=vbuf, hp=hp, stp=stp: e.matmul(
                            psQ[si][:, a - si * 512:b_ - si * 512],
                            v_t[vbuf][:, kt * 128:(kt + 1) * 128],
                            probsT(kt % 2)[:, a:b_],
                            start=(kt == 0), stop=stp),
                           waits=([(dH[vbuf], hvl[vbuf]), (dv, mmk), (dv, ctx_free[0])] if n_ == 0 else []))
                        ef('pe', lambda e, kt=kt, si=si, a=a, b_=b_, stp=stp: e.matmul(
                            psQ[2 + si][0:1, a - si * 512:b_ - si * 512], ones_cb,
                            probsT(kt % 2)[:, a:b_],
                            start=(kt == 0), stop=stp),
                           inc=(pe, 1) if n_ == len(segs) - 1 else None)
                    pb_free[kt % 2] = pe.n
                    mpv = pe.n
                ef('dv', lambda e: e.reciprocal(linv_f[:, 0:512], psQ[2][0:1, :]),
                   waits=[(pe, mpv)], inc=(dv, 1))
                ef('dv', lambda e: e.reciprocal(linv_f[:, 512:1024], psQ[3][0:1, :]), inc=(dv, 1))
                ef('dv', lambda e: e.tensor_copy(linv_hi, linv_f), inc=(dv, 1))
                ef('dv', lambda e: e.tensor_tensor(linv_lo, linv_f, linv_hi, ALU.subtract), inc=(dv, 1))
                ml = dv.n
                for si in range(2):
                    ef('pe', lambda e, si=si: e.matmul(psQ[2 + si][:, :], ones_rb,
                                                       linv_hi[:, si * 512:(si + 1) * 512],
                                                       start=True, stop=False),
                       waits=[(dv, ml)])
                    ef('pe', lambda e, si=si: e.matmul(psQ[2 + si][:, :], ones_rb,
                                                       linv_lo[:, si * 512:(si + 1) * 512],
                                                       start=False, stop=True), inc=(pe, 1))
                mlb = pe.n
                for si in range(2):
                    ef('sc', lambda e, si=si: e.copy(linv_b[:, si * 512:(si + 1) * 512], psQ[2 + si][:, :]),
                       waits=[(pe, mlb)], inc=(sc, 1))
                l_free[0] = sc.n
                for si in range(2):
                    ef('dv', lambda e, h=h, si=si: e.tensor_tensor(
                        ytile(h)[:, si * 512:(si + 1) * 512], psQ[si][:, :],
                        linv_b[:, si * 512:(si + 1) * 512], ALU.mult),
                       waits=[(sc, l_free[0])], inc=(dv, 1))
                ctx_free[0] = dv.n
                hfree_q[buf] = ctx_free[0]
                hfree_k[buf] = mpv
                hfree_v[vbuf] = mpv
                if h + 2 < NH:
                    load_head(h + 2)
            mattn = dv.n
            oload = [0] * NDK
            ofree = [0] * NDK

            def load_o(dc):
                ef('gp', lambda e, dc=dc: e.dma_start(
                    out=_t3(wtP(dc % 2), 128), in_=_tl(wo_d[:, dc * 128:(dc + 1) * 128])),
                   waits=[(pe, pfree[2 * NDK - 1]),
                          (dv, ofree[dc - 2] if dc >= 2 else 0)], inc=(wP[dc % 2], 16))
                oload[dc] = wP[dc % 2].n

            load_o(0)
            load_o(1)
            for dc in range(NDK):
                for ts in range(2):
                    pX = ps[(dc % 2) * 2 + ts]
                    for hh in range(NH):
                        ef('pe', lambda e, dc=dc, ts=ts, hh=hh, pX=pX: e.matmul(
                            pX[:, :], wtP(dc % 2)[:, hh * 128:(hh + 1) * 128], ytile(hh, ts),
                            start=(hh == 0), stop=(hh == NH - 1)),
                           waits=([(wP[dc % 2], oload[dc]), (dv, mattn), (sc, l_free[0]),
                                   (dv, ofree[dc - 2] if dc >= 2 else 0)] if hh == 0 else []),
                           inc=(pe, 1) if hh == NH - 1 else None)
                    mo = pe.n
                    ef('dv', lambda e, dc=dc, ts=ts, pX=pX: e.tensor_tensor(
                        rtile(dc, ts), pX[:, :], rtile(dc, ts), ALU.add),
                       waits=[(pe, mo)], inc=(dv, 1))
                ofree[dc] = dv.n
                if dc + 2 < NDK:
                    load_o(dc + 2)

        mrho_dead_dv = [0]
        ffn(0)
        mrho_dead_dv[0] = dv.n
        if STAGE >= 2:
            attention()
        if STAGE >= 3:
            ffn(1)

        ef('sy', lambda e: e.dma_start(out=_tl(out), in_=_t3(resid[:, :], T)),
           waits=[(dv, dv.n), (pe, pe.n), (sc, sc.n)], inc=(dOUT, 16))
        ef('sy', lambda e: e.nop(), waits=[(dOUT, dOUT.n)])

        @block.sync
        def _(e):
            P.emit('sy', e)

        @block.gpsimd
        def _(e):
            P.emit('gp', e)

        @block.tensor
        def _(e):
            P.emit('pe', e)

        @block.scalar
        def _(e):
            P.emit('sc', e)

        @block.vector
        def _(e):
            P.emit('dv', e)

    return nc


_NC_CACHE = {}


def _amask(j):
    m = np.zeros((128, 16 * 128), dtype=np.float32)
    for kt in range(16):
        m0 = kt // 2
        qt = 2 * m0 + j
        gk = kt * 128 + np.arange(128)[:, None]
        gq = qt * 128 + np.arange(128)[None, :]
        m[:, kt * 128:(kt + 1) * 128] = (gq >= gk).astype(np.float32)
    return m


def kernel(L0_attn_hidden_states, L0_attn_Wq, L0_attn_bq, L0_attn_Wk, L0_attn_bk,
           L0_attn_Wv, L0_attn_bv, L0_attn_Wo, L0_ffn_gate_w, L0_ffn_w_up,
           L0_ffn_w_gate, L0_ffn_w_down, L0_ffn_up_A, L0_ffn_up_B, L0_ffn_gate_A,
           L0_ffn_gate_B, L0_ffn_down_A, L0_ffn_down_B):
    x = np.ascontiguousarray(np.asarray(L0_attn_hidden_states, dtype=np.float32))
    B = x.shape[0]
    cfg = lambda a: np.ascontiguousarray(np.asarray(a, dtype=np.float32))

    gate_w = cfg(L0_ffn_gate_w)
    gwt = np.zeros((128, 128), dtype=np.float32)
    for dk in range(NDK):
        gwt[:, dk * 8:(dk + 1) * 8] = gate_w[:, dk * 128:(dk + 1) * 128].T
    common = {
        "wq": cfg(L0_attn_Wq), "wk": cfg(L0_attn_Wk), "wv": cfg(L0_attn_Wv),
        "wo": cfg(L0_attn_Wo),
        "bqd": cfg(np.asarray(L0_attn_bq, dtype=np.float32).reshape(NDK, 128).T),
        "bkd": cfg(np.asarray(L0_attn_bk, dtype=np.float32).reshape(NDK, 128).T),
        "bvd": cfg(np.asarray(L0_attn_bv, dtype=np.float32).reshape(1, D)),
        "wupT": cfg(np.asarray(L0_ffn_w_up).T),
        "wgateT": cfg(np.asarray(L0_ffn_w_gate).T),
        "wdownT": cfg(np.asarray(L0_ffn_w_down).T),
        "upAT": cfg(np.asarray(L0_ffn_up_A).reshape(128, D).T),
        "gateAT": cfg(np.asarray(L0_ffn_gate_A).reshape(128, D).T),
        "upBT": cfg(np.asarray(L0_ffn_up_B).transpose(0, 2, 1).reshape(128, H)),
        "gateBT": cfg(np.asarray(L0_ffn_gate_B).transpose(0, 2, 1).reshape(128, H)),
        "dnAT": cfg(np.asarray(L0_ffn_down_A).reshape(128, H).T),
        "dnBT": cfg(np.asarray(L0_ffn_down_B).transpose(0, 2, 1).reshape(128, D)),
        "gwt": gwt,
        "identf": np.eye(128, dtype=np.float32),
        "ones_r": np.ones((1, 128), dtype=np.float32),
        "ones_c": np.ones((128, 1), dtype=np.float32),
        "ecol_d": (np.arange(128, dtype=np.float32) // 16).reshape(128, 1),
        "iota8_d": np.broadcast_to(np.arange(8, dtype=np.float32), (128, 8)).copy(),
    }
    amasks = [_amask(0), _amask(1)]

    if "nc" not in _NC_CACHE:
        _NC_CACHE["nc"] = build_nc()
    nc = _NC_CACHE["nc"]

    in_maps = []
    for c in range(8):
        b, j = c // 2, c % 2
        xt = x[b].reshape(16, 128, D)[j::2].reshape(T, D).T
        m = dict(common)
        m["xs"] = np.ascontiguousarray(xt)
        m["amask_d"] = amasks[j]
        in_maps.append(m)

    res = run_bass_kernel_spmd(nc, in_maps, core_ids=list(range(8)))

    y = np.empty_like(x)
    for c in range(8):
        b, j = c // 2, c % 2
        oc = np.asarray(res.results[c]["out"])
        yt = oc.T.reshape(NTC, 128, D)
        for i in range(NTC):
            y[b, (2 * i + j) * 128:(2 * i + j + 1) * 128, :] = yt[i]
    return y


# version 07
# speedup vs baseline: 1.2854x; 1.2854x over previous
import math
import numpy as np
import concourse.bass as bass
import concourse.mybir as mybir
from concourse.bass_utils import run_bass_kernel_spmd
from contextlib import ExitStack

F32 = mybir.dt.float32
F32R = mybir.dt.float32r
BF16 = mybir.dt.bfloat16
AF = mybir.ActivationFunctionType
ALU = mybir.AluOpType
AX = mybir.AxisListType

D, H, NH, DH, T, E, R = 2048, 5632, 16, 128, 1024, 8, 16
NDK = D // 128
NHC = H // 128
HALF = NHC // 2
NTC = T // 128
TS = 512
SM = 1.0 / math.sqrt(DH)
STAGE = 3


class _Sem:
    __slots__ = ("name", "n", "h")
    def __init__(self, name):
        self.name, self.n, self.h = name, 0, None


class _B:
    def __init__(self):
        self.ops = []
        self.sems = []
        self._lw = {}
        self.auto_chain = {}

    def sem(self, name):
        s = _Sem(name)
        self.sems.append(s)
        return s

    def op(self, eng, fn, waits=(), inc=None):
        waits = list(waits)
        if eng in self.auto_chain:
            s = self.auto_chain[eng]
            waits.append((s, s.n))
            if inc is None:
                inc = (s, 1)
        w = []
        for (s, v) in waits:
            if v <= 0:
                continue
            k = (eng, s.name)
            if self._lw.get(k, -1) >= v:
                continue
            self._lw[k] = v
            w.append((s, v))
        if inc is not None:
            inc[0].n += inc[1]
        self.ops.append((eng, fn, w, inc))
        return inc[0].n if inc else None

    def emit(self, name, eng):
        for (e, fn, w, inc) in self.ops:
            if e != name:
                continue
            for (s, v) in w:
                eng.wait_ge(s.h, v)
            inst = fn(eng)
            if inc is not None:
                inst.then_inc(inc[0].h, inc[1])


def _tl(src2d):
    return src2d.rearrange("(a p) c -> p a c", p=128)


def _t3(dst2d, c):
    return dst2d.rearrange("p (a c) -> p a c", c=c)


def build_nc():
    nc = bass.Bass("TRN2", target_bir_lowering=False)

    xs = nc.dram_tensor("xs", [D, T], F32, kind="ExternalInput").ap()
    wq_d = nc.dram_tensor("wq", [D, D], F32, kind="ExternalInput").ap()
    wk_d = nc.dram_tensor("wk", [D, D], F32, kind="ExternalInput").ap()
    wv_d = nc.dram_tensor("wv", [D, D], F32, kind="ExternalInput").ap()
    wo_d = nc.dram_tensor("wo", [D, D], F32, kind="ExternalInput").ap()
    bqd = nc.dram_tensor("bqd", [128, NDK], F32, kind="ExternalInput").ap()
    bkd = nc.dram_tensor("bkd", [128, NDK], F32, kind="ExternalInput").ap()
    bvd = nc.dram_tensor("bvd", [1, D], F32, kind="ExternalInput").ap()
    wupT = nc.dram_tensor("wupT", [D, H], F32, kind="ExternalInput").ap()
    wgateT = nc.dram_tensor("wgateT", [D, H], F32, kind="ExternalInput").ap()
    wdownT = nc.dram_tensor("wdownT", [H, D], F32, kind="ExternalInput").ap()
    upAT = nc.dram_tensor("upAT", [D, 128], F32, kind="ExternalInput").ap()
    gateAT = nc.dram_tensor("gateAT", [D, 128], F32, kind="ExternalInput").ap()
    upBT = nc.dram_tensor("upBT", [128, H], F32, kind="ExternalInput").ap()
    gateBT = nc.dram_tensor("gateBT", [128, H], F32, kind="ExternalInput").ap()
    dnAT = nc.dram_tensor("dnAT", [H, 128], F32, kind="ExternalInput").ap()
    dnBT = nc.dram_tensor("dnBT", [128, D], F32, kind="ExternalInput").ap()
    gwt = nc.dram_tensor("gwt", [128, 128], F32, kind="ExternalInput").ap()
    identf = nc.dram_tensor("identf", [128, 128], F32, kind="ExternalInput").ap()
    ones_r = nc.dram_tensor("ones_r", [1, 128], F32, kind="ExternalInput").ap()
    ones_c = nc.dram_tensor("ones_c", [128, 1], F32, kind="ExternalInput").ap()
    ecol_d = nc.dram_tensor("ecol_d", [128, 1], F32, kind="ExternalInput").ap()
    iota8_d = nc.dram_tensor("iota8_d", [128, 8], F32, kind="ExternalInput").ap()
    amask_d = nc.dram_tensor("amask_d", [128, 16 * 128], F32, kind="ExternalInput").ap()
    out = nc.dram_tensor("out", [D, T], F32, kind="ExternalOutput").ap()

    qdram = nc.dram_tensor("qdram", [D, T], BF16).ap()
    cin = nc.dram_tensor("cin", [2048, 2048], BF16).ap()
    cout = nc.dram_tensor("cout", [4096, 2048], BF16).ap()
    trd = nc.dram_tensor("trd", [32, 128], F32).ap()
    gin = [nc.dram_tensor(f"gin{i}", [1, 2], F32).ap() for i in range(2)]
    gout = [nc.dram_tensor(f"gout{i}", [2, 2], F32).ap() for i in range(2)]

    P = _B()
    dA = P.sem("dA")
    dO = P.sem("dO")
    wA = [P.sem("wA0"), P.sem("wA1")]
    wB = [P.sem("wB0"), P.sem("wB1")]
    wP = [P.sem("wP0"), P.sem("wP1")]
    wV = [P.sem("wV0"), P.sem("wV1")]
    dK = [P.sem("dK0"), P.sem("dK1")]
    dH = [P.sem("dH0"), P.sem("dH1")]
    dOUT = P.sem("dOUT")
    dP = P.sem("dP")
    wS = [P.sem("wS0"), P.sem("wS1")]
    cc = [P.sem(f"cc{i}") for i in range(3)]
    pe, sc, dv = P.sem("pe"), P.sem("sc"), P.sem("dv")
    P.auto_chain = {"dv": dv, "sc": sc}

    with ExitStack() as stk:
        en = stk.enter_context
        resid = en(nc.sbuf_tensor("resid", [128, NDK * 1024], F32))
        act = en(nc.sbuf_tensor("act", [128, NDK * 1024], BF16))
        arena = en(nc.sbuf_tensor("arena", [128, HALF * 1024], BF16))
        wa = en(nc.sbuf_tensor("wa", [128, 4 * 2048], BF16))
        wb = en(nc.sbuf_tensor("wb", [128, 2 * 1408], BF16))
        wsm = en(nc.sbuf_tensor("wsm", [128, 6 * 128], BF16))
        lnc = en(nc.sbuf_tensor("lnc", [128, 2048], BF16))
        sf = en(nc.sbuf_tensor("sf", [128, 2560], F32))
        sb = en(nc.sbuf_tensor("sb", [128, 10240], BF16))
        rowz = en(nc.sbuf_tensor("rowz", [1, 4], F32))
        smalls = en(nc.sbuf_tensor("smalls", [128, 64], F32))
        pack = en(nc.sbuf_tensor("pack", [128, 32], F32))
        trsb = en(nc.sbuf_tensor("trsb", [32, 128], F32))
        consts = en(nc.sbuf_tensor("consts", [128, 48], F32))
        cbf = en(nc.sbuf_tensor("cbf", [128, 16], BF16))
        crow = en(nc.sbuf_tensor("crow", [1, 192], F32))
        gstage = en(nc.sbuf_tensor("gstage", [1, 8], F32))
        kstg = en(nc.sbuf_tensor("kstg", [128, 2 * 512], BF16))
        gwtile = en(nc.sbuf_tensor("gwtile", [128, 128], BF16))
        identsb = en(nc.sbuf_tensor("identsb", [128, 128], F32))
        ps = [en(nc.psum_tensor(f"ps{i}", [128, 512], F32)) for i in range(4)]
        psQ = [en(nc.psum_tensor(f"psQ{i}", [128, 512], F32)) for i in range(4)]
        for s in P.sems:
            s.h = en(nc.semaphore(s.name))
        block = en(nc.Block())

        def rtile(dc, ts=None):
            a = resid[:, dc * 1024:(dc + 1) * 1024]
            return a if ts is None else a[:, ts * TS:(ts + 1) * TS]

        def ytile(dc, ts=None):
            a = act[:, dc * 1024:(dc + 1) * 1024]
            return a if ts is None else a[:, ts * TS:(ts + 1) * TS]

        ecol = consts[:, 0:1]
        iotasb = consts[:, 8:16]
        bq_s = consts[:, 16:32]
        bk_s = consts[:, 32:48]
        ones_cb = cbf[:, 0:1]
        ones_rb = crow[:, 0:64].bitcast(BF16)
        m0A, m1A = smalls[:, 0:8], smalls[:, 8:16]
        e0A, e1A = smalls[:, 16:24], smalls[:, 24:32]
        eqs, tmp8, lmt = smalls[:, 32:40], smalls[:, 40:48], smalls[:, 48:56]
        s01 = smalls[:, 56:58]
        s01r = smalls[:, 58:60].bitcast(F32R)
        rzb = smalls[:, 60:62]

        row_mean = sb[0:1, 0:2048].bitcast(F32)
        row_var = sb[0:1, 2048:4096].bitcast(F32)
        rstd_hi = sb[0:1, 4096:5120]
        rstd_lo = sb[0:1, 5120:6144]
        bneg_hi = sb[0:1, 6144:7168]
        bneg_lo = sb[0:1, 7168:8192]
        w_hi = [sb[0:1, 0:1024], sb[0:1, 2048:3072]]
        w_lo = [sb[0:1, 1024:2048], sb[0:1, 3072:4096]]
        row_wf = [sb[0:1, 4096:6144].bitcast(F32), sb[0:1, 6144:8192].bitcast(F32)]
        row_idx = [sb[0:1, 8192:9216], sb[0:1, 9216:10240]]
        linv_f = sb[0:1, 4096:6144].bitcast(F32)
        linv_hi = sb[0:1, 6144:7168]
        linv_lo = sb[0:1, 7168:8192]
        row_bv = sb[0:1, 8192:10240]

        rstd_b = sf[:, 0:1024]
        bneg_b = sf[:, 1024:2048]
        bA = sf[:, 0:512]
        bB = sf[:, 512:1024]
        s1t = sf[:, 1024:1536]
        t2t = sf[:, 1536:2048]
        siluX = sf[:, 2048:2560]
        bv_b = sf[:, 0:2048]
        linv_b = sf[:, 0:1024]

        def rho(i, ts):
            o = (i * 2 + ts) * 512
            return sb[:, o:o + 512]

        def crowb(i, ts):
            o = 4096 + (i * 2 + ts) * 512
            return sb[:, o:o + 512]

        def Mm(i, ts):
            o = 6144 + (i * 2 + ts) * 512
            return sb[:, o:o + 512]

        def probsT(buf):
            return sb[:, buf * 1024:(buf + 1) * 1024]
        amask_sb = sb[:, 2048:4096]
        cht = [sb[:, 8192 + i * 512:8192 + (i + 1) * 512] for i in range(4)]

        def mixed(k, ts):
            o = k * 1024 + ts * 512
            return arena[:, o:o + 512]
        q_t = [arena[:, i * 1024:(i + 1) * 1024] for i in range(2)]
        k_t = [arena[:, 2048 + i * 2048:2048 + (i + 1) * 2048] for i in range(2)]
        v_t = [arena[:, 6144 + i * 2048:6144 + (i + 1) * 2048] for i in range(2)]

        def wtA(slot, buf):
            o = (slot * 2 + buf) * 2048
            return wa[:, o:o + 2048]

        def wtP(buf):
            return wa[:, buf * 2048:buf * 2048 + 2048]

        def wtB(buf):
            return wb[:, buf * 1408:buf * 1408 + 1408]

        def wvt(buf):
            return wb[:, buf * 1408:buf * 1408 + 512]

        def btl(slot, buf):
            o = (slot * 2 + buf) * 128
            return wsm[:, o:o + 128]

        upA_sb = lnc[:, 0:2048]
        dnB_sb = lnc[:, 0:2048]
        xsq = [lnc[:, 0:1024]] * 2
        xbf = [lnc[:, 1024:2048]] * 2

        ef = P.op
        S = {"y": 0}

        ef('sy', lambda e: e.dma_start(out=_t3(resid[:, :], T), in_=_tl(xs)), inc=(dA, 16))
        mx = dA.n
        ef('sy', lambda e: e.dma_start(out=identsb[:, :], in_=identf), inc=(dA, 16))
        ef('sy', lambda e: e.dma_start(out=consts[:, 0:1], in_=ecol_d), inc=(dA, 16))
        ef('sy', lambda e: e.dma_start(out=consts[:, 8:16], in_=iota8_d), inc=(dA, 16))
        ef('sy', lambda e: e.dma_start(out=consts[:, 16:32], in_=bqd), inc=(dA, 16))
        ef('sy', lambda e: e.dma_start(out=consts[:, 32:48], in_=bkd), inc=(dA, 16))
        mca = dA.n
        mx = dA.n
        ef('gp', lambda e: e.dma_start(out=ones_cb, in_=ones_c), inc=(dO, 16))
        ef('gp', lambda e: e.dma_start(out=ones_rb, in_=ones_r), inc=(dO, 16))
        ef('gp', lambda e: e.dma_start(out=gwtile[:, :], in_=gwt), inc=(dO, 16))
        mcg = dO.n

        def layer_norm(base_pe, base_dv, base_sc):
            mson = [0] * NDK
            for dc in range(NDK):
                b = dc % 2
                wpe = max(mson[dc - 1] if dc >= 1 else 0, base_pe)
                ef('dv', lambda e, dc=dc, b=b: e.tensor_tensor(xsq[b], rtile(dc), rtile(dc), ALU.mult),
                   waits=[(dA, mx), (pe, wpe), (dv, base_dv)], inc=(dv, 1))
                mdv = dv.n
                ef('sc', lambda e, dc=dc, b=b: e.copy(xbf[b], rtile(dc)),
                   waits=[(dA, mx), (pe, wpe), (dv, base_dv), (sc, base_sc)], inc=(sc, 1))
                msc = sc.n
                for h2 in range(2):
                    tgt = ps[h2]
                    ef('pe', lambda e, b=b, h2=h2, tgt=tgt, dc=dc: e.matmul(
                        tgt[0:1, :], ones_cb, xsq[b][:, h2 * 512:(h2 + 1) * 512],
                        start=(dc == 0), stop=(dc == NDK - 1)),
                       waits=[(dv, mdv), (dO, mcg), (dv, base_dv), (sc, base_sc)])
                    tgt2 = ps[2 + h2]
                    mson[dc] = ef('pe', lambda e, b=b, h2=h2, tgt2=tgt2, dc=dc: e.matmul(
                        tgt2[0:1, :], ones_cb, xbf[b][:, h2 * 512:(h2 + 1) * 512],
                        start=(dc == 0), stop=(dc == NDK - 1)),
                       waits=[(sc, msc)], inc=(pe, 1))
            m_stats = pe.n
            S["lnstats"] = m_stats
            ef('sc', lambda e: e.activation(row_mean[:, 0:512], ps[2][0:1, :], AF.Copy, scale=1.0 / D),
               waits=[(pe, m_stats)], inc=(sc, 1))
            ef('sc', lambda e: e.activation(row_mean[:, 512:1024], ps[3][0:1, :], AF.Copy, scale=1.0 / D), inc=(sc, 1))
            ef('sc', lambda e: e.activation(row_var[:, 0:512], ps[0][0:1, :], AF.Copy, scale=1.0 / D), inc=(sc, 1))
            ef('sc', lambda e: e.activation(row_var[:, 512:1024], ps[1][0:1, :], AF.Copy, scale=1.0 / D), inc=(sc, 1))
            mrow = sc.n
            ef('dv', lambda e: e.tensor_tensor(ps[0][0:1, :], row_mean[:, 0:512], row_mean[:, 0:512], ALU.mult),
               waits=[(sc, mrow)], inc=(dv, 1))
            ef('dv', lambda e: e.tensor_tensor(ps[1][0:1, :], row_mean[:, 512:1024], row_mean[:, 512:1024], ALU.mult), inc=(dv, 1))
            ef('dv', lambda e: e.tensor_tensor(row_var[:, 0:512], row_var[:, 0:512], ps[0][0:1, :], ALU.subtract), inc=(dv, 1))
            ef('dv', lambda e: e.tensor_tensor(row_var[:, 512:1024], row_var[:, 512:1024], ps[1][0:1, :], ALU.subtract), inc=(dv, 1))
            mv = dv.n
            ef('dv', lambda e: e.tensor_single_scalar(row_var[:, :], row_var[:, :], 1e-5, ALU.add),
               waits=[(dv, mv)], inc=(dv, 1))
            ef('sc', lambda e: e.activation(row_var[:, :], row_var[:, :], AF.Sqrt),
               waits=[(dv, dv.n)], inc=(sc, 1))
            ef('dv', lambda e: e.reciprocal(row_var[:, :], row_var[:, :]), waits=[(sc, sc.n)], inc=(dv, 1))
            ef('dv', lambda e: e.scalar_tensor_tensor(row_mean[:, :], row_mean[:, :], -1.0,
                                                      row_var[:, :], ALU.mult, ALU.mult), inc=(dv, 1))
            ef('dv', lambda e: e.tensor_copy(rstd_hi, row_var[:, :]), inc=(dv, 1))
            ef('dv', lambda e: e.tensor_tensor(rstd_lo, row_var[:, :], rstd_hi, ALU.subtract), inc=(dv, 1))
            ef('dv', lambda e: e.tensor_copy(bneg_hi, row_mean[:, :]), inc=(dv, 1))
            ef('dv', lambda e: e.tensor_tensor(bneg_lo, row_mean[:, :], bneg_hi, ALU.subtract), inc=(dv, 1))
            mrs = dv.n
            for h2 in range(2):
                ef('pe', lambda e, h2=h2: e.matmul(ps[h2][:, :], ones_rb,
                                                   rstd_hi[:, h2 * 512:(h2 + 1) * 512],
                                                   start=True, stop=False),
                   waits=[(dv, mrs)])
                ef('pe', lambda e, h2=h2: e.matmul(ps[h2][:, :], ones_rb,
                                                   rstd_lo[:, h2 * 512:(h2 + 1) * 512],
                                                   start=False, stop=True), inc=(pe, 1))
                ef('pe', lambda e, h2=h2: e.matmul(ps[2 + h2][:, :], ones_rb,
                                                   bneg_hi[:, h2 * 512:(h2 + 1) * 512],
                                                   start=True, stop=False))
                ef('pe', lambda e, h2=h2: e.matmul(ps[2 + h2][:, :], ones_rb,
                                                   bneg_lo[:, h2 * 512:(h2 + 1) * 512],
                                                   start=False, stop=True), inc=(pe, 1))
            mbc = pe.n
            ef('sc', lambda e: e.copy(rstd_b[:, 0:512], ps[0][:, :]), waits=[(pe, mbc)], inc=(sc, 1))
            ef('sc', lambda e: e.copy(rstd_b[:, 512:1024], ps[1][:, :]), inc=(sc, 1))
            ef('sc', lambda e: e.copy(bneg_b[:, 0:512], ps[2][:, :]), inc=(sc, 1))
            ef('sc', lambda e: e.copy(bneg_b[:, 512:1024], ps[3][:, :]), inc=(sc, 1))
            mbce = sc.n
            mnorm = [0] * NDK
            for dc in range(NDK):
                pa, pb = ps[2 * (dc % 2)], ps[1 + 2 * (dc % 2)]
                wv_ = max(mnorm[dc - 2] if dc >= 2 else 0, 0)
                m = 0
                for h2 in range(2):
                    tgt = pa if h2 == 0 else pb
                    ef('dv', lambda e, dc=dc, h2=h2, tgt=tgt: e.tensor_tensor(
                        tgt[:, :], rtile(dc)[:, h2 * 512:(h2 + 1) * 512],
                        rstd_b[:, h2 * 512:(h2 + 1) * 512], ALU.mult),
                       waits=[(sc, mbce), (dv, wv_)])
                    m = ef('dv', lambda e, dc=dc, h2=h2, tgt=tgt: e.tensor_tensor(
                        ytile(dc)[:, h2 * 512:(h2 + 1) * 512], tgt[:, :],
                        bneg_b[:, h2 * 512:(h2 + 1) * 512], ALU.add), inc=(dv, 1))
                mnorm[dc] = m
            S["y"] = dv.n
            return dv.n

        def gating(call):
            my = S["y"]
            mls = [0] * NTC
            for tc in range(NTC):
                Lp = ps[tc % 2]
                wdv = mls[tc - 2] if tc >= 2 else 0
                for dk in range(NDK):
                    ef('pe', lambda e, tc=tc, dk=dk, Lp=Lp: e.matmul(
                        Lp[:, 0:8], ytile(dk)[:, tc * 128:(tc + 1) * 128],
                        gwtile[:, dk * 8:(dk + 1) * 8],
                        start=(dk == 0), stop=(dk == NDK - 1)),
                       waits=[(dv, max(my, wdv)), (dO, mcg)],
                       inc=(pe, 1) if dk == NDK - 1 else None)
                mL = pe.n
                ef('dv', lambda e, tc=tc, Lp=Lp: e.tensor_reduce(m0A[:, tc:tc + 1], Lp[:, 0:8], AX.X, ALU.max),
                   waits=[(pe, mL), (dA, mca)], inc=(dv, 1))
                ef('sc', lambda e, tc=tc: e.activation(e0A[:, tc:tc + 1], m0A[:, tc:tc + 1], AF.Exp),
                   waits=[(dv, dv.n)], inc=(sc, 1))
                ef('dv', lambda e, tc=tc, Lp=Lp: e.tensor_single_scalar(eqs, Lp[:, 0:8], m0A[:, tc:tc + 1], ALU.is_equal))
                ef('dv', lambda e: e.scalar_tensor_tensor(tmp8, eqs, -1000.0, iotasb, ALU.mult, ALU.add))
                ef('dv', lambda e, tc=tc: e.tensor_reduce(pack[:, tc:tc + 1], tmp8, AX.X, ALU.min))
                ef('dv', lambda e, tc=tc: e.tensor_single_scalar(pack[:, tc:tc + 1], pack[:, tc:tc + 1], 1000.0, ALU.add))
                ef('dv', lambda e, tc=tc: e.tensor_single_scalar(eqs, iotasb, pack[:, tc:tc + 1], ALU.is_equal))
                ef('dv', lambda e, Lp=Lp: e.scalar_tensor_tensor(lmt, eqs, -1e30, Lp[:, 0:8], ALU.mult, ALU.add),
                   inc=(dv, 1))
                mls[tc] = dv.n
                ef('dv', lambda e, tc=tc: e.tensor_reduce(m1A[:, tc:tc + 1], lmt, AX.X, ALU.max), inc=(dv, 1))
                ef('sc', lambda e, tc=tc: e.activation(e1A[:, tc:tc + 1], m1A[:, tc:tc + 1], AF.Exp),
                   waits=[(dv, dv.n)], inc=(sc, 1))
                ef('dv', lambda e, tc=tc: e.tensor_single_scalar(eqs, lmt, m1A[:, tc:tc + 1], ALU.is_equal))
                ef('dv', lambda e: e.scalar_tensor_tensor(tmp8, eqs, -1000.0, iotasb, ALU.mult, ALU.add))
                ef('dv', lambda e, tc=tc: e.tensor_reduce(pack[:, 8 + tc:9 + tc], tmp8, AX.X, ALU.min))
                ef('dv', lambda e, tc=tc: e.tensor_single_scalar(pack[:, 8 + tc:9 + tc], pack[:, 8 + tc:9 + tc],
                                                                 1000.0, ALU.add), inc=(dv, 1))
            msce = sc.n
            s01hl = smalls[:, 56:58].bitcast(BF16)
            ef('dv', lambda e: e.tensor_reduce(s01[:, 0:1], e0A, AX.X, ALU.add),
               waits=[(sc, msce)], inc=(dv, 1))
            ef('dv', lambda e: e.tensor_reduce(s01[:, 1:2], e1A, AX.X, ALU.add), inc=(dv, 1))
            ef('dv', lambda e: e.tensor_copy(s01hl[:, 0:2], s01), inc=(dv, 1))
            ef('dv', lambda e: e.tensor_tensor(s01hl[:, 2:4], s01, s01hl[:, 0:2], ALU.subtract), inc=(dv, 1))
            mS = dv.n
            ef('pe', lambda e: e.matmul(ps[2][0:1, 0:4], ones_cb, s01hl, start=True, stop=True),
               waits=[(dv, mS), (dO, mcg)], inc=(pe, 1))
            ef('sc', lambda e: e.copy(gstage[:, 0:4], ps[2][0:1, 0:4]), waits=[(pe, pe.n)], inc=(sc, 1))
            ef('dv', lambda e: e.tensor_tensor(gstage[:, 4:6], gstage[:, 0:2], gstage[:, 2:4], ALU.add),
               waits=[(sc, sc.n)], inc=(dv, 1))
            ef('sy', lambda e, call=call: e.dma_start(out=gin[call], in_=gstage[:, 4:6]),
               waits=[(dv, dv.n)], inc=(dA, 16))
            mgin = dA.n
            ef('gp', lambda e, call=call: e.collective_compute(
                "AllGather", ALU.bypass, replica_groups=[[0, 1], [2, 3], [4, 5], [6, 7]],
                ins=[gin[call].opt()], outs=[gout[call].opt()]),
               waits=[(dA, mgin)], inc=(cc[call], 1))
            ef('sy', lambda e, call=call: e.dma_start(out=gstage[:, 0:2], in_=gout[call][0:1, :]),
               waits=[(cc[call], 1)], inc=(dA, 16))
            ef('sy', lambda e, call=call: e.dma_start(out=gstage[:, 2:4], in_=gout[call][1:2, :]),
               inc=(dA, 16))
            mgz = dA.n
            ef('dv', lambda e: e.tensor_tensor(gstage[:, 6:8], gstage[:, 0:2], gstage[:, 2:4], ALU.add),
               waits=[(dA, mgz)], inc=(dv, 1))
            ef('dv', lambda e: e.reciprocal(gstage[:, 6:8], gstage[:, 6:8]), inc=(dv, 1))
            rzhl = rowz.bitcast(BF16)
            ef('dv', lambda e: e.tensor_copy(rzhl[:, 0:2], gstage[:, 6:8]), inc=(dv, 1))
            ef('dv', lambda e: e.tensor_tensor(rzhl[:, 2:4], gstage[:, 6:8], rzhl[:, 0:2], ALU.subtract),
               inc=(dv, 1))
            mrz = dv.n
            ef('pe', lambda e: e.matmul(ps[3][:, 0:2], ones_rb, rzhl[:, 0:2], start=True, stop=False),
               waits=[(dv, mrz)])
            ef('pe', lambda e: e.matmul(ps[3][:, 0:2], ones_rb, rzhl[:, 2:4], start=False, stop=True),
               inc=(pe, 1))
            ef('sc', lambda e: e.copy(rzb, ps[3][:, 0:2]), waits=[(pe, pe.n)], inc=(sc, 1))
            ef('dv', lambda e: e.tensor_single_scalar(pack[:, 16:24], e0A, rzb[:, 0:1], ALU.mult),
               waits=[(sc, sc.n)])
            ef('dv', lambda e: e.tensor_single_scalar(pack[:, 24:32], e1A, rzb[:, 1:2], ALU.mult),
               inc=(dv, 1))
            mpk = dv.n
            ef('pe', lambda e: e.transpose(ps[2][0:32, 0:128], pack[:, :], identsb[:, :]),
               waits=[(dv, mpk), (dA, mca)], inc=(pe, 1))
            ef('sc', lambda e: e.copy(trsb[:, :], ps[2][0:32, 0:128]), waits=[(pe, pe.n)], inc=(sc, 1))
            ef('sy', lambda e: e.dma_start(out=trd, in_=trsb[:, :]), waits=[(sc, sc.n)], inc=(dA, 16))
            mtr = dA.n
            ef('sy', lambda e: e.dma_start(out=row_wf[0],
                                           in_=trd[16:24, :].rearrange("a b -> (a b)").unsqueeze(0)),
               waits=[(dA, mtr)], inc=(dA, 16))
            ef('sy', lambda e: e.dma_start(out=row_wf[1],
                                           in_=trd[24:32, :].rearrange("a b -> (a b)").unsqueeze(0)),
               inc=(dA, 16))
            mwf = dA.n
            for wi in range(2):
                ef('dv', lambda e, wi=wi: e.tensor_copy(w_hi[wi], row_wf[wi]),
                   waits=[(dA, mwf)], inc=(dv, 1))
                ef('dv', lambda e, wi=wi: e.tensor_tensor(w_lo[wi], row_wf[wi], w_hi[wi], ALU.subtract),
                   inc=(dv, 1))
            mwhl = dv.n
            for ts in range(2):
                for wi in range(2):
                    tgt = ps[2 + wi]
                    ef('pe', lambda e, ts=ts, wi=wi, tgt=tgt: e.matmul(
                        tgt[:, :], ones_rb, w_hi[wi][:, ts * 512:(ts + 1) * 512],
                        start=True, stop=False),
                       waits=[(dv, mwhl), (sc, sc.n)])
                    ef('pe', lambda e, ts=ts, wi=wi, tgt=tgt: e.matmul(
                        tgt[:, :], ones_rb, w_lo[wi][:, ts * 512:(ts + 1) * 512],
                        start=False, stop=True), inc=(pe, 1))
                mcb = pe.n
                ef('sc', lambda e, ts=ts: e.copy(crowb(0, ts), ps[2][:, :]), waits=[(pe, mcb)], inc=(sc, 1))
                ef('sc', lambda e, ts=ts: e.copy(crowb(1, ts), ps[3][:, :]), inc=(sc, 1))
            ef('gp', lambda e: e.dma_start(out=row_idx[0],
                                           in_=trd[0:8, :].rearrange("a b -> (a b)").unsqueeze(0)),
               waits=[(dA, mtr)], inc=(dO, 16))
            ef('gp', lambda e: e.dma_start(out=row_idx[1],
                                           in_=trd[8:16, :].rearrange("a b -> (a b)").unsqueeze(0)),
               inc=(dO, 16))
            mrows = dO.n
            mmask_prev_dv = 0
            for ts in range(2):
                ef('pe', lambda e, ts=ts: e.matmul(ps[0][:, :], ones_rb,
                                                   row_idx[0][:, ts * 512:(ts + 1) * 512],
                                                   start=True, stop=True),
                   waits=[(dO, mrows), (dv, mmask_prev_dv)], inc=(pe, 1))
                ef('pe', lambda e, ts=ts: e.matmul(ps[1][:, :], ones_rb,
                                                   row_idx[1][:, ts * 512:(ts + 1) * 512],
                                                   start=True, stop=True), inc=(pe, 1))
                mb = pe.n
                ef('dv', lambda e, ts=ts: e.tensor_scalar(Mm(0, ts), ps[0][:, :], ecol, 2.0,
                                                          ALU.is_equal, ALU.mult),
                   waits=[(pe, mb)], inc=(dv, 1))
                ef('dv', lambda e, ts=ts: e.tensor_scalar(Mm(1, ts), ps[1][:, :], ecol, 2.0,
                                                          ALU.is_equal, ALU.mult), inc=(dv, 1))
                mmask_prev_dv = dv.n
            return dv.n, sc.n

        def ffn(call):
            base_pe, base_dv, base_sc = pe.n, dv.n, sc.n
            layer_norm(base_pe, base_dv, base_sc)
            mmask_dv, mmask_sc = gating(call)

            for w_i, Ad in enumerate((upAT, gateAT)):
                wa_pe = S["lnstats"] if w_i == 0 else pe.n
                ef('gp', lambda e, Ad=Ad: e.dma_start(out=_t3(upA_sb, 128), in_=_tl(Ad)),
                   waits=[(pe, wa_pe)], inc=(dP, 16))
                mA = dP.n
                for ts in range(2):
                    tgt = psQ[2 * w_i + ts]
                    for dk in range(NDK):
                        ef('pe', lambda e, ts=ts, dk=dk, tgt=tgt: e.matmul(
                            tgt[:, :], upA_sb[:, dk * 128:(dk + 1) * 128], ytile(dk, ts),
                            start=(dk == 0), stop=(dk == NDK - 1)),
                           waits=[(dP, mA), (dv, mmask_dv), (sc, mmask_sc)],
                           inc=(pe, 1) if dk == NDK - 1 else None)
                    mr = pe.n
                    ef('dv', lambda e, ts=ts, w_i=w_i, tgt=tgt: e.tensor_tensor(
                        rho(2 * w_i, ts), tgt[:, :], Mm(0, ts), ALU.mult),
                       waits=[(pe, mr)], inc=(dv, 1))
                    ef('dv', lambda e, ts=ts, tgt=tgt: e.tensor_tensor(
                        cht[0], tgt[:, :], Mm(1, ts), ALU.mult), inc=(dv, 1))
                    ef('dv', lambda e, ts=ts, w_i=w_i: e.tensor_tensor(
                        rho(2 * w_i + 1, ts), cht[0], rho(2 * w_i, ts), ALU.subtract), inc=(dv, 1))
            mrho = dv.n

            wload = [0] * NHC
            wfree = [0] * NHC

            wload_s = [0] * NHC

            def load_hc(hc):
                wpe = max(wfree[hc - 2] if hc >= 2 else 0, base_pe)
                sem = wA[hc % 2]
                ef('gp', lambda e, hc=hc: e.dma_start(
                    out=_t3(wtA(0, hc % 2), 128), in_=_tl(wupT[:, hc * 128:(hc + 1) * 128])),
                   waits=[(pe, wpe)], inc=(sem, 16))
                ef('gp', lambda e, hc=hc: e.dma_start(
                    out=_t3(wtA(1, hc % 2), 128), in_=_tl(wgateT[:, hc * 128:(hc + 1) * 128])),
                   inc=(sem, 16))
                wload[hc] = sem.n

            def load_small(hc):
                wpe = max(wfree_q[hc - 2] if hc >= 2 else 0, base_pe)
                sem = wS[hc % 2]
                ef('gp', lambda e, hc=hc: e.dma_start(
                    out=btl(0, hc % 2), in_=upBT[:, hc * 128:(hc + 1) * 128]),
                   waits=[(pe, wpe)], inc=(sem, 16))
                ef('gp', lambda e, hc=hc: e.dma_start(
                    out=btl(1, hc % 2), in_=gateBT[:, hc * 128:(hc + 1) * 128]), inc=(sem, 16))
                ef('gp', lambda e, hc=hc: e.dma_start(
                    out=_t3(btl(2, hc % 2), 128), in_=_tl(dnAT[hc * 128:(hc + 1) * 128, :])), inc=(sem, 16))
                wload_s[hc] = sem.n

            load_hc(0)
            load_hc(1)
            load_small(0)
            load_small(1)
            slA_free = [0, 0]
            slB_free = [0, 0]
            cht_free = [0, 0]
            pend_q = []
            wfree_q = [0] * NHC
            silu0_free = [0]
            silu1_free = [0]
            bA_free = [0]
            bB_free = [0]
            mdnb = [0]
            mrho2 = [0]
            bfree = [0] * NDK
            bfree2 = [0] * 32
            psfree_dv = [0]

            for half in range(2):
                for k in range(HALF):
                    hc = half * HALF + k
                    par = hc % 2
                    for ts in range(2):
                        pA, pB, pDu, pDg = ps[0], ps[1], ps[2], ps[3]
                        for dk in range(NDK):
                            ef('pe', lambda e, ts=ts, dk=dk, par=par: e.matmul(
                                pA[:, :], wtA(0, par)[:, dk * 128:(dk + 1) * 128], ytile(dk, ts),
                                start=(dk == 0), stop=False),
                               waits=([(wA[par], wload[hc]), (sc, slA_free[0]), (dv, mmask_dv),
                                       (dv, psfree_dv[0])]
                                      if dk == 0 else []))
                        ef('pe', lambda e, ts=ts, par=par: e.matmul(
                            pA[:, :], btl(0, par), rho(0, ts), start=False, stop=True),
                           waits=[(dv, mrho), (wS[par], wload_s[hc])], inc=(pe, 1))
                        mAup = pe.n
                        for dk in range(NDK):
                            ef('pe', lambda e, ts=ts, dk=dk, par=par: e.matmul(
                                pB[:, :], wtA(1, par)[:, dk * 128:(dk + 1) * 128], ytile(dk, ts),
                                start=(dk == 0), stop=False),
                               waits=([(sc, slA_free[1])] if dk == 0 else []))
                        ef('pe', lambda e, ts=ts, par=par: e.matmul(
                            pB[:, :], btl(1, par), rho(2, ts), start=False, stop=True),
                           inc=(pe, 1))
                        mAg = pe.n
                        ef('pe', lambda e, ts=ts, par=par: e.matmul(
                            pDu[:, :], btl(0, par), rho(1, ts), start=True, stop=True),
                           waits=[(dv, slB_free[0]), (dv, psfree_dv[0])], inc=(pe, 1))
                        mDu = pe.n
                        ef('pe', lambda e, ts=ts, par=par: e.matmul(
                            pDg[:, :], btl(1, par), rho(3, ts), start=True, stop=True),
                           waits=[(dv, slB_free[1]), (dv, psfree_dv[0])], inc=(pe, 1))
                        mDg = pe.n
                        ef('sc', lambda e: e.copy(bA, pA[:, :]),
                           waits=[(pe, mAup), (dv, bA_free[0])], inc=(sc, 1))
                        slA_free[0] = sc.n
                        ef('sc', lambda e: e.copy(bB, pB[:, :]),
                           waits=[(pe, mAg), (dv, bB_free[0])], inc=(sc, 1))
                        slA_free[1] = sc.n
                        it2 = (hc * 2 + ts) % 2
                        ch2 = cht[2] if it2 == 0 else kstg[:, 0:512]
                        ch3 = cht[3] if it2 == 0 else kstg[:, 512:1024]
                        ef('sc', lambda e: e.activation(siluX, bA, AF.Silu),
                           waits=[(dv, silu0_free[0])], inc=(sc, 1))
                        ms0 = sc.n
                        ef('dv', lambda e: e.tensor_tensor(cht[0], siluX, bB, ALU.mult),
                           waits=[(sc, ms0), (pe, cht_free[it2])], inc=(dv, 1))
                        silu0_free[0] = dv.n
                        ef('dv', lambda e: e.tensor_tensor(s1t, bA, pDu[:, :], ALU.add),
                           waits=[(pe, mDu)], inc=(dv, 1))
                        slB_free[0] = dv.n
                        bA_free[0] = dv.n
                        ef('sc', lambda e: e.activation(siluX, s1t, AF.Silu),
                           waits=[(dv, dv.n)], inc=(sc, 1))
                        ms1 = sc.n
                        ef('dv', lambda e: e.tensor_tensor(t2t, bB, pDg[:, :], ALU.add),
                           waits=[(pe, mDg)], inc=(dv, 1))
                        slB_free[1] = dv.n
                        bB_free[0] = dv.n
                        ef('dv', lambda e: e.tensor_tensor(cht[1], siluX, t2t, ALU.mult),
                           waits=[(sc, ms1)], inc=(dv, 1))
                        silu0_free[0] = dv.n
                        silu1_free[0] = dv.n
                        ef('dv', lambda e, ts=ts, ch2=ch2: e.tensor_tensor(ch2, cht[0], crowb(0, ts), ALU.mult))
                        ef('dv', lambda e, ts=ts, ch3=ch3: e.tensor_tensor(ch3, cht[1], crowb(1, ts), ALU.mult))
                        ef('dv', lambda e, k=k, ts=ts, ch2=ch2, ch3=ch3: e.tensor_tensor(
                            mixed(k, ts), ch2, ch3, ALU.add), inc=(dv, 1))
                        mch = dv.n
                        def qmm(e, hc=hc, par=par, ts=ts, ch2=ch2):
                            return e.matmul(psQ[ts][:, :], btl(2, par), ch2,
                                            start=(hc == 0), stop=(hc == NHC - 1))
                        def qmm2(e, hc=hc, par=par, ts=ts, ch3=ch3):
                            return e.matmul(psQ[2 + ts][:, :], btl(2, par), ch3,
                                            start=(hc == 0), stop=(hc == NHC - 1))
                        pend_q.append((qmm, qmm2, mch, hc, it2))
                        if len(pend_q) > 1:
                            f1, f2, mw, phc, pit2 = pend_q.pop(0)
                            ef('pe', f1, waits=[(dv, mw)])
                            ef('pe', f2, inc=(pe, 1))
                            cht_free[pit2] = pe.n
                            wfree_q[phc] = pe.n
                            if ts == 0 and hc + 1 < NHC and hc >= 1:
                                load_small(hc + 1)
                        if ts == 1:
                            wfree[hc] = pe.n
                    if hc + 2 < NHC:
                        load_hc(hc + 2)

                if half == 1:
                    while pend_q:
                        f1, f2, mw, phc, pit2 = pend_q.pop(0)
                        ef('pe', f1, waits=[(dv, mw)])
                        ef('pe', f2, inc=(pe, 1))
                        cht_free[pit2] = pe.n
                        wfree_q[phc] = pe.n
                mmixed = dv.n
                if half == 1:
                    ef('gp', lambda e: e.dma_start(out=dnB_sb, in_=dnBT),
                       waits=[(pe, pe.n)], inc=(dP, 16))
                    mdnb[0] = dP.n
                    for ts in range(2):
                        ef('dv', lambda e, ts=ts: e.tensor_tensor(cht[0], psQ[ts][:, :], Mm(0, ts), ALU.mult),
                           waits=[(pe, max(cht_free[0], cht_free[1]))], inc=(dv, 1))
                        ef('dv', lambda e, ts=ts: e.tensor_tensor(cht[1], psQ[2 + ts][:, :], Mm(1, ts), ALU.mult),
                           inc=(dv, 1))
                        ef('dv', lambda e, ts=ts: e.tensor_tensor(rho(0, ts), cht[0], cht[1], ALU.add),
                           inc=(dv, 1))
                    mrho2[0] = dv.n
                bload2 = [0] * 32
                bfree2_prev = list(bfree2) if half == 1 else [0] * 32

                def load_b(dc, pc, half=half):
                    idx = 2 * dc + pc
                    prev = bfree2[idx - 2] if idx >= 2 else (bfree2_prev[idx] if half == 1 else 0)
                    ef('gp', lambda e, dc=dc, pc=pc, half=half: e.dma_start(
                        out=_t3(wtB((2 * dc + pc) % 2), 128),
                        in_=_tl(wdownT[half * 2816 + pc * 1408:half * 2816 + (pc + 1) * 1408,
                                       dc * 128:(dc + 1) * 128])),
                       waits=[(pe, max(prev, base_pe)), (dv, base_dv)], inc=(wB[idx % 2], 16))
                    bload2[idx] = wB[idx % 2].n

                load_b(0, 0)
                load_b(0, 1)
                for dc in range(NDK):
                    pf = [ps[(dc % 2) * 2], ps[(dc % 2) * 2 + 1]]
                    mpc = [0, 0]
                    for k in range(HALF):
                        pc = k // 11
                        kk = k % 11
                        buf = (2 * dc + pc) % 2
                        for ts in range(2):
                            isl = (half == 0 and k == HALF - 1 and ts == 1)
                            w0 = []
                            if kk == 0 and ts == 0:
                                w0 = [(wB[(2 * dc + pc) % 2], bload2[2 * dc + pc]), (dv, mmixed),
                                      (dv, bfree[dc - 2] if dc >= 2 else 0),
                                      (sc, slA_free[0]), (sc, slA_free[1]),
                                      (dv, slB_free[0]), (dv, slB_free[1]),
                                      (dv, silu1_free[0])]
                            ef('pe', lambda e, ts=ts, k=k, kk=kk, buf=buf, pX=pf[ts],
                               stp=(half == 0 and k == HALF - 1): e.matmul(
                                pX[:, :], wtB(buf)[:, kk * 128:(kk + 1) * 128], mixed(k, ts),
                                start=(k == 0), stop=stp),
                               waits=w0,
                               inc=(pe, 1) if (kk == 10 and ts == 1) or isl else None)
                        if kk == 10 and ts == 1:
                            mpc[pc] = pe.n
                        if k == 10:
                            bfree2[2 * dc] = pe.n
                            ni = 2 * dc + 2
                            if ni < 32:
                                load_b(ni // 2, ni % 2)
                        if k == HALF - 1:
                            bfree2[2 * dc + 1] = pe.n
                            ni = 2 * dc + 3
                            if ni < 32:
                                load_b(ni // 2, ni % 2)
                    for ts in range(2):
                        if half == 1:
                            ef('pe', lambda e, dc=dc, ts=ts, pX=pf[ts]: e.matmul(
                                pX[:, :], dnB_sb[:, dc * 128:(dc + 1) * 128], rho(0, ts),
                                start=False, stop=True),
                               waits=[(dP, mdnb[0]), (dv, mrho2[0])], inc=(pe, 1))
                        mpb = pe.n
                        ef('dv', lambda e, dc=dc, ts=ts, pX=pf[ts]: e.scalar_tensor_tensor(
                            rtile(dc, ts), pX[:, :], 0.5, rtile(dc, ts), ALU.mult, ALU.add),
                           waits=[(pe, mpb)], inc=(dv, 1))
                    bfree[dc] = dv.n
                psfree_dv[0] = dv.n

        def attention():
            base_pe, base_dv, base_sc = pe.n, dv.n, sc.n
            layer_norm(base_pe, base_dv, base_sc)
            my = S["y"]
            ef('gp', lambda e: e.dma_start(out=row_bv, in_=bvd),
               waits=[(dv, my)], inc=(dO, 16))
            mbv = dO.n
            for h4 in range(4):
                ef('pe', lambda e, h4=h4: e.matmul(ps[h4][:, :], ones_rb,
                                                   row_bv[:, h4 * 512:(h4 + 1) * 512],
                                                   start=True, stop=True),
                   waits=[(dO, mbv), (dv, my)], inc=(pe, 1))
            mbc2 = pe.n
            for h4 in range(4):
                ef('sc', lambda e, h4=h4: e.copy(bv_b[:, h4 * 512:(h4 + 1) * 512], ps[h4][:, :]),
                   waits=[(pe, mbc2)], inc=(sc, 1))
            mbve = sc.n
            pload = [0] * (2 * NDK)
            pfree = [0] * (2 * NDK)
            pfree_sc = [0] * (2 * NDK)
            kdma = {0: 0, 1: 0}
            kview = cin.rearrange("a (b c) -> (a b) c", b=2)

            def load_p(i):
                wpe = max(pfree[i - 2] if i >= 2 else 0, base_pe)
                Wd = wq_d if i < NDK else wk_d
                oc = i % NDK
                ef('gp', lambda e, Wd=Wd, oc=oc, i=i: e.dma_start(
                    out=_t3(wtP(i % 2), 128), in_=_tl(Wd[:, oc * 128:(oc + 1) * 128])),
                   waits=[(pe, wpe)], inc=(wP[i % 2], 16))
                pload[i] = wP[i % 2].n

            load_p(0)
            load_p(1)
            for i in range(2 * NDK):
                oc = i % NDK
                isq = i < NDK
                for ts in range(2):
                    pX = ps[(i % 2) * 2 + ts]
                    for dk in range(NDK):
                        ef('pe', lambda e, i=i, ts=ts, dk=dk, pX=pX: e.matmul(
                            pX[:, :], wtP(i % 2)[:, dk * 128:(dk + 1) * 128], ytile(dk, ts),
                            start=(dk == 0), stop=(dk == NDK - 1)),
                           waits=([(wP[i % 2], pload[i]), (dv, my), (sc, mbve),
                                   (dv, pfree_sc[i - 2] if i >= 2 else 0)] if dk == 0 else []),
                           inc=(pe, 1) if dk == NDK - 1 else None)
                    mproj = pe.n
                    bias = bq_s[:, oc:oc + 1] if isq else bk_s[:, oc:oc + 1]
                    scl = SM if isq else 1.0
                    ef('dv', lambda e, ts=ts, pX=pX, bias=bias, scl=scl: e.tensor_scalar(
                        kstg[:, ts * 512:(ts + 1) * 512], pX[:, :], bias, scl,
                        ALU.add, ALU.mult),
                       waits=[(pe, mproj), (dK[ts], kdma[ts])], inc=(dv, 1))
                    dst = (qdram[oc * 128:(oc + 1) * 128, ts * 512:(ts + 1) * 512] if isq
                           else kview[oc * 128:(oc + 1) * 128, ts * 512:(ts + 1) * 512])
                    ef('sy', lambda e, dst=dst, ts=ts: e.dma_start(
                        out=dst, in_=kstg[:, ts * 512:(ts + 1) * 512]),
                       waits=[(dv, dv.n)], inc=(dK[ts], 16))
                    kdma[ts] = dK[ts].n
                pfree[i] = pe.n
                pfree_sc[i] = dv.n
                if i + 2 < 2 * NDK:
                    load_p(i + 2)
            vw_free = [base_pe, base_pe]
            vev_free = [base_dv] * 4
            for tcb in range(2):
                for dvb in range(4):
                    mvw = 0
                    for dk in range(NDK):
                        buf = dk % 2
                        ef('gp', lambda e, dvb=dvb, dk=dk, buf=buf: e.dma_start(
                            out=wvt(buf), in_=wv_d[dk * 128:(dk + 1) * 128, dvb * 512:(dvb + 1) * 512]),
                           waits=[(pe, vw_free[buf]), (dv, base_dv)], inc=(wV[buf], 16))
                        mvw = wV[buf].n
                        for j4 in range(4):
                            tc = tcb * 4 + j4
                            ef('pe', lambda e, tc=tc, dk=dk, j4=j4, buf=buf: e.matmul(
                                psQ[j4][:, :], ytile(dk)[:, tc * 128:(tc + 1) * 128], wvt(buf),
                                start=(dk == 0), stop=(dk == NDK - 1)),
                               waits=([(wV[buf], mvw), (dv, vev_free[j4]), (dv, my)] if True else []),
                               inc=(pe, 1) if j4 == 3 else None)
                        vw_free[buf] = pe.n
                    mvp = pe.n
                    for j4 in range(4):
                        tc = tcb * 4 + j4
                        sslot = j4 % 2
                        ef('dv', lambda e, j4=j4, dvb=dvb, sslot=sslot: e.tensor_tensor(
                            kstg[:, sslot * 512:sslot * 512 + 512], psQ[j4][:, :],
                            bv_b[:, dvb * 512:(dvb + 1) * 512], ALU.add),
                           waits=[(pe, mvp), (sc, mbve), (dK[sslot], kdma[sslot])], inc=(dv, 1))
                        vev_free[j4] = dv.n
                        ef('sy', lambda e, tc=tc, dvb=dvb, sslot=sslot: e.dma_start(
                            out=cin[1024 + tc * 128:1024 + (tc + 1) * 128, dvb * 512:(dvb + 1) * 512],
                            in_=kstg[:, sslot * 512:sslot * 512 + 512]),
                           waits=[(dv, dv.n)], inc=(dK[sslot], 16))
                        kdma[sslot] = dK[sslot].n
            mcin = None
            ef('gp', lambda e: e.collective_compute(
                "AllGather", ALU.bypass, replica_groups=[[0, 1], [2, 3], [4, 5], [6, 7]],
                ins=[cin.opt()], outs=[cout.opt()]),
               waits=[(dK[0], dK[0].n), (dK[1], dK[1].n)], inc=(cc[2], 1))
            ef('gp', lambda e: e.dma_start(out=amask_sb, in_=amask_d),
               waits=[(dv, mrho_dead_dv[0])], inc=(dO, 16))
            mam = dO.n
            kvout = cout.rearrange("a (b c) -> (a b) c", b=2)
            hload = [0] * NH
            hvl = [0, 0]
            hfree_q = [base_dv, base_dv]
            hfree_k = [base_pe, base_pe]
            hfree_v = [base_pe, base_pe]
            sco_free = [base_sc, base_sc]
            pb_free = [base_pe, base_pe]
            ctx_free = [base_dv]
            l_free = [base_sc]

            def load_head(h):
                buf = h % 2
                sem = dH[buf]
                ef('sy', lambda e, h=h, buf=buf: e.dma_start(
                    out=q_t[buf], in_=qdram[h * 128:(h + 1) * 128, :]),
                   waits=[(dv, hfree_q[buf])], inc=(sem, 16))
                for j in range(2):
                    src = kvout[4096 * j + h * 128:4096 * j + (h + 1) * 128, :]
                    dst = k_t[buf].rearrange("p (g c) -> p g c", c=128)[:, j::2, :]
                    ef('sy', lambda e, src=src, dst=dst: e.dma_start(
                        out=dst, in_=src.rearrange("p (g c) -> p g c", c=128)),
                       waits=[(cc[2], 1), (pe, hfree_k[buf])], inc=(sem, 16))
                vbuf = buf
                for j in range(2):
                    src = cout[2048 * j + 1024:2048 * j + 2048, h * 128:(h + 1) * 128]
                    dst = v_t[vbuf].rearrange("p (g w) -> p g w", w=128)[:, j::2, :]
                    ef('sy', lambda e, src=src, dst=dst: e.dma_start(
                        out=dst, in_=src.rearrange("(g p) w -> p g w", p=128)),
                       waits=[(cc[2], 1), (pe, hfree_v[vbuf])], inc=(sem, 16))
                hvl[vbuf] = sem.n
                hload[h] = sem.n

            load_head(0)
            load_head(1)
            for h in range(NH):
                buf = h % 2
                vbuf = h % 2
                hp = h % 2
                mpv = 0
                for kt in range(16):
                    s0 = 128 * (kt // 2)
                    pr = [ps[0], ps[1]] if kt % 2 == 0 else [ps[2], ps[3]]
                    segs = ([(0, s0, 512)] if s0 < 512 else []) + [(1, max(512, s0), 1024)]
                    for n_, (si, a, b_) in enumerate(segs):
                        ef('pe', lambda e, kt=kt, buf=buf, pr=pr, si=si, a=a, b_=b_: e.matmul(
                            pr[si][:, a - si * 512:b_ - si * 512],
                            k_t[buf][:, kt * 128:(kt + 1) * 128], q_t[buf][:, a:b_],
                            start=True, stop=True),
                           waits=([(dH[buf], hload[h]), (sc, sco_free[kt % 2]),
                                   (sc, l_free[0])] if n_ == 0 else []),
                           inc=(pe, 1) if n_ == len(segs) - 1 else None)
                    msco = pe.n
                    for (si, a, b_) in segs:
                        ef('sc', lambda e, kt=kt, si=si, a=a, b_=b_, pr=pr: e.activation(
                            probsT(kt % 2)[:, a:b_], pr[si][:, a - si * 512:b_ - si * 512], AF.Exp),
                           waits=[(pe, max(msco, pb_free[kt % 2])), (dO, mam)], inc=(sc, 1))
                    msce2 = sc.n
                    sco_free[kt % 2] = msce2
                    ef('dv', lambda e, kt=kt, s0=s0: e.tensor_tensor(
                        probsT(kt % 2)[:, s0:s0 + 128], probsT(kt % 2)[:, s0:s0 + 128],
                        amask_sb[:, kt * 128:(kt + 1) * 128], ALU.mult),
                       waits=[(sc, msce2)], inc=(dv, 1))
                    mmk = dv.n
                    for n_, (si, a, b_) in enumerate(segs):
                        stp = (kt == (15 if si == 1 else 7))
                        ef('pe', lambda e, h=h, kt=kt, si=si, a=a, b_=b_, vbuf=vbuf, hp=hp, stp=stp: e.matmul(
                            psQ[si][:, a - si * 512:b_ - si * 512],
                            v_t[vbuf][:, kt * 128:(kt + 1) * 128],
                            probsT(kt % 2)[:, a:b_],
                            start=(kt == 0), stop=stp),
                           waits=([(dH[vbuf], hvl[vbuf]), (dv, mmk), (dv, ctx_free[0])] if n_ == 0 else []))
                        ef('pe', lambda e, kt=kt, si=si, a=a, b_=b_, stp=stp: e.matmul(
                            psQ[2 + si][0:1, a - si * 512:b_ - si * 512], ones_cb,
                            probsT(kt % 2)[:, a:b_],
                            start=(kt == 0), stop=stp),
                           inc=(pe, 1) if n_ == len(segs) - 1 else None)
                    pb_free[kt % 2] = pe.n
                    mpv = pe.n
                ef('dv', lambda e: e.reciprocal(linv_f[:, 0:512], psQ[2][0:1, :]),
                   waits=[(pe, mpv)], inc=(dv, 1))
                ef('dv', lambda e: e.reciprocal(linv_f[:, 512:1024], psQ[3][0:1, :]), inc=(dv, 1))
                ef('dv', lambda e: e.tensor_copy(linv_hi, linv_f), inc=(dv, 1))
                ef('dv', lambda e: e.tensor_tensor(linv_lo, linv_f, linv_hi, ALU.subtract), inc=(dv, 1))
                ml = dv.n
                for si in range(2):
                    ef('pe', lambda e, si=si: e.matmul(psQ[2 + si][:, :], ones_rb,
                                                       linv_hi[:, si * 512:(si + 1) * 512],
                                                       start=True, stop=False),
                       waits=[(dv, ml)])
                    ef('pe', lambda e, si=si: e.matmul(psQ[2 + si][:, :], ones_rb,
                                                       linv_lo[:, si * 512:(si + 1) * 512],
                                                       start=False, stop=True), inc=(pe, 1))
                mlb = pe.n
                for si in range(2):
                    ef('sc', lambda e, si=si: e.copy(linv_b[:, si * 512:(si + 1) * 512], psQ[2 + si][:, :]),
                       waits=[(pe, mlb)], inc=(sc, 1))
                l_free[0] = sc.n
                for si in range(2):
                    ef('dv', lambda e, h=h, si=si: e.tensor_tensor(
                        ytile(h)[:, si * 512:(si + 1) * 512], psQ[si][:, :],
                        linv_b[:, si * 512:(si + 1) * 512], ALU.mult),
                       waits=[(sc, l_free[0])], inc=(dv, 1))
                ctx_free[0] = dv.n
                hfree_q[buf] = ctx_free[0]
                hfree_k[buf] = mpv
                hfree_v[vbuf] = mpv
                if h + 2 < NH:
                    load_head(h + 2)
            mattn = dv.n
            oload = [0] * NDK
            ofree = [0] * NDK

            def load_o(dc):
                ef('gp', lambda e, dc=dc: e.dma_start(
                    out=_t3(wtP(dc % 2), 128), in_=_tl(wo_d[:, dc * 128:(dc + 1) * 128])),
                   waits=[(pe, pfree[2 * NDK - 1]),
                          (dv, ofree[dc - 2] if dc >= 2 else 0)], inc=(wP[dc % 2], 16))
                oload[dc] = wP[dc % 2].n

            load_o(0)
            load_o(1)
            for dc in range(NDK):
                for ts in range(2):
                    pX = ps[(dc % 2) * 2 + ts]
                    for hh in range(NH):
                        ef('pe', lambda e, dc=dc, ts=ts, hh=hh, pX=pX: e.matmul(
                            pX[:, :], wtP(dc % 2)[:, hh * 128:(hh + 1) * 128], ytile(hh, ts),
                            start=(hh == 0), stop=(hh == NH - 1)),
                           waits=([(wP[dc % 2], oload[dc]), (dv, mattn), (sc, l_free[0]),
                                   (dv, ofree[dc - 2] if dc >= 2 else 0)] if hh == 0 else []),
                           inc=(pe, 1) if hh == NH - 1 else None)
                    mo = pe.n
                    ef('dv', lambda e, dc=dc, ts=ts, pX=pX: e.tensor_tensor(
                        rtile(dc, ts), pX[:, :], rtile(dc, ts), ALU.add),
                       waits=[(pe, mo)], inc=(dv, 1))
                ofree[dc] = dv.n
                if dc + 2 < NDK:
                    load_o(dc + 2)

        mrho_dead_dv = [0]
        ffn(0)
        mrho_dead_dv[0] = dv.n
        if STAGE >= 2:
            attention()
        if STAGE >= 3:
            ffn(1)

        ef('sy', lambda e: e.dma_start(out=_tl(out), in_=_t3(resid[:, :], T)),
           waits=[(dv, dv.n), (pe, pe.n), (sc, sc.n)], inc=(dOUT, 16))
        ef('sy', lambda e: e.nop(), waits=[(dOUT, dOUT.n)])

        @block.sync
        def _(e):
            P.emit('sy', e)

        @block.gpsimd
        def _(e):
            P.emit('gp', e)

        @block.tensor
        def _(e):
            P.emit('pe', e)

        @block.scalar
        def _(e):
            P.emit('sc', e)

        @block.vector
        def _(e):
            P.emit('dv', e)

    return nc


_NC_CACHE = {}


def _amask(j):
    m = np.zeros((128, 16 * 128), dtype=np.float32)
    for kt in range(16):
        m0 = kt // 2
        qt = 2 * m0 + j
        gk = kt * 128 + np.arange(128)[:, None]
        gq = qt * 128 + np.arange(128)[None, :]
        m[:, kt * 128:(kt + 1) * 128] = (gq >= gk).astype(np.float32)
    return m


def kernel(L0_attn_hidden_states, L0_attn_Wq, L0_attn_bq, L0_attn_Wk, L0_attn_bk,
           L0_attn_Wv, L0_attn_bv, L0_attn_Wo, L0_ffn_gate_w, L0_ffn_w_up,
           L0_ffn_w_gate, L0_ffn_w_down, L0_ffn_up_A, L0_ffn_up_B, L0_ffn_gate_A,
           L0_ffn_gate_B, L0_ffn_down_A, L0_ffn_down_B):
    x = np.ascontiguousarray(np.asarray(L0_attn_hidden_states, dtype=np.float32))
    B = x.shape[0]
    cfg = lambda a: np.ascontiguousarray(np.asarray(a, dtype=np.float32))

    gate_w = cfg(L0_ffn_gate_w)
    gwt = np.zeros((128, 128), dtype=np.float32)
    for dk in range(NDK):
        gwt[:, dk * 8:(dk + 1) * 8] = gate_w[:, dk * 128:(dk + 1) * 128].T
    common = {
        "wq": cfg(L0_attn_Wq), "wk": cfg(L0_attn_Wk), "wv": cfg(L0_attn_Wv),
        "wo": cfg(L0_attn_Wo),
        "bqd": cfg(np.asarray(L0_attn_bq, dtype=np.float32).reshape(NDK, 128).T),
        "bkd": cfg(np.asarray(L0_attn_bk, dtype=np.float32).reshape(NDK, 128).T),
        "bvd": cfg(np.asarray(L0_attn_bv, dtype=np.float32).reshape(1, D)),
        "wupT": cfg(np.asarray(L0_ffn_w_up).T),
        "wgateT": cfg(np.asarray(L0_ffn_w_gate).T),
        "wdownT": cfg(np.asarray(L0_ffn_w_down).T),
        "upAT": cfg(np.asarray(L0_ffn_up_A).reshape(128, D).T),
        "gateAT": cfg(np.asarray(L0_ffn_gate_A).reshape(128, D).T),
        "upBT": cfg(np.asarray(L0_ffn_up_B).transpose(0, 2, 1).reshape(128, H)),
        "gateBT": cfg(np.asarray(L0_ffn_gate_B).transpose(0, 2, 1).reshape(128, H)),
        "dnAT": cfg(np.asarray(L0_ffn_down_A).reshape(128, H).T),
        "dnBT": cfg(np.asarray(L0_ffn_down_B).transpose(0, 2, 1).reshape(128, D)),
        "gwt": gwt,
        "identf": np.eye(128, dtype=np.float32),
        "ones_r": np.ones((1, 128), dtype=np.float32),
        "ones_c": np.ones((128, 1), dtype=np.float32),
        "ecol_d": (np.arange(128, dtype=np.float32) // 16).reshape(128, 1),
        "iota8_d": np.broadcast_to(np.arange(8, dtype=np.float32), (128, 8)).copy(),
    }
    amasks = [_amask(0), _amask(1)]

    if "nc" not in _NC_CACHE:
        _NC_CACHE["nc"] = build_nc()
    nc = _NC_CACHE["nc"]

    in_maps = []
    for c in range(8):
        b, j = c // 2, c % 2
        xt = x[b].reshape(16, 128, D)[j::2].reshape(T, D).T
        m = dict(common)
        m["xs"] = np.ascontiguousarray(xt)
        m["amask_d"] = amasks[j]
        in_maps.append(m)

    res = run_bass_kernel_spmd(nc, in_maps, core_ids=list(range(8)))

    y = np.empty_like(x)
    for c in range(8):
        b, j = c // 2, c % 2
        oc = np.asarray(res.results[c]["out"])
        yt = oc.T.reshape(NTC, 128, D)
        for i in range(NTC):
            y[b, (2 * i + j) * 128:(2 * i + j + 1) * 128, :] = yt[i]
    return y


# version 08
# speedup vs baseline: 1.3690x; 1.0651x over previous
import math
import numpy as np
import ml_dtypes
import concourse.bass as bass
import concourse.mybir as mybir
from concourse.bass_utils import run_bass_kernel_spmd
from contextlib import ExitStack

F32 = mybir.dt.float32
F32R = mybir.dt.float32r
BF16 = mybir.dt.bfloat16
AF = mybir.ActivationFunctionType
ALU = mybir.AluOpType
AX = mybir.AxisListType

D, H, NH, DH, T, E, R = 2048, 5632, 16, 128, 1024, 8, 16
NDK = D // 128
NHC = H // 128
HALF = NHC // 2
NTC = T // 128
TS = 512
SM = 1.0 / math.sqrt(DH)
STAGE = 3


class _Sem:
    __slots__ = ("name", "n", "h")
    def __init__(self, name):
        self.name, self.n, self.h = name, 0, None


class _B:
    def __init__(self):
        self.ops = []
        self.sems = []
        self._lw = {}
        self.auto_chain = {}

    def sem(self, name):
        s = _Sem(name)
        self.sems.append(s)
        return s

    def op(self, eng, fn, waits=(), inc=None):
        waits = list(waits)
        if eng in self.auto_chain:
            s = self.auto_chain[eng]
            waits.append((s, s.n))
            if inc is None:
                inc = (s, 1)
        w = []
        for (s, v) in waits:
            if v <= 0:
                continue
            k = (eng, s.name)
            if self._lw.get(k, -1) >= v:
                continue
            self._lw[k] = v
            w.append((s, v))
        if inc is not None:
            inc[0].n += inc[1]
        self.ops.append((eng, fn, w, inc))
        return inc[0].n if inc else None

    def emit(self, name, eng):
        for (e, fn, w, inc) in self.ops:
            if e != name:
                continue
            for (s, v) in w:
                eng.wait_ge(s.h, v)
            inst = fn(eng)
            if inc is not None:
                inst.then_inc(inc[0].h, inc[1])


def _tl(src2d):
    return src2d.rearrange("(a p) c -> p a c", p=128)


def _t3(dst2d, c):
    return dst2d.rearrange("p (a c) -> p a c", c=c)


def build_nc():
    nc = bass.Bass("TRN2", target_bir_lowering=False)

    xs = nc.dram_tensor("xs", [D, T], F32, kind="ExternalInput").ap()
    wq_d = nc.dram_tensor("wq", [D, D], F32, kind="ExternalInput").ap()
    wk_d = nc.dram_tensor("wk", [D, D], F32, kind="ExternalInput").ap()
    wv_d = nc.dram_tensor("wv", [D, D], F32, kind="ExternalInput").ap()
    wo_d = nc.dram_tensor("wo", [D, D], F32, kind="ExternalInput").ap()
    bqd = nc.dram_tensor("bqd", [128, NDK], F32, kind="ExternalInput").ap()
    bkd = nc.dram_tensor("bkd", [128, NDK], F32, kind="ExternalInput").ap()
    bvd = nc.dram_tensor("bvd", [1, D], F32, kind="ExternalInput").ap()
    wupT = nc.dram_tensor("wupT", [D, H], F32, kind="ExternalInput").ap()
    wgateT = nc.dram_tensor("wgateT", [D, H], F32, kind="ExternalInput").ap()
    wdownT = nc.dram_tensor("wdownT", [H, D], F32, kind="ExternalInput").ap()
    upAT = nc.dram_tensor("upAT", [D, 128], F32, kind="ExternalInput").ap()
    gateAT = nc.dram_tensor("gateAT", [D, 128], F32, kind="ExternalInput").ap()
    upBT = nc.dram_tensor("upBT", [128, H], F32, kind="ExternalInput").ap()
    gateBT = nc.dram_tensor("gateBT", [128, H], F32, kind="ExternalInput").ap()
    dnAT = nc.dram_tensor("dnAT", [H, 128], F32, kind="ExternalInput").ap()
    dnBT = nc.dram_tensor("dnBT", [128, D], F32, kind="ExternalInput").ap()
    gwt = nc.dram_tensor("gwt", [128, 128], F32, kind="ExternalInput").ap()
    identf = nc.dram_tensor("identf", [128, 128], F32, kind="ExternalInput").ap()
    ones_r = nc.dram_tensor("ones_r", [1, 128], F32, kind="ExternalInput").ap()
    ones_c = nc.dram_tensor("ones_c", [128, 1], F32, kind="ExternalInput").ap()
    ecol_d = nc.dram_tensor("ecol_d", [128, 1], F32, kind="ExternalInput").ap()
    iota8_d = nc.dram_tensor("iota8_d", [128, 8], F32, kind="ExternalInput").ap()
    amask_d = nc.dram_tensor("amask_d", [128, 16 * 128], BF16, kind="ExternalInput").ap()
    out = nc.dram_tensor("out", [D, T], F32, kind="ExternalOutput").ap()

    qdram = nc.dram_tensor("qdram", [D, T], BF16).ap()
    cin = nc.dram_tensor("cin", [2048, 2048], BF16).ap()
    cout = nc.dram_tensor("cout", [4096, 2048], BF16).ap()
    trd = nc.dram_tensor("trd", [32, 128], F32).ap()
    gin = [nc.dram_tensor(f"gin{i}", [1, 2], F32).ap() for i in range(2)]
    gout = [nc.dram_tensor(f"gout{i}", [2, 2], F32).ap() for i in range(2)]

    P = _B()
    dA = P.sem("dA")
    dO = P.sem("dO")
    wA = [P.sem("wA0"), P.sem("wA1")]
    wB = [P.sem("wB0"), P.sem("wB1")]
    wP = [P.sem("wP0"), P.sem("wP1")]
    wV = [P.sem("wV0"), P.sem("wV1")]
    dK = [P.sem("dK0"), P.sem("dK1")]
    dH = [P.sem("dH0"), P.sem("dH1")]
    dOUT = P.sem("dOUT")
    dP = P.sem("dP")
    wS = [P.sem("wS0"), P.sem("wS1")]
    cc = [P.sem(f"cc{i}") for i in range(3)]
    pe, sc, dv = P.sem("pe"), P.sem("sc"), P.sem("dv")
    P.auto_chain = {"dv": dv, "sc": sc}

    with ExitStack() as stk:
        en = stk.enter_context
        resid = en(nc.sbuf_tensor("resid", [128, NDK * 1024], F32))
        act = en(nc.sbuf_tensor("act", [128, NDK * 1024], BF16))
        arena = en(nc.sbuf_tensor("arena", [128, HALF * 1024], BF16))
        wa = en(nc.sbuf_tensor("wa", [128, 4 * 2048], BF16))
        wb = en(nc.sbuf_tensor("wb", [128, 2 * 1408], BF16))
        wsm = en(nc.sbuf_tensor("wsm", [128, 6 * 128], BF16))
        lnc = en(nc.sbuf_tensor("lnc", [128, 2048], BF16))
        sf = en(nc.sbuf_tensor("sf", [128, 2560], F32))
        sb = en(nc.sbuf_tensor("sb", [128, 10240], BF16))
        rowz = en(nc.sbuf_tensor("rowz", [1, 4], F32))
        smalls = en(nc.sbuf_tensor("smalls", [128, 64], F32))
        pack = en(nc.sbuf_tensor("pack", [128, 32], F32))
        trsb = en(nc.sbuf_tensor("trsb", [32, 128], F32))
        consts = en(nc.sbuf_tensor("consts", [128, 48], F32))
        cbf = en(nc.sbuf_tensor("cbf", [128, 16], BF16))
        crow = en(nc.sbuf_tensor("crow", [1, 192], F32))
        gstage = en(nc.sbuf_tensor("gstage", [1, 8], F32))
        kstg = en(nc.sbuf_tensor("kstg", [128, 2 * 512], BF16))
        gwtile = en(nc.sbuf_tensor("gwtile", [128, 128], BF16))
        identsb = en(nc.sbuf_tensor("identsb", [128, 128], F32))
        ps = [en(nc.psum_tensor(f"ps{i}", [128, 512], F32)) for i in range(4)]
        psQ = [en(nc.psum_tensor(f"psQ{i}", [128, 512], F32)) for i in range(4)]
        for s in P.sems:
            s.h = en(nc.semaphore(s.name))
        block = en(nc.Block())

        def rtile(dc, ts=None):
            a = resid[:, dc * 1024:(dc + 1) * 1024]
            return a if ts is None else a[:, ts * TS:(ts + 1) * TS]

        def ytile(dc, ts=None):
            a = act[:, dc * 1024:(dc + 1) * 1024]
            return a if ts is None else a[:, ts * TS:(ts + 1) * TS]

        ecol = consts[:, 0:1]
        iotasb = consts[:, 8:16]
        bq_s = consts[:, 16:32]
        bk_s = consts[:, 32:48]
        ones_cb = cbf[:, 0:1]
        ones_cf = consts[:, 1:2]
        ones_rf = crow[:, 0:128]
        ones_rb = crow[:, 128:192].bitcast(BF16)
        m0A, m1A = smalls[:, 0:8], smalls[:, 8:16]
        e0A, e1A = smalls[:, 16:24], smalls[:, 24:32]
        eqs, tmp8, lmt = smalls[:, 32:40], smalls[:, 40:48], smalls[:, 48:56]
        s01 = smalls[:, 56:58]
        s01r = smalls[:, 58:60].bitcast(F32R)
        rzb = smalls[:, 60:62]

        row_mean = sb[0:1, 0:2048].bitcast(F32)
        row_var = sb[0:1, 2048:4096].bitcast(F32)
        rstd_hi = sb[0:1, 4096:5120]
        rstd_lo = sb[0:1, 5120:6144]
        bneg_hi = sb[0:1, 6144:7168]
        bneg_lo = sb[0:1, 7168:8192]
        w_hi = [sb[0:1, 0:1024], sb[0:1, 2048:3072]]
        w_lo = [sb[0:1, 1024:2048], sb[0:1, 3072:4096]]
        row_wf = [sb[0:1, 4096:6144].bitcast(F32), sb[0:1, 6144:8192].bitcast(F32)]
        row_idx = [sb[0:1, 8192:9216], sb[0:1, 9216:10240]]
        linv_f = sb[0:1, 4096:6144].bitcast(F32)
        linv_hi = sb[0:1, 6144:7168]
        linv_lo = sb[0:1, 7168:8192]
        row_bv = sb[0:1, 8192:10240]

        rstd_b = sf[:, 0:1024]
        bneg_b = sf[:, 1024:2048]
        bA = sf[:, 0:512]
        bB = sf[:, 512:1024]
        s1t = sf[:, 1024:1536]
        t2t = sf[:, 1536:2048]
        siluX = sf[:, 2048:2560]
        bv_b = sf[:, 0:2048]
        linv_b = sf[:, 0:1024]

        def rho(i, ts):
            o = (i * 2 + ts) * 512
            return sb[:, o:o + 512]

        def crowb(i, ts):
            o = 4096 + (i * 2 + ts) * 512
            return sb[:, o:o + 512]

        def Mm(i, ts):
            o = 6144 + (i * 2 + ts) * 512
            return sb[:, o:o + 512]

        def probsT(buf):
            return sb[:, buf * 1024:(buf + 1) * 1024]
        amask_sb = sb[:, 2048:4096]
        cht = [sb[:, 8192 + i * 512:8192 + (i + 1) * 512] for i in range(4)]

        def mixed(k, ts):
            o = k * 1024 + ts * 512
            return arena[:, o:o + 512]
        q_t = [arena[:, i * 1024:(i + 1) * 1024] for i in range(2)]
        k_t = [arena[:, 2048 + i * 2048:2048 + (i + 1) * 2048] for i in range(2)]
        v_t = [arena[:, 6144 + i * 2048:6144 + (i + 1) * 2048] for i in range(2)]

        def wtA(slot, buf):
            o = (slot * 2 + buf) * 2048
            return wa[:, o:o + 2048]

        def wtP(buf):
            return wa[:, buf * 2048:buf * 2048 + 2048]

        def wtB(buf):
            return wb[:, buf * 1408:buf * 1408 + 1408]

        def wvt(buf):
            return wb[:, buf * 640:buf * 640 + 512]

        def btl(slot, buf):
            o = (slot * 2 + buf) * 128
            return wsm[:, o:o + 128]

        upA_sb = lnc[:, 0:2048]
        dnB_sb = lnc[:, 0:2048]
        xsq = [lnc[:, 0:1024]] * 2
        xbf = [lnc[:, 1024:2048]] * 2

        ef = P.op
        S = {"y": 0}

        ef('sy', lambda e: e.dma_start(out=_t3(resid[:, :], T), in_=_tl(xs)), inc=(dA, 16))
        mx = dA.n
        ef('sy', lambda e: e.dma_start(out=identsb[:, :], in_=identf), inc=(dA, 16))
        ef('sy', lambda e: e.dma_start(out=consts[:, 0:1], in_=ecol_d), inc=(dA, 16))
        ef('sy', lambda e: e.dma_start(out=consts[:, 8:16], in_=iota8_d), inc=(dA, 16))
        ef('sy', lambda e: e.dma_start(out=consts[:, 16:32], in_=bqd), inc=(dA, 16))
        ef('sy', lambda e: e.dma_start(out=consts[:, 32:48], in_=bkd), inc=(dA, 16))
        mca = dA.n
        mx = dA.n
        ef('gp', lambda e: e.dma_start(out=ones_cb, in_=ones_c), inc=(dO, 16))
        ef('gp', lambda e: e.dma_start(out=ones_rb, in_=ones_r), inc=(dO, 16))
        ef('sy', lambda e: e.dma_start(out=ones_rf, in_=ones_r), inc=(dA, 16))
        ef('sy', lambda e: e.dma_start(out=ones_cf, in_=ones_c), inc=(dA, 16))
        ef('gp', lambda e: e.dma_start(out=gwtile[:, :], in_=gwt), inc=(dO, 16))
        mcg = dO.n

        def layer_norm(base_pe, base_dv, base_sc):
            mson = [0] * NDK
            for dc in range(NDK):
                b = dc % 2
                wpe = max(mson[dc - 1] if dc >= 1 else 0, base_pe)
                ef('dv', lambda e, dc=dc, b=b: e.tensor_tensor(xsq[b], rtile(dc), rtile(dc), ALU.mult),
                   waits=[(dA, dA.n), (pe, wpe), (dv, base_dv)], inc=(dv, 1))
                mdv = dv.n
                ef('sc', lambda e, dc=dc, b=b: e.copy(xbf[b], rtile(dc)),
                   waits=[(dA, dA.n), (pe, wpe), (dv, base_dv), (sc, base_sc)], inc=(sc, 1))
                msc = sc.n
                for h2 in range(2):
                    tgt = ps[h2]
                    ef('pe', lambda e, b=b, h2=h2, tgt=tgt, dc=dc: e.matmul(
                        tgt[0:1, :], ones_cb, xsq[b][:, h2 * 512:(h2 + 1) * 512],
                        start=(dc == 0), stop=(dc == NDK - 1)),
                       waits=[(dv, mdv), (dO, mcg), (dv, base_dv), (sc, base_sc)])
                    tgt2 = ps[2 + h2]
                    mson[dc] = ef('pe', lambda e, b=b, h2=h2, tgt2=tgt2, dc=dc: e.matmul(
                        tgt2[0:1, :], ones_cb, xbf[b][:, h2 * 512:(h2 + 1) * 512],
                        start=(dc == 0), stop=(dc == NDK - 1)),
                       waits=[(sc, msc)], inc=(pe, 1))
            m_stats = pe.n
            S["lnstats"] = m_stats
            ef('sc', lambda e: e.activation(row_mean[:, 0:512], ps[2][0:1, :], AF.Copy, scale=1.0 / D),
               waits=[(pe, m_stats)], inc=(sc, 1))
            ef('sc', lambda e: e.activation(row_mean[:, 512:1024], ps[3][0:1, :], AF.Copy, scale=1.0 / D), inc=(sc, 1))
            ef('sc', lambda e: e.activation(row_var[:, 0:512], ps[0][0:1, :], AF.Copy, scale=1.0 / D), inc=(sc, 1))
            ef('sc', lambda e: e.activation(row_var[:, 512:1024], ps[1][0:1, :], AF.Copy, scale=1.0 / D), inc=(sc, 1))
            mrow = sc.n
            ef('dv', lambda e: e.tensor_tensor(ps[0][0:1, :], row_mean[:, 0:512], row_mean[:, 0:512], ALU.mult),
               waits=[(sc, mrow)], inc=(dv, 1))
            ef('dv', lambda e: e.tensor_tensor(ps[1][0:1, :], row_mean[:, 512:1024], row_mean[:, 512:1024], ALU.mult), inc=(dv, 1))
            ef('dv', lambda e: e.tensor_tensor(row_var[:, 0:512], row_var[:, 0:512], ps[0][0:1, :], ALU.subtract), inc=(dv, 1))
            ef('dv', lambda e: e.tensor_tensor(row_var[:, 512:1024], row_var[:, 512:1024], ps[1][0:1, :], ALU.subtract), inc=(dv, 1))
            mv = dv.n
            ef('dv', lambda e: e.tensor_single_scalar(row_var[:, :], row_var[:, :], 1e-5, ALU.add),
               waits=[(dv, mv)], inc=(dv, 1))
            ef('sc', lambda e: e.activation(row_var[:, :], row_var[:, :], AF.Sqrt),
               waits=[(dv, dv.n)], inc=(sc, 1))
            ef('dv', lambda e: e.reciprocal(row_var[:, :], row_var[:, :]), waits=[(sc, sc.n)], inc=(dv, 1))
            ef('dv', lambda e: e.scalar_tensor_tensor(row_mean[:, :], row_mean[:, :], -1.0,
                                                      row_var[:, :], ALU.mult, ALU.mult), inc=(dv, 1))
            mrs = dv.n
            for h2 in range(2):
                ef('pe', lambda e, h2=h2: e.matmul(ps[h2][:, :], ones_rf,
                                                   row_var[:, h2 * 512:(h2 + 1) * 512],
                                                   start=True, stop=True),
                   waits=[(dv, mrs), (dA, dA.n)], inc=(pe, 1))
                ef('pe', lambda e, h2=h2: e.matmul(ps[2 + h2][:, :], ones_rf,
                                                   row_mean[:, h2 * 512:(h2 + 1) * 512],
                                                   start=True, stop=True), inc=(pe, 1))
            mbc = pe.n
            ef('sc', lambda e: e.copy(rstd_b[:, 0:512], ps[0][:, :]), waits=[(pe, mbc)], inc=(sc, 1))
            ef('sc', lambda e: e.copy(rstd_b[:, 512:1024], ps[1][:, :]), inc=(sc, 1))
            ef('sc', lambda e: e.copy(bneg_b[:, 0:512], ps[2][:, :]), inc=(sc, 1))
            ef('sc', lambda e: e.copy(bneg_b[:, 512:1024], ps[3][:, :]), inc=(sc, 1))
            mbce = sc.n
            mnorm = [0] * NDK
            for dc in range(NDK):
                pa, pb = ps[2 * (dc % 2)], ps[1 + 2 * (dc % 2)]
                wv_ = max(mnorm[dc - 2] if dc >= 2 else 0, 0)
                m = 0
                for h2 in range(2):
                    tgt = pa if h2 == 0 else pb
                    ef('dv', lambda e, dc=dc, h2=h2, tgt=tgt: e.tensor_tensor(
                        tgt[:, :], rtile(dc)[:, h2 * 512:(h2 + 1) * 512],
                        rstd_b[:, h2 * 512:(h2 + 1) * 512], ALU.mult),
                       waits=[(sc, mbce), (dv, wv_)])
                    m = ef('dv', lambda e, dc=dc, h2=h2, tgt=tgt: e.tensor_tensor(
                        ytile(dc)[:, h2 * 512:(h2 + 1) * 512], tgt[:, :],
                        bneg_b[:, h2 * 512:(h2 + 1) * 512], ALU.add), inc=(dv, 1))
                mnorm[dc] = m
            S["y"] = dv.n
            return dv.n

        def gating(call):
            my = S["y"]
            mls = [0] * NTC
            for tc in range(NTC):
                Lp = ps[tc % 2]
                wdv = mls[tc - 2] if tc >= 2 else 0
                for dk in range(NDK):
                    ef('pe', lambda e, tc=tc, dk=dk, Lp=Lp: e.matmul(
                        Lp[:, 0:8], ytile(dk)[:, tc * 128:(tc + 1) * 128],
                        gwtile[:, dk * 8:(dk + 1) * 8],
                        start=(dk == 0), stop=(dk == NDK - 1)),
                       waits=[(dv, max(my, wdv)), (dO, mcg)],
                       inc=(pe, 1) if dk == NDK - 1 else None)
                mL = pe.n
                ef('dv', lambda e, tc=tc, Lp=Lp: e.tensor_reduce(m0A[:, tc:tc + 1], Lp[:, 0:8], AX.X, ALU.max),
                   waits=[(pe, mL), (dA, dA.n)], inc=(dv, 1))
                ef('sc', lambda e, tc=tc: e.activation(e0A[:, tc:tc + 1], m0A[:, tc:tc + 1], AF.Exp),
                   waits=[(dv, dv.n)], inc=(sc, 1))
                ef('dv', lambda e, tc=tc, Lp=Lp: e.tensor_single_scalar(eqs, Lp[:, 0:8], m0A[:, tc:tc + 1], ALU.is_equal))
                ef('dv', lambda e: e.scalar_tensor_tensor(tmp8, eqs, -1000.0, iotasb, ALU.mult, ALU.add))
                ef('dv', lambda e, tc=tc: e.tensor_reduce(pack[:, tc:tc + 1], tmp8, AX.X, ALU.min))
                ef('dv', lambda e, tc=tc: e.tensor_single_scalar(pack[:, tc:tc + 1], pack[:, tc:tc + 1], 1000.0, ALU.add))
                ef('dv', lambda e, tc=tc: e.tensor_single_scalar(eqs, iotasb, pack[:, tc:tc + 1], ALU.is_equal))
                ef('dv', lambda e, Lp=Lp: e.scalar_tensor_tensor(lmt, eqs, -1e30, Lp[:, 0:8], ALU.mult, ALU.add),
                   inc=(dv, 1))
                mls[tc] = dv.n
                ef('dv', lambda e, tc=tc: e.tensor_reduce(m1A[:, tc:tc + 1], lmt, AX.X, ALU.max), inc=(dv, 1))
                ef('sc', lambda e, tc=tc: e.activation(e1A[:, tc:tc + 1], m1A[:, tc:tc + 1], AF.Exp),
                   waits=[(dv, dv.n)], inc=(sc, 1))
                ef('dv', lambda e, tc=tc: e.tensor_single_scalar(eqs, lmt, m1A[:, tc:tc + 1], ALU.is_equal))
                ef('dv', lambda e: e.scalar_tensor_tensor(tmp8, eqs, -1000.0, iotasb, ALU.mult, ALU.add))
                ef('dv', lambda e, tc=tc: e.tensor_reduce(pack[:, 8 + tc:9 + tc], tmp8, AX.X, ALU.min))
                ef('dv', lambda e, tc=tc: e.tensor_single_scalar(pack[:, 8 + tc:9 + tc], pack[:, 8 + tc:9 + tc],
                                                                 1000.0, ALU.add), inc=(dv, 1))
            msce = sc.n
            ef('dv', lambda e: e.tensor_reduce(s01[:, 0:1], e0A, AX.X, ALU.add),
               waits=[(sc, msce)], inc=(dv, 1))
            ef('dv', lambda e: e.tensor_reduce(s01[:, 1:2], e1A, AX.X, ALU.add), inc=(dv, 1))
            mS = dv.n
            ef('pe', lambda e: e.matmul(ps[2][0:1, 0:2], ones_cf, s01, start=True, stop=True),
               waits=[(dv, mS), (dA, dA.n)], inc=(pe, 1))
            ef('sc', lambda e: e.copy(gstage[:, 4:6], ps[2][0:1, 0:2]), waits=[(pe, pe.n)], inc=(sc, 1))
            ef('sy', lambda e, call=call: e.dma_start(out=gin[call], in_=gstage[:, 4:6]),
               waits=[(sc, sc.n)], inc=(dA, 16))
            mgin = dA.n
            ef('gp', lambda e, call=call: e.collective_compute(
                "AllGather", ALU.bypass, replica_groups=[[0, 1], [2, 3], [4, 5], [6, 7]],
                ins=[gin[call].opt()], outs=[gout[call].opt()]),
               waits=[(dA, mgin)], inc=(cc[call], 1))
            ef('sy', lambda e, call=call: e.dma_start(out=gstage[:, 0:2], in_=gout[call][0:1, :]),
               waits=[(cc[call], 1)], inc=(dA, 16))
            ef('sy', lambda e, call=call: e.dma_start(out=gstage[:, 2:4], in_=gout[call][1:2, :]),
               inc=(dA, 16))
            mgz = dA.n
            ef('dv', lambda e: e.tensor_tensor(gstage[:, 6:8], gstage[:, 0:2], gstage[:, 2:4], ALU.add),
               waits=[(dA, mgz)], inc=(dv, 1))
            ef('dv', lambda e: e.reciprocal(gstage[:, 6:8], gstage[:, 6:8]), inc=(dv, 1))
            mrz = dv.n
            ef('pe', lambda e: e.matmul(ps[3][:, 0:2], ones_rf, gstage[:, 6:8], start=True, stop=True),
               waits=[(dv, mrz)], inc=(pe, 1))
            ef('sc', lambda e: e.copy(rzb, ps[3][:, 0:2]), waits=[(pe, pe.n)], inc=(sc, 1))
            ef('dv', lambda e: e.tensor_single_scalar(pack[:, 16:24], e0A, rzb[:, 0:1], ALU.mult),
               waits=[(sc, sc.n)])
            ef('dv', lambda e: e.tensor_single_scalar(pack[:, 24:32], e1A, rzb[:, 1:2], ALU.mult),
               inc=(dv, 1))
            mpk = dv.n
            ef('pe', lambda e: e.transpose(ps[2][0:32, 0:128], pack[:, :], identsb[:, :]),
               waits=[(dv, mpk)], inc=(pe, 1))
            ef('sc', lambda e: e.copy(trsb[:, :], ps[2][0:32, 0:128]), waits=[(pe, pe.n)], inc=(sc, 1))
            ef('sy', lambda e: e.dma_start(out=trd, in_=trsb[:, :]), waits=[(sc, sc.n)], inc=(dA, 16))
            mtr = dA.n
            ef('sy', lambda e: e.dma_start(out=row_wf[0],
                                           in_=trd[16:24, :].rearrange("a b -> (a b)").unsqueeze(0)),
               waits=[(dA, mtr)], inc=(dA, 16))
            ef('sy', lambda e: e.dma_start(out=row_wf[1],
                                           in_=trd[24:32, :].rearrange("a b -> (a b)").unsqueeze(0)),
               inc=(dA, 16))
            mwf = dA.n
            for ts in range(2):
                for wi in range(2):
                    tgt = ps[2 + wi]
                    ef('pe', lambda e, ts=ts, wi=wi, tgt=tgt: e.matmul(
                        tgt[:, :], ones_rf, row_wf[wi][:, ts * 512:(ts + 1) * 512],
                        start=True, stop=True),
                       waits=[(dA, mwf), (sc, sc.n)], inc=(pe, 1))
                mcb = pe.n
                ef('sc', lambda e, ts=ts: e.copy(crowb(0, ts), ps[2][:, :]), waits=[(pe, mcb)], inc=(sc, 1))
                ef('sc', lambda e, ts=ts: e.copy(crowb(1, ts), ps[3][:, :]), inc=(sc, 1))
            ef('gp', lambda e: e.dma_start(out=row_idx[0],
                                           in_=trd[0:8, :].rearrange("a b -> (a b)").unsqueeze(0)),
               waits=[(dA, mtr)], inc=(dO, 16))
            ef('gp', lambda e: e.dma_start(out=row_idx[1],
                                           in_=trd[8:16, :].rearrange("a b -> (a b)").unsqueeze(0)),
               inc=(dO, 16))
            mrows = dO.n
            mmask_prev_dv = 0
            for ts in range(2):
                ef('pe', lambda e, ts=ts: e.matmul(ps[0][:, :], ones_rb,
                                                   row_idx[0][:, ts * 512:(ts + 1) * 512],
                                                   start=True, stop=True),
                   waits=[(dO, mrows), (dv, mmask_prev_dv)], inc=(pe, 1))
                ef('pe', lambda e, ts=ts: e.matmul(ps[1][:, :], ones_rb,
                                                   row_idx[1][:, ts * 512:(ts + 1) * 512],
                                                   start=True, stop=True), inc=(pe, 1))
                mb = pe.n
                ef('dv', lambda e, ts=ts: e.tensor_scalar(Mm(0, ts), ps[0][:, :], ecol, 2.0,
                                                          ALU.is_equal, ALU.mult),
                   waits=[(pe, mb)], inc=(dv, 1))
                ef('dv', lambda e, ts=ts: e.tensor_scalar(Mm(1, ts), ps[1][:, :], ecol, 2.0,
                                                          ALU.is_equal, ALU.mult), inc=(dv, 1))
                mmask_prev_dv = dv.n
            return dv.n, sc.n

        def ffn(call):
            base_pe, base_dv, base_sc = pe.n, dv.n, sc.n
            layer_norm(base_pe, base_dv, base_sc)
            mr_done = [0]
            for w_i, Ad in enumerate((upAT, gateAT)):
                wa_pe = S["lnstats"] if w_i == 0 else pe.n
                ef('gp', lambda e, Ad=Ad: e.dma_start(out=_t3(upA_sb, 128), in_=_tl(Ad)),
                   waits=[(pe, wa_pe)], inc=(dP, 16))
                mA = dP.n
                for ts in range(2):
                    tgt = psQ[2 * w_i + ts]
                    for dk in range(NDK):
                        ef('pe', lambda e, ts=ts, dk=dk, tgt=tgt: e.matmul(
                            tgt[:, :], upA_sb[:, dk * 128:(dk + 1) * 128], ytile(dk, ts),
                            start=(dk == 0), stop=(dk == NDK - 1)),
                           waits=[(dP, mA), (dv, S["y"]), (dv, base_dv)],
                           inc=(pe, 1) if dk == NDK - 1 else None)
            mr_done[0] = pe.n
            mmask_dv, mmask_sc = gating(call)

            for w_i in range(2):
                for ts in range(2):
                    tgt = psQ[2 * w_i + ts]
                    ef('dv', lambda e, ts=ts, w_i=w_i, tgt=tgt: e.tensor_tensor(
                        rho(2 * w_i, ts), tgt[:, :], Mm(0, ts), ALU.mult),
                       waits=[(pe, mr_done[0]), (dv, mmask_dv)], inc=(dv, 1))
                    ef('dv', lambda e, ts=ts, tgt=tgt: e.tensor_tensor(
                        cht[0], tgt[:, :], Mm(1, ts), ALU.mult), inc=(dv, 1))
                    ef('dv', lambda e, ts=ts, w_i=w_i: e.tensor_tensor(
                        rho(2 * w_i + 1, ts), cht[0], rho(2 * w_i, ts), ALU.subtract), inc=(dv, 1))
            mrho = dv.n

            wload = [0] * NHC
            wfree = [0] * NHC

            wload_s = [0] * NHC

            def load_hc(hc):
                wpe = max(wfree[hc - 2] if hc >= 2 else 0, base_pe)
                sem = wA[hc % 2]
                ef('gp', lambda e, hc=hc: e.dma_start(
                    out=_t3(wtA(0, hc % 2), 128), in_=_tl(wupT[:, hc * 128:(hc + 1) * 128])),
                   waits=[(pe, wpe)], inc=(sem, 16))
                ef('gp', lambda e, hc=hc: e.dma_start(
                    out=_t3(wtA(1, hc % 2), 128), in_=_tl(wgateT[:, hc * 128:(hc + 1) * 128])),
                   inc=(sem, 16))
                wload[hc] = sem.n

            def load_small(hc):
                wpe = max(wfree_q[hc - 2] if hc >= 2 else 0, base_pe)
                sem = wS[hc % 2]
                ef('gp', lambda e, hc=hc: e.dma_start(
                    out=btl(0, hc % 2), in_=upBT[:, hc * 128:(hc + 1) * 128]),
                   waits=[(pe, wpe)], inc=(sem, 16))
                ef('gp', lambda e, hc=hc: e.dma_start(
                    out=btl(1, hc % 2), in_=gateBT[:, hc * 128:(hc + 1) * 128]), inc=(sem, 16))
                ef('gp', lambda e, hc=hc: e.dma_start(
                    out=_t3(btl(2, hc % 2), 128), in_=_tl(dnAT[hc * 128:(hc + 1) * 128, :])), inc=(sem, 16))
                wload_s[hc] = sem.n

            load_hc(0)
            load_hc(1)
            load_small(0)
            load_small(1)
            slA_free = [0, 0]
            slB_free = [0, 0]
            cht_free = [0, 0]
            pend_q = []
            wfree_q = [0] * NHC
            silu0_free = [0]
            silu1_free = [0]
            bA_free = [0]
            bB_free = [0]
            mdnb = [0]
            mrho2 = [0]
            bfree = [0] * NDK
            bfree2 = [0] * 32
            psfree_dv = [0]

            for half in range(2):
                for k in range(HALF):
                    hc = half * HALF + k
                    par = hc % 2
                    for ts in range(2):
                        pA, pB, pDu, pDg = ps[0], ps[1], ps[2], ps[3]
                        for dk in range(NDK):
                            ef('pe', lambda e, ts=ts, dk=dk, par=par: e.matmul(
                                pA[:, :], wtA(0, par)[:, dk * 128:(dk + 1) * 128], ytile(dk, ts),
                                start=(dk == 0), stop=False),
                               waits=([(wA[par], wload[hc]), (sc, slA_free[0]), (dv, mmask_dv),
                                       (dv, psfree_dv[0])]
                                      if dk == 0 else []))
                        ef('pe', lambda e, ts=ts, par=par: e.matmul(
                            pA[:, :], btl(0, par), rho(0, ts), start=False, stop=True),
                           waits=[(dv, mrho), (wS[par], wload_s[hc])], inc=(pe, 1))
                        mAup = pe.n
                        for dk in range(NDK):
                            ef('pe', lambda e, ts=ts, dk=dk, par=par: e.matmul(
                                pB[:, :], wtA(1, par)[:, dk * 128:(dk + 1) * 128], ytile(dk, ts),
                                start=(dk == 0), stop=False),
                               waits=([(sc, slA_free[1])] if dk == 0 else []))
                        ef('pe', lambda e, ts=ts, par=par: e.matmul(
                            pB[:, :], btl(1, par), rho(2, ts), start=False, stop=True),
                           inc=(pe, 1))
                        mAg = pe.n
                        ef('pe', lambda e, ts=ts, par=par: e.matmul(
                            pDu[:, :], btl(0, par), rho(1, ts), start=True, stop=True),
                           waits=[(dv, slB_free[0]), (dv, psfree_dv[0]), (sc, mmask_sc)], inc=(pe, 1))
                        mDu = pe.n
                        ef('pe', lambda e, ts=ts, par=par: e.matmul(
                            pDg[:, :], btl(1, par), rho(3, ts), start=True, stop=True),
                           waits=[(dv, slB_free[1]), (dv, psfree_dv[0]), (sc, mmask_sc)], inc=(pe, 1))
                        mDg = pe.n
                        ef('sc', lambda e: e.copy(bA, pA[:, :]),
                           waits=[(pe, mAup), (dv, bA_free[0])], inc=(sc, 1))
                        slA_free[0] = sc.n
                        ef('sc', lambda e: e.copy(bB, pB[:, :]),
                           waits=[(pe, mAg), (dv, bB_free[0])], inc=(sc, 1))
                        slA_free[1] = sc.n
                        it2 = (hc * 2 + ts) % 2
                        ch2 = cht[2] if it2 == 0 else kstg[:, 0:512]
                        ch3 = cht[3] if it2 == 0 else kstg[:, 512:1024]
                        ef('sc', lambda e: e.activation(siluX, bA, AF.Silu),
                           waits=[(dv, silu0_free[0])], inc=(sc, 1))
                        ms0 = sc.n
                        ef('dv', lambda e: e.tensor_tensor(cht[0], siluX, bB, ALU.mult),
                           waits=[(sc, ms0), (pe, cht_free[it2])], inc=(dv, 1))
                        silu0_free[0] = dv.n
                        ef('dv', lambda e: e.tensor_tensor(s1t, bA, pDu[:, :], ALU.add),
                           waits=[(pe, mDu)], inc=(dv, 1))
                        slB_free[0] = dv.n
                        bA_free[0] = dv.n
                        ef('sc', lambda e: e.activation(siluX, s1t, AF.Silu),
                           waits=[(dv, dv.n)], inc=(sc, 1))
                        ms1 = sc.n
                        ef('dv', lambda e: e.tensor_tensor(t2t, bB, pDg[:, :], ALU.add),
                           waits=[(pe, mDg)], inc=(dv, 1))
                        slB_free[1] = dv.n
                        bB_free[0] = dv.n
                        ef('dv', lambda e: e.tensor_tensor(cht[1], siluX, t2t, ALU.mult),
                           waits=[(sc, ms1)], inc=(dv, 1))
                        silu0_free[0] = dv.n
                        silu1_free[0] = dv.n
                        ef('dv', lambda e, ts=ts, ch2=ch2: e.tensor_tensor(ch2, cht[0], crowb(0, ts), ALU.mult))
                        ef('dv', lambda e, ts=ts, ch3=ch3: e.tensor_tensor(ch3, cht[1], crowb(1, ts), ALU.mult))
                        ef('dv', lambda e, k=k, ts=ts, ch2=ch2, ch3=ch3: e.tensor_tensor(
                            mixed(k, ts), ch2, ch3, ALU.add), inc=(dv, 1))
                        mch = dv.n
                        def qmm(e, hc=hc, par=par, ts=ts, ch2=ch2):
                            return e.matmul(psQ[ts][:, :], btl(2, par), ch2,
                                            start=(hc == 0), stop=(hc == NHC - 1))
                        def qmm2(e, hc=hc, par=par, ts=ts, ch3=ch3):
                            return e.matmul(psQ[2 + ts][:, :], btl(2, par), ch3,
                                            start=(hc == 0), stop=(hc == NHC - 1))
                        pend_q.append((qmm, qmm2, mch, hc, it2))
                        if len(pend_q) > 1:
                            f1, f2, mw, phc, pit2 = pend_q.pop(0)
                            ef('pe', f1, waits=[(dv, mw)])
                            ef('pe', f2, inc=(pe, 1))
                            cht_free[pit2] = pe.n
                            wfree_q[phc] = pe.n
                            if ts == 0 and hc + 1 < NHC and hc >= 1:
                                load_small(hc + 1)
                        if ts == 1:
                            wfree[hc] = pe.n
                    if hc + 2 < NHC:
                        load_hc(hc + 2)

                if half == 1:
                    while pend_q:
                        f1, f2, mw, phc, pit2 = pend_q.pop(0)
                        ef('pe', f1, waits=[(dv, mw)])
                        ef('pe', f2, inc=(pe, 1))
                        cht_free[pit2] = pe.n
                        wfree_q[phc] = pe.n
                mmixed = dv.n
                if half == 1:
                    ef('gp', lambda e: e.dma_start(out=dnB_sb, in_=dnBT),
                       waits=[(pe, pe.n)], inc=(dP, 16))
                    mdnb[0] = dP.n
                    for ts in range(2):
                        ef('dv', lambda e, ts=ts: e.tensor_tensor(cht[0], psQ[ts][:, :], Mm(0, ts), ALU.mult),
                           waits=[(pe, max(cht_free[0], cht_free[1]))], inc=(dv, 1))
                        ef('dv', lambda e, ts=ts: e.tensor_tensor(cht[1], psQ[2 + ts][:, :], Mm(1, ts), ALU.mult),
                           inc=(dv, 1))
                        ef('dv', lambda e, ts=ts: e.tensor_tensor(rho(0, ts), cht[0], cht[1], ALU.add),
                           inc=(dv, 1))
                    mrho2[0] = dv.n
                bload2 = [0] * 32
                bfree2_prev = list(bfree2) if half == 1 else [0] * 32

                def load_b(dc, pc, half=half):
                    idx = 2 * dc + pc
                    prev = bfree2[idx - 2] if idx >= 2 else (bfree2_prev[idx] if half == 1 else 0)
                    ef('gp', lambda e, dc=dc, pc=pc, half=half: e.dma_start(
                        out=_t3(wtB((2 * dc + pc) % 2), 128),
                        in_=_tl(wdownT[half * 2816 + pc * 1408:half * 2816 + (pc + 1) * 1408,
                                       dc * 128:(dc + 1) * 128])),
                       waits=[(pe, max(prev, base_pe)), (dv, base_dv)], inc=(wB[idx % 2], 16))
                    bload2[idx] = wB[idx % 2].n

                load_b(0, 0)
                load_b(0, 1)
                for dc in range(NDK):
                    pf = [ps[(dc % 2) * 2], ps[(dc % 2) * 2 + 1]]
                    mpc = [0, 0]
                    for k in range(HALF):
                        pc = k // 11
                        kk = k % 11
                        buf = (2 * dc + pc) % 2
                        for ts in range(2):
                            isl = (half == 0 and k == HALF - 1 and ts == 1)
                            w0 = []
                            if kk == 0 and ts == 0:
                                w0 = [(wB[(2 * dc + pc) % 2], bload2[2 * dc + pc]), (dv, mmixed),
                                      (dv, bfree[dc - 2] if dc >= 2 else 0),
                                      (sc, slA_free[0]), (sc, slA_free[1]),
                                      (dv, slB_free[0]), (dv, slB_free[1]),
                                      (dv, silu1_free[0])]
                            ef('pe', lambda e, ts=ts, k=k, kk=kk, buf=buf, pX=pf[ts],
                               stp=(half == 0 and k == HALF - 1): e.matmul(
                                pX[:, :], wtB(buf)[:, kk * 128:(kk + 1) * 128], mixed(k, ts),
                                start=(k == 0), stop=stp),
                               waits=w0,
                               inc=(pe, 1) if (kk == 10 and ts == 1) or isl else None)
                        if kk == 10 and ts == 1:
                            mpc[pc] = pe.n
                        if k == 10:
                            bfree2[2 * dc] = pe.n
                            ni = 2 * dc + 2
                            if ni < 32:
                                load_b(ni // 2, ni % 2)
                        if k == HALF - 1:
                            bfree2[2 * dc + 1] = pe.n
                            ni = 2 * dc + 3
                            if ni < 32:
                                load_b(ni // 2, ni % 2)
                    for ts in range(2):
                        if half == 1:
                            ef('pe', lambda e, dc=dc, ts=ts, pX=pf[ts]: e.matmul(
                                pX[:, :], dnB_sb[:, dc * 128:(dc + 1) * 128], rho(0, ts),
                                start=False, stop=True),
                               waits=[(dP, mdnb[0]), (dv, mrho2[0])], inc=(pe, 1))
                        mpb = pe.n
                        ef('dv', lambda e, dc=dc, ts=ts, pX=pf[ts]: e.scalar_tensor_tensor(
                            rtile(dc, ts), pX[:, :], 0.5, rtile(dc, ts), ALU.mult, ALU.add),
                           waits=[(pe, mpb)], inc=(dv, 1))
                    bfree[dc] = dv.n
                psfree_dv[0] = dv.n

        def attention():
            base_pe, base_dv, base_sc = pe.n, dv.n, sc.n
            layer_norm(base_pe, base_dv, base_sc)
            my = S["y"]
            ef('gp', lambda e: e.dma_start(out=row_bv, in_=bvd),
               waits=[(dv, my)], inc=(dO, 16))
            mbv = dO.n
            for h4 in range(4):
                ef('pe', lambda e, h4=h4: e.matmul(ps[h4][:, :], ones_rb,
                                                   row_bv[:, h4 * 512:(h4 + 1) * 512],
                                                   start=True, stop=True),
                   waits=[(dO, mbv), (dv, my)], inc=(pe, 1))
            mbc2 = pe.n
            for h4 in range(4):
                ef('sc', lambda e, h4=h4: e.copy(bv_b[:, h4 * 512:(h4 + 1) * 512], ps[h4][:, :]),
                   waits=[(pe, mbc2)], inc=(sc, 1))
            mbve = sc.n
            pload = [0] * (2 * NDK)
            pfree = [0] * (2 * NDK)
            pfree_sc = [0] * (2 * NDK)
            kdma = {0: 0, 1: 0}
            kview = cin.rearrange("a (b c) -> (a b) c", b=2)

            def load_p(i):
                wpe = max(pfree[i - 2] if i >= 2 else 0, base_pe)
                Wd = wq_d if i < NDK else wk_d
                oc = i % NDK
                ef('gp', lambda e, Wd=Wd, oc=oc, i=i: e.dma_start(
                    out=_t3(wtP(i % 2), 128), in_=_tl(Wd[:, oc * 128:(oc + 1) * 128])),
                   waits=[(pe, wpe)], inc=(wP[i % 2], 16))
                pload[i] = wP[i % 2].n

            load_p(0)
            load_p(1)
            for i in range(2 * NDK):
                oc = i % NDK
                isq = i < NDK
                for ts in range(2):
                    pX = ps[(i % 2) * 2 + ts]
                    for dk in range(NDK):
                        ef('pe', lambda e, i=i, ts=ts, dk=dk, pX=pX: e.matmul(
                            pX[:, :], wtP(i % 2)[:, dk * 128:(dk + 1) * 128], ytile(dk, ts),
                            start=(dk == 0), stop=(dk == NDK - 1)),
                           waits=([(wP[i % 2], pload[i]), (dv, my), (sc, mbve),
                                   (dv, pfree_sc[i - 2] if i >= 2 else 0)] if dk == 0 else []),
                           inc=(pe, 1) if dk == NDK - 1 else None)
                    mproj = pe.n
                    bias = bq_s[:, oc:oc + 1] if isq else bk_s[:, oc:oc + 1]
                    scl = SM if isq else 1.0
                    ef('dv', lambda e, ts=ts, pX=pX, bias=bias, scl=scl: e.tensor_scalar(
                        kstg[:, ts * 512:(ts + 1) * 512], pX[:, :], bias, scl,
                        ALU.add, ALU.mult),
                       waits=[(pe, mproj), (dK[ts], kdma[ts])], inc=(dv, 1))
                    dst = (qdram[oc * 128:(oc + 1) * 128, ts * 512:(ts + 1) * 512] if isq
                           else kview[oc * 128:(oc + 1) * 128, ts * 512:(ts + 1) * 512])
                    ef('sy', lambda e, dst=dst, ts=ts: e.dma_start(
                        out=dst, in_=kstg[:, ts * 512:(ts + 1) * 512]),
                       waits=[(dv, dv.n)], inc=(dK[ts], 16))
                    kdma[ts] = dK[ts].n
                pfree[i] = pe.n
                pfree_sc[i] = dv.n
                if i + 2 < 2 * NDK:
                    load_p(i + 2)
            vsem = [wV[0], wV[1], wS[0], wS[1]]
            vw_free = [base_pe] * 4
            vload = {}
            vseq = [(tcb, dvb, dk) for tcb in range(2) for dvb in range(4) for dk in range(NDK)]

            def loadv(i):
                tcb, dvb, dk = vseq[i]
                buf = i % 4
                ef('gp', lambda e, dvb=dvb, dk=dk, buf=buf: e.dma_start(
                    out=wvt(buf), in_=wv_d[dk * 128:(dk + 1) * 128, dvb * 512:(dvb + 1) * 512]),
                   waits=[(pe, vw_free[buf]), (dv, base_dv)], inc=(vsem[buf], 16))
                vload[i] = vsem[buf].n

            for i in range(4):
                loadv(i)
            vev_free = [base_dv] * 4
            for tcb in range(2):
                for dvb in range(4):
                    for dk in range(NDK):
                        i = (tcb * 4 + dvb) * NDK + dk
                        buf = i % 4
                        for j4 in range(4):
                            tc = tcb * 4 + j4
                            w0 = [(vsem[buf], vload[i])] if j4 == 0 else []
                            if dk == 0:
                                w0 = w0 + [(dv, vev_free[j4]), (dv, my)]
                            ef('pe', lambda e, tc=tc, dk=dk, j4=j4, buf=buf: e.matmul(
                                psQ[j4][:, :], ytile(dk)[:, tc * 128:(tc + 1) * 128], wvt(buf),
                                start=(dk == 0), stop=(dk == NDK - 1)),
                               waits=w0,
                               inc=(pe, 1) if j4 == 3 else None)
                        vw_free[buf] = pe.n
                        if i + 4 < len(vseq):
                            loadv(i + 4)
                    mvp = pe.n
                    for j4 in range(4):
                        tc = tcb * 4 + j4
                        sslot = j4 % 2
                        ef('dv', lambda e, j4=j4, dvb=dvb, sslot=sslot: e.tensor_tensor(
                            kstg[:, sslot * 512:sslot * 512 + 512], psQ[j4][:, :],
                            bv_b[:, dvb * 512:(dvb + 1) * 512], ALU.add),
                           waits=[(pe, mvp), (sc, mbve), (dK[sslot], kdma[sslot])], inc=(dv, 1))
                        vev_free[j4] = dv.n
                        ef('sy', lambda e, tc=tc, dvb=dvb, sslot=sslot: e.dma_start(
                            out=cin[1024 + tc * 128:1024 + (tc + 1) * 128, dvb * 512:(dvb + 1) * 512],
                            in_=kstg[:, sslot * 512:sslot * 512 + 512]),
                           waits=[(dv, dv.n)], inc=(dK[sslot], 16))
                        kdma[sslot] = dK[sslot].n
            mcin = None
            ef('gp', lambda e: e.collective_compute(
                "AllGather", ALU.bypass, replica_groups=[[0, 1], [2, 3], [4, 5], [6, 7]],
                ins=[cin.opt()], outs=[cout.opt()]),
               waits=[(dK[0], dK[0].n), (dK[1], dK[1].n)], inc=(cc[2], 1))
            ef('sy', lambda e: e.dma_start(out=amask_sb, in_=amask_d),
               waits=[(dv, mrho_dead_dv[0])], inc=(dA, 16))
            mam = dA.n
            kvout = cout.rearrange("a (b c) -> (a b) c", b=2)
            hload = [0] * NH
            hvl = [0, 0]
            hfree_q = [base_dv, base_dv]
            hfree_k = [base_pe, base_pe]
            hfree_v = [base_pe, base_pe]
            sco_free = [base_sc, base_sc]
            pb_free = [base_pe, base_pe]
            ctx_free = [base_dv]
            l_free = [base_sc]

            def load_head(h):
                buf = h % 2
                sem = dH[buf]
                ef('sy', lambda e, h=h, buf=buf: e.dma_start(
                    out=q_t[buf], in_=qdram[h * 128:(h + 1) * 128, :]),
                   waits=[(dv, hfree_q[buf])], inc=(sem, 16))
                for j in range(2):
                    src = kvout[4096 * j + h * 128:4096 * j + (h + 1) * 128, :]
                    dst = k_t[buf].rearrange("p (g c) -> p g c", c=128)[:, j::2, :]
                    ef('sy', lambda e, src=src, dst=dst: e.dma_start(
                        out=dst, in_=src.rearrange("p (g c) -> p g c", c=128)),
                       waits=[(cc[2], 1), (pe, hfree_k[buf])], inc=(sem, 16))
                vbuf = buf
                for j in range(2):
                    src = cout[2048 * j + 1024:2048 * j + 2048, h * 128:(h + 1) * 128]
                    dst = v_t[vbuf].rearrange("p (g w) -> p g w", w=128)[:, j::2, :]
                    ef('sy', lambda e, src=src, dst=dst: e.dma_start(
                        out=dst, in_=src.rearrange("(g p) w -> p g w", p=128)),
                       waits=[(cc[2], 1), (pe, hfree_v[vbuf])], inc=(sem, 16))
                hvl[vbuf] = sem.n
                hload[h] = sem.n

            load_head(0)
            load_head(1)
            for h in range(NH):
                buf = h % 2
                vbuf = h % 2
                hp = h % 2
                mpv = 0
                for kt in range(16):
                    s0 = 128 * (kt // 2)
                    pr = [ps[0], ps[1]] if kt % 2 == 0 else [ps[2], ps[3]]
                    segs = ([(0, s0, 512)] if s0 < 512 else []) + [(1, max(512, s0), 1024)]
                    for n_, (si, a, b_) in enumerate(segs):
                        ef('pe', lambda e, kt=kt, buf=buf, pr=pr, si=si, a=a, b_=b_: e.matmul(
                            pr[si][:, a - si * 512:b_ - si * 512],
                            k_t[buf][:, kt * 128:(kt + 1) * 128], q_t[buf][:, a:b_],
                            start=True, stop=True),
                           waits=([(dH[buf], hload[h]), (sc, sco_free[kt % 2]),
                                   (sc, l_free[0])] if n_ == 0 else []),
                           inc=(pe, 1) if n_ == len(segs) - 1 else None)
                    msco = pe.n
                    for (si, a, b_) in segs:
                        ef('sc', lambda e, kt=kt, si=si, a=a, b_=b_, pr=pr: e.activation(
                            probsT(kt % 2)[:, a:b_], pr[si][:, a - si * 512:b_ - si * 512], AF.Exp),
                           waits=[(pe, max(msco, pb_free[kt % 2])), (dA, mam)], inc=(sc, 1))
                    msce2 = sc.n
                    sco_free[kt % 2] = msce2
                    ef('dv', lambda e, kt=kt, s0=s0: e.tensor_tensor(
                        probsT(kt % 2)[:, s0:s0 + 128], probsT(kt % 2)[:, s0:s0 + 128],
                        amask_sb[:, kt * 128:(kt + 1) * 128], ALU.mult),
                       waits=[(sc, msce2)], inc=(dv, 1))
                    mmk = dv.n
                    for n_, (si, a, b_) in enumerate(segs):
                        stp = (kt == (15 if si == 1 else 7))
                        ef('pe', lambda e, h=h, kt=kt, si=si, a=a, b_=b_, vbuf=vbuf, hp=hp, stp=stp: e.matmul(
                            psQ[si][:, a - si * 512:b_ - si * 512],
                            v_t[vbuf][:, kt * 128:(kt + 1) * 128],
                            probsT(kt % 2)[:, a:b_],
                            start=(kt == 0), stop=stp),
                           waits=([(dH[vbuf], hvl[vbuf]), (dv, mmk), (dv, ctx_free[0])] if n_ == 0 else []))
                        ef('pe', lambda e, kt=kt, si=si, a=a, b_=b_, stp=stp: e.matmul(
                            psQ[2 + si][0:1, a - si * 512:b_ - si * 512], ones_cb,
                            probsT(kt % 2)[:, a:b_],
                            start=(kt == 0), stop=stp),
                           inc=(pe, 1) if n_ == len(segs) - 1 else None)
                    pb_free[kt % 2] = pe.n
                    mpv = pe.n
                ef('dv', lambda e: e.reciprocal(linv_f[:, 0:512], psQ[2][0:1, :]),
                   waits=[(pe, mpv)], inc=(dv, 1))
                ef('dv', lambda e: e.reciprocal(linv_f[:, 512:1024], psQ[3][0:1, :]), inc=(dv, 1))
                ml = dv.n
                for si in range(2):
                    ef('pe', lambda e, si=si: e.matmul(psQ[2 + si][:, :], ones_rf,
                                                       linv_f[:, si * 512:(si + 1) * 512],
                                                       start=True, stop=True),
                       waits=[(dv, ml)], inc=(pe, 1))
                mlb = pe.n
                for si in range(2):
                    ef('sc', lambda e, si=si: e.copy(linv_b[:, si * 512:(si + 1) * 512], psQ[2 + si][:, :]),
                       waits=[(pe, mlb)], inc=(sc, 1))
                l_free[0] = sc.n
                for si in range(2):
                    ef('dv', lambda e, h=h, si=si: e.tensor_tensor(
                        ytile(h)[:, si * 512:(si + 1) * 512], psQ[si][:, :],
                        linv_b[:, si * 512:(si + 1) * 512], ALU.mult),
                       waits=[(sc, l_free[0])], inc=(dv, 1))
                ctx_free[0] = dv.n
                hfree_q[buf] = ctx_free[0]
                hfree_k[buf] = mpv
                hfree_v[vbuf] = mpv
                if h + 2 < NH:
                    load_head(h + 2)
            mattn = dv.n
            oload = [0] * NDK
            ofree = [0] * NDK

            def load_o(dc):
                ef('gp', lambda e, dc=dc: e.dma_start(
                    out=_t3(wtP(dc % 2), 128), in_=_tl(wo_d[:, dc * 128:(dc + 1) * 128])),
                   waits=[(pe, pfree[2 * NDK - 1]),
                          (dv, ofree[dc - 2] if dc >= 2 else 0)], inc=(wP[dc % 2], 16))
                oload[dc] = wP[dc % 2].n

            load_o(0)
            load_o(1)
            for dc in range(NDK):
                for ts in range(2):
                    pX = ps[(dc % 2) * 2 + ts]
                    for hh in range(NH):
                        ef('pe', lambda e, dc=dc, ts=ts, hh=hh, pX=pX: e.matmul(
                            pX[:, :], wtP(dc % 2)[:, hh * 128:(hh + 1) * 128], ytile(hh, ts),
                            start=(hh == 0), stop=(hh == NH - 1)),
                           waits=([(wP[dc % 2], oload[dc]), (dv, mattn), (sc, l_free[0]),
                                   (dv, ofree[dc - 2] if dc >= 2 else 0)] if hh == 0 else []),
                           inc=(pe, 1) if hh == NH - 1 else None)
                    mo = pe.n
                    ef('dv', lambda e, dc=dc, ts=ts, pX=pX: e.tensor_tensor(
                        rtile(dc, ts), pX[:, :], rtile(dc, ts), ALU.add),
                       waits=[(pe, mo)], inc=(dv, 1))
                ofree[dc] = dv.n
                if dc + 2 < NDK:
                    load_o(dc + 2)

        mrho_dead_dv = [0]
        ffn(0)
        mrho_dead_dv[0] = dv.n
        if STAGE >= 2:
            attention()
        if STAGE >= 3:
            ffn(1)

        ef('sy', lambda e: e.dma_start(out=_tl(out), in_=_t3(resid[:, :], T)),
           waits=[(dv, dv.n), (pe, pe.n), (sc, sc.n)], inc=(dOUT, 16))
        ef('sy', lambda e: e.nop(), waits=[(dOUT, dOUT.n)])

        @block.sync
        def _(e):
            P.emit('sy', e)

        @block.gpsimd
        def _(e):
            P.emit('gp', e)

        @block.tensor
        def _(e):
            P.emit('pe', e)

        @block.scalar
        def _(e):
            P.emit('sc', e)

        @block.vector
        def _(e):
            P.emit('dv', e)

    return nc


_NC_CACHE = {}


def _amask(j):
    m = np.zeros((128, 16 * 128), dtype=np.float32)
    for kt in range(16):
        m0 = kt // 2
        qt = 2 * m0 + j
        gk = kt * 128 + np.arange(128)[:, None]
        gq = qt * 128 + np.arange(128)[None, :]
        m[:, kt * 128:(kt + 1) * 128] = (gq >= gk).astype(np.float32)
    return m


def kernel(L0_attn_hidden_states, L0_attn_Wq, L0_attn_bq, L0_attn_Wk, L0_attn_bk,
           L0_attn_Wv, L0_attn_bv, L0_attn_Wo, L0_ffn_gate_w, L0_ffn_w_up,
           L0_ffn_w_gate, L0_ffn_w_down, L0_ffn_up_A, L0_ffn_up_B, L0_ffn_gate_A,
           L0_ffn_gate_B, L0_ffn_down_A, L0_ffn_down_B):
    x = np.ascontiguousarray(np.asarray(L0_attn_hidden_states, dtype=np.float32))
    B = x.shape[0]
    cfg = lambda a: np.ascontiguousarray(np.asarray(a, dtype=np.float32))

    gate_w = cfg(L0_ffn_gate_w)
    gwt = np.zeros((128, 128), dtype=np.float32)
    for dk in range(NDK):
        gwt[:, dk * 8:(dk + 1) * 8] = gate_w[:, dk * 128:(dk + 1) * 128].T
    common = {
        "wq": cfg(L0_attn_Wq), "wk": cfg(L0_attn_Wk), "wv": cfg(L0_attn_Wv),
        "wo": cfg(L0_attn_Wo),
        "bqd": cfg(np.asarray(L0_attn_bq, dtype=np.float32).reshape(NDK, 128).T),
        "bkd": cfg(np.asarray(L0_attn_bk, dtype=np.float32).reshape(NDK, 128).T),
        "bvd": cfg(np.asarray(L0_attn_bv, dtype=np.float32).reshape(1, D)),
        "wupT": cfg(np.asarray(L0_ffn_w_up).T),
        "wgateT": cfg(np.asarray(L0_ffn_w_gate).T),
        "wdownT": cfg(np.asarray(L0_ffn_w_down).T),
        "upAT": cfg(np.asarray(L0_ffn_up_A).reshape(128, D).T),
        "gateAT": cfg(np.asarray(L0_ffn_gate_A).reshape(128, D).T),
        "upBT": cfg(np.asarray(L0_ffn_up_B).transpose(0, 2, 1).reshape(128, H)),
        "gateBT": cfg(np.asarray(L0_ffn_gate_B).transpose(0, 2, 1).reshape(128, H)),
        "dnAT": cfg(np.asarray(L0_ffn_down_A).reshape(128, H).T),
        "dnBT": cfg(np.asarray(L0_ffn_down_B).transpose(0, 2, 1).reshape(128, D)),
        "gwt": gwt,
        "identf": np.eye(128, dtype=np.float32),
        "ones_r": np.ones((1, 128), dtype=np.float32),
        "ones_c": np.ones((128, 1), dtype=np.float32),
        "ecol_d": (np.arange(128, dtype=np.float32) // 16).reshape(128, 1),
        "iota8_d": np.broadcast_to(np.arange(8, dtype=np.float32), (128, 8)).copy(),
    }
    amasks = [_amask(0), _amask(1)]

    if "nc" not in _NC_CACHE:
        _NC_CACHE["nc"] = build_nc()
    nc = _NC_CACHE["nc"]

    in_maps = []
    for c in range(8):
        b, j = c // 2, c % 2
        xt = x[b].reshape(16, 128, D)[j::2].reshape(T, D).T
        m = dict(common)
        m["xs"] = np.ascontiguousarray(xt)
        m["amask_d"] = amasks[j].astype(ml_dtypes.bfloat16)
        in_maps.append(m)

    res = run_bass_kernel_spmd(nc, in_maps, core_ids=list(range(8)))

    y = np.empty_like(x)
    for c in range(8):
        b, j = c // 2, c % 2
        oc = np.asarray(res.results[c]["out"])
        yt = oc.T.reshape(NTC, 128, D)
        for i in range(NTC):
            y[b, (2 * i + j) * 128:(2 * i + j + 1) * 128, :] = yt[i]
    return y


# version 09
# speedup vs baseline: 1.4728x; 1.0759x over previous
import math
import numpy as np
import ml_dtypes
import concourse.bass as bass
import concourse.mybir as mybir
from concourse.bass_utils import run_bass_kernel_spmd
from contextlib import ExitStack

F32 = mybir.dt.float32
F32R = mybir.dt.float32r
BF16 = mybir.dt.bfloat16
AF = mybir.ActivationFunctionType
ALU = mybir.AluOpType
AX = mybir.AxisListType

D, H, NH, DH, T, E, R = 2048, 5632, 16, 128, 1024, 8, 16
NDK = D // 128
NHC = H // 128
HALF = NHC // 2
NTC = T // 128
TS = 512
SM = 1.0 / math.sqrt(DH)
STAGE = 3


class _Sem:
    __slots__ = ("name", "n", "h")
    def __init__(self, name):
        self.name, self.n, self.h = name, 0, None


class _B:
    def __init__(self):
        self.ops = []
        self.sems = []
        self._lw = {}
        self.auto_chain = {}

    def sem(self, name):
        s = _Sem(name)
        self.sems.append(s)
        return s

    def op(self, eng, fn, waits=(), inc=None):
        waits = list(waits)
        if eng in self.auto_chain:
            s = self.auto_chain[eng]
            waits.append((s, s.n))
            if inc is None:
                inc = (s, 1)
        w = []
        for (s, v) in waits:
            if v <= 0:
                continue
            k = (eng, s.name)
            if self._lw.get(k, -1) >= v:
                continue
            self._lw[k] = v
            w.append((s, v))
        if inc is not None:
            inc[0].n += inc[1]
        self.ops.append((eng, fn, w, inc))
        return inc[0].n if inc else None

    def emit(self, name, eng):
        for (e, fn, w, inc) in self.ops:
            if e != name:
                continue
            for (s, v) in w:
                eng.wait_ge(s.h, v)
            inst = fn(eng)
            if inc is not None:
                inst.then_inc(inc[0].h, inc[1])


def _tl(src2d):
    return src2d.rearrange("(a p) c -> p a c", p=128)


def _t3(dst2d, c):
    return dst2d.rearrange("p (a c) -> p a c", c=c)


def build_nc():
    nc = bass.Bass("TRN2", target_bir_lowering=False)

    xs = nc.dram_tensor("xs", [D, T], F32, kind="ExternalInput").ap()
    wq_d = nc.dram_tensor("wq", [D, D], F32, kind="ExternalInput").ap()
    wk_d = nc.dram_tensor("wk", [D, D], F32, kind="ExternalInput").ap()
    wv_d = nc.dram_tensor("wv", [D, D], F32, kind="ExternalInput").ap()
    wo_d = nc.dram_tensor("wo", [D, D], F32, kind="ExternalInput").ap()
    bqd = nc.dram_tensor("bqd", [128, NDK], F32, kind="ExternalInput").ap()
    bkd = nc.dram_tensor("bkd", [128, NDK], F32, kind="ExternalInput").ap()
    bvd = nc.dram_tensor("bvd", [1, D], F32, kind="ExternalInput").ap()
    wupT = nc.dram_tensor("wupT", [D, H], F32, kind="ExternalInput").ap()
    wgateT = nc.dram_tensor("wgateT", [D, H], F32, kind="ExternalInput").ap()
    wdownT = nc.dram_tensor("wdownT", [H, D], F32, kind="ExternalInput").ap()
    upAT = nc.dram_tensor("upAT", [D, 128], F32, kind="ExternalInput").ap()
    gateAT = nc.dram_tensor("gateAT", [D, 128], F32, kind="ExternalInput").ap()
    upBT = nc.dram_tensor("upBT", [128, H], F32, kind="ExternalInput").ap()
    gateBT = nc.dram_tensor("gateBT", [128, H], F32, kind="ExternalInput").ap()
    dnAT = nc.dram_tensor("dnAT", [H, 128], F32, kind="ExternalInput").ap()
    dnBT = nc.dram_tensor("dnBT", [128, D], F32, kind="ExternalInput").ap()
    gwt = nc.dram_tensor("gwt", [128, 128], F32, kind="ExternalInput").ap()
    identf = nc.dram_tensor("identf", [128, 128], F32, kind="ExternalInput").ap()
    ones_r = nc.dram_tensor("ones_r", [1, 128], F32, kind="ExternalInput").ap()
    ones_c = nc.dram_tensor("ones_c", [128, 1], F32, kind="ExternalInput").ap()
    ecol_d = nc.dram_tensor("ecol_d", [128, 1], F32, kind="ExternalInput").ap()
    iota8_d = nc.dram_tensor("iota8_d", [128, 8], F32, kind="ExternalInput").ap()
    amask_d = nc.dram_tensor("amask_d", [128, 16 * 128], BF16, kind="ExternalInput").ap()
    out = nc.dram_tensor("out", [D, T], F32, kind="ExternalOutput").ap()

    qdram = nc.dram_tensor("qdram", [D, T], BF16).ap()
    cin = nc.dram_tensor("cin", [2048, 2048], BF16).ap()
    cout = nc.dram_tensor("cout", [4096, 2048], BF16).ap()
    trd = nc.dram_tensor("trd", [32, 128], F32).ap()
    gin = [nc.dram_tensor(f"gin{i}", [1, 2], F32).ap() for i in range(2)]
    gout = [nc.dram_tensor(f"gout{i}", [2, 2], F32).ap() for i in range(2)]

    P = _B()
    dA = P.sem("dA")
    dO = P.sem("dO")
    wA = [P.sem("wA0"), P.sem("wA1")]
    wB = [P.sem("wB0"), P.sem("wB1")]
    wP = [P.sem("wP0"), P.sem("wP1")]
    wV = [P.sem("wV0"), P.sem("wV1")]
    dK = [P.sem("dK0"), P.sem("dK1")]
    dH = [P.sem("dH0"), P.sem("dH1")]
    dOUT = P.sem("dOUT")
    dP = P.sem("dP")
    wS = [P.sem("wS0"), P.sem("wS1")]
    cc = [P.sem(f"cc{i}") for i in range(3)]
    pe, sc, dv = P.sem("pe"), P.sem("sc"), P.sem("dv")
    P.auto_chain = {"dv": dv, "sc": sc}

    with ExitStack() as stk:
        en = stk.enter_context
        resid = en(nc.sbuf_tensor("resid", [128, NDK * 1024], F32))
        act = en(nc.sbuf_tensor("act", [128, NDK * 1024], BF16))
        arena = en(nc.sbuf_tensor("arena", [128, HALF * 1024], BF16))
        wa = en(nc.sbuf_tensor("wa", [128, 4 * 2048], BF16))
        wb = en(nc.sbuf_tensor("wb", [128, 2 * 1408], BF16))
        wsm = en(nc.sbuf_tensor("wsm", [128, 6 * 128], BF16))
        lnc = en(nc.sbuf_tensor("lnc", [128, 2048], BF16))
        sf = en(nc.sbuf_tensor("sf", [128, 2560], F32))
        sb = en(nc.sbuf_tensor("sb", [128, 10240], BF16))
        rowz = en(nc.sbuf_tensor("rowz", [1, 4], F32))
        smalls = en(nc.sbuf_tensor("smalls", [128, 64], F32))
        pack = en(nc.sbuf_tensor("pack", [128, 32], F32))
        trsb = en(nc.sbuf_tensor("trsb", [32, 128], F32))
        consts = en(nc.sbuf_tensor("consts", [128, 48], F32))
        cbf = en(nc.sbuf_tensor("cbf", [128, 16], BF16))
        crow = en(nc.sbuf_tensor("crow", [1, 192], F32))
        gstage = en(nc.sbuf_tensor("gstage", [1, 8], F32))
        kstg = en(nc.sbuf_tensor("kstg", [128, 2 * 512], BF16))
        gwtile = en(nc.sbuf_tensor("gwtile", [128, 128], BF16))
        identsb = en(nc.sbuf_tensor("identsb", [128, 128], F32))
        ps = [en(nc.psum_tensor(f"ps{i}", [128, 512], F32)) for i in range(4)]
        psQ = [en(nc.psum_tensor(f"psQ{i}", [128, 512], F32)) for i in range(4)]
        for s in P.sems:
            s.h = en(nc.semaphore(s.name))
        block = en(nc.Block())

        def rtile(dc, ts=None):
            a = resid[:, dc * 1024:(dc + 1) * 1024]
            return a if ts is None else a[:, ts * TS:(ts + 1) * TS]

        def ytile(dc, ts=None):
            a = act[:, dc * 1024:(dc + 1) * 1024]
            return a if ts is None else a[:, ts * TS:(ts + 1) * TS]

        ecol = consts[:, 0:1]
        iotasb = consts[:, 8:16]
        bq_s = consts[:, 16:32]
        bk_s = consts[:, 32:48]
        ones_cb = cbf[:, 0:1]
        ones_cf = consts[:, 1:2]
        ones_rf = crow[:, 0:128]
        ones_rb = crow[:, 128:192].bitcast(BF16)
        m0A, m1A = smalls[:, 0:8], smalls[:, 8:16]
        e0A, e1A = smalls[:, 16:24], smalls[:, 24:32]
        eqs, tmp8, lmt = smalls[:, 32:40], smalls[:, 40:48], smalls[:, 48:56]
        s01 = smalls[:, 56:58]
        s01r = smalls[:, 58:60].bitcast(F32R)
        rzb = smalls[:, 60:62]

        row_mean = sb[0:1, 0:2048].bitcast(F32)
        row_var = sb[0:1, 2048:4096].bitcast(F32)
        rstd_hi = sb[0:1, 4096:5120]
        rstd_lo = sb[0:1, 5120:6144]
        bneg_hi = sb[0:1, 6144:7168]
        bneg_lo = sb[0:1, 7168:8192]
        w_hi = [sb[0:1, 0:1024], sb[0:1, 2048:3072]]
        w_lo = [sb[0:1, 1024:2048], sb[0:1, 3072:4096]]
        row_wf = [sb[0:1, 4096:6144].bitcast(F32), sb[0:1, 6144:8192].bitcast(F32)]
        row_idx = [sb[0:1, 8192:9216], sb[0:1, 9216:10240]]
        linv_f = sb[0:1, 4096:6144].bitcast(F32)
        linv_hi = sb[0:1, 6144:7168]
        linv_lo = sb[0:1, 7168:8192]
        row_bv = sb[0:1, 8192:10240]

        rstd_b = sf[:, 0:1024]
        bneg_b = sf[:, 1024:2048]
        bA = sf[:, 0:512]
        bB = sf[:, 512:1024]
        s1t = sf[:, 1024:1536]
        t2t = sf[:, 1536:2048]
        siluX = sf[:, 2048:2560]
        bv_b = sf[:, 0:2048]
        linv_b = sf[:, 0:1024]

        def rho(i, ts):
            o = (i * 2 + ts) * 512
            return sb[:, o:o + 512]

        def crowb(i, ts):
            o = 4096 + (i * 2 + ts) * 512
            return sb[:, o:o + 512]

        def Mm(i, ts):
            o = 6144 + (i * 2 + ts) * 512
            return sb[:, o:o + 512]

        def probsT(buf):
            return sb[:, buf * 1024:(buf + 1) * 1024]
        amask_sb = sb[:, 2048:4096]
        cht = [sb[:, 8192 + i * 512:8192 + (i + 1) * 512] for i in range(4)]

        def mixed(k, ts):
            o = k * 1024 + ts * 512
            return arena[:, o:o + 512]
        q_t = [arena[:, i * 1024:(i + 1) * 1024] for i in range(2)]
        k_t = [arena[:, 2048 + i * 2048:2048 + (i + 1) * 2048] for i in range(2)]
        v_t = [arena[:, 6144 + i * 2048:6144 + (i + 1) * 2048] for i in range(2)]

        def wtA(slot, buf):
            o = (slot * 2 + buf) * 2048
            return wa[:, o:o + 2048]

        def wtP(buf):
            return wa[:, buf * 2048:buf * 2048 + 2048]

        def wtB(buf):
            return wb[:, buf * 1408:buf * 1408 + 1408]

        def wvt(buf):
            return wb[:, buf * 640:buf * 640 + 512]

        def btl(slot, buf):
            o = (slot * 2 + buf) * 128
            return wsm[:, o:o + 128]

        upA_sb = lnc[:, 0:2048]
        dnB_sb = lnc[:, 0:2048]
        xsq = [lnc[:, 0:1024]] * 2
        xbf = [lnc[:, 1024:2048]] * 2

        ef = P.op
        S = {"y": 0}

        ef('sy', lambda e: e.dma_start(out=_t3(resid[:, :], T), in_=_tl(xs)), inc=(dA, 16))
        mx = dA.n
        ef('sy', lambda e: e.dma_start(out=identsb[:, :], in_=identf), inc=(dA, 16))
        ef('sy', lambda e: e.dma_start(out=consts[:, 0:1], in_=ecol_d), inc=(dA, 16))
        ef('sy', lambda e: e.dma_start(out=consts[:, 8:16], in_=iota8_d), inc=(dA, 16))
        ef('sy', lambda e: e.dma_start(out=consts[:, 16:32], in_=bqd), inc=(dA, 16))
        ef('sy', lambda e: e.dma_start(out=consts[:, 32:48], in_=bkd), inc=(dA, 16))
        mca = dA.n
        mx = dA.n
        ef('gp', lambda e: e.dma_start(out=ones_cb, in_=ones_c), inc=(dO, 16))
        ef('gp', lambda e: e.dma_start(out=ones_rb, in_=ones_r), inc=(dO, 16))
        ef('sy', lambda e: e.dma_start(out=ones_rf, in_=ones_r), inc=(dA, 16))
        ef('sy', lambda e: e.dma_start(out=ones_cf, in_=ones_c), inc=(dA, 16))
        ef('gp', lambda e: e.dma_start(out=gwtile[:, :], in_=gwt), inc=(dO, 16))
        mcg = dO.n

        def layer_norm(base_pe, base_dv, base_sc):
            mson = [0] * NDK
            for dc in range(NDK):
                b = dc % 2
                wpe = max(mson[dc - 1] if dc >= 1 else 0, base_pe)
                ef('dv', lambda e, dc=dc, b=b: e.tensor_tensor(xsq[b], rtile(dc), rtile(dc), ALU.mult),
                   waits=[(dA, dA.n), (pe, wpe), (dv, base_dv)], inc=(dv, 1))
                mdv = dv.n
                ef('sc', lambda e, dc=dc, b=b: e.copy(xbf[b], rtile(dc)),
                   waits=[(dA, dA.n), (pe, wpe), (dv, base_dv), (sc, base_sc)], inc=(sc, 1))
                msc = sc.n
                for h2 in range(2):
                    tgt = ps[h2]
                    ef('pe', lambda e, b=b, h2=h2, tgt=tgt, dc=dc: e.matmul(
                        tgt[0:1, :], ones_cb, xsq[b][:, h2 * 512:(h2 + 1) * 512],
                        start=(dc == 0), stop=(dc == NDK - 1)),
                       waits=[(dv, mdv), (dO, mcg), (dv, base_dv), (sc, base_sc)])
                    tgt2 = ps[2 + h2]
                    mson[dc] = ef('pe', lambda e, b=b, h2=h2, tgt2=tgt2, dc=dc: e.matmul(
                        tgt2[0:1, :], ones_cb, xbf[b][:, h2 * 512:(h2 + 1) * 512],
                        start=(dc == 0), stop=(dc == NDK - 1)),
                       waits=[(sc, msc)], inc=(pe, 1))
            m_stats = pe.n
            S["lnstats"] = m_stats
            ef('sc', lambda e: e.activation(row_mean[:, 0:512], ps[2][0:1, :], AF.Copy, scale=1.0 / D),
               waits=[(pe, m_stats)], inc=(sc, 1))
            ef('sc', lambda e: e.activation(row_mean[:, 512:1024], ps[3][0:1, :], AF.Copy, scale=1.0 / D), inc=(sc, 1))
            ef('sc', lambda e: e.activation(row_var[:, 0:512], ps[0][0:1, :], AF.Copy, scale=1.0 / D), inc=(sc, 1))
            ef('sc', lambda e: e.activation(row_var[:, 512:1024], ps[1][0:1, :], AF.Copy, scale=1.0 / D), inc=(sc, 1))
            mrow = sc.n
            ef('dv', lambda e: e.tensor_tensor(ps[0][0:1, :], row_mean[:, 0:512], row_mean[:, 0:512], ALU.mult),
               waits=[(sc, mrow)], inc=(dv, 1))
            ef('dv', lambda e: e.tensor_tensor(ps[1][0:1, :], row_mean[:, 512:1024], row_mean[:, 512:1024], ALU.mult), inc=(dv, 1))
            ef('dv', lambda e: e.tensor_tensor(row_var[:, 0:512], row_var[:, 0:512], ps[0][0:1, :], ALU.subtract), inc=(dv, 1))
            ef('dv', lambda e: e.tensor_tensor(row_var[:, 512:1024], row_var[:, 512:1024], ps[1][0:1, :], ALU.subtract), inc=(dv, 1))
            mv = dv.n
            ef('dv', lambda e: e.tensor_single_scalar(row_var[:, :], row_var[:, :], 1e-5, ALU.add),
               waits=[(dv, mv)], inc=(dv, 1))
            ef('sc', lambda e: e.activation(row_var[:, :], row_var[:, :], AF.Sqrt),
               waits=[(dv, dv.n)], inc=(sc, 1))
            ef('dv', lambda e: e.reciprocal(row_var[:, :], row_var[:, :]), waits=[(sc, sc.n)], inc=(dv, 1))
            ef('dv', lambda e: e.scalar_tensor_tensor(row_mean[:, :], row_mean[:, :], -1.0,
                                                      row_var[:, :], ALU.mult, ALU.mult), inc=(dv, 1))
            mrs = dv.n
            for h2 in range(2):
                ef('pe', lambda e, h2=h2: e.matmul(ps[h2][:, :], ones_rf,
                                                   row_var[:, h2 * 512:(h2 + 1) * 512],
                                                   start=True, stop=True),
                   waits=[(dv, mrs), (dA, dA.n)], inc=(pe, 1))
                ef('pe', lambda e, h2=h2: e.matmul(ps[2 + h2][:, :], ones_rf,
                                                   row_mean[:, h2 * 512:(h2 + 1) * 512],
                                                   start=True, stop=True), inc=(pe, 1))
            mbc = pe.n
            ef('sc', lambda e: e.copy(rstd_b[:, 0:512], ps[0][:, :]), waits=[(pe, mbc)], inc=(sc, 1))
            ef('sc', lambda e: e.copy(rstd_b[:, 512:1024], ps[1][:, :]), inc=(sc, 1))
            ef('sc', lambda e: e.copy(bneg_b[:, 0:512], ps[2][:, :]), inc=(sc, 1))
            ef('sc', lambda e: e.copy(bneg_b[:, 512:1024], ps[3][:, :]), inc=(sc, 1))
            mbce = sc.n
            mnorm = [0] * NDK
            for dc in range(NDK):
                pa, pb = ps[2 * (dc % 2)], ps[1 + 2 * (dc % 2)]
                wv_ = max(mnorm[dc - 2] if dc >= 2 else 0, 0)
                m = 0
                for h2 in range(2):
                    tgt = pa if h2 == 0 else pb
                    ef('dv', lambda e, dc=dc, h2=h2, tgt=tgt: e.tensor_tensor(
                        tgt[:, :], rtile(dc)[:, h2 * 512:(h2 + 1) * 512],
                        rstd_b[:, h2 * 512:(h2 + 1) * 512], ALU.mult),
                       waits=[(sc, mbce), (dv, wv_)])
                    m = ef('dv', lambda e, dc=dc, h2=h2, tgt=tgt: e.tensor_tensor(
                        ytile(dc)[:, h2 * 512:(h2 + 1) * 512], tgt[:, :],
                        bneg_b[:, h2 * 512:(h2 + 1) * 512], ALU.add), inc=(dv, 1))
                mnorm[dc] = m
            S["y"] = dv.n
            return dv.n

        def gating(call):
            my = S["y"]
            mls = [0] * NTC
            for tc in range(NTC):
                Lp = ps[tc % 2]
                wdv = mls[tc - 2] if tc >= 2 else 0
                for dk in range(NDK):
                    ef('pe', lambda e, tc=tc, dk=dk, Lp=Lp: e.matmul(
                        Lp[:, 0:8], ytile(dk)[:, tc * 128:(tc + 1) * 128],
                        gwtile[:, dk * 8:(dk + 1) * 8],
                        start=(dk == 0), stop=(dk == NDK - 1)),
                       waits=[(dv, max(my, wdv)), (dO, mcg)],
                       inc=(pe, 1) if dk == NDK - 1 else None)
                mL = pe.n
                ef('dv', lambda e, tc=tc, Lp=Lp: e.tensor_reduce(m0A[:, tc:tc + 1], Lp[:, 0:8], AX.X, ALU.max),
                   waits=[(pe, mL), (dA, dA.n)], inc=(dv, 1))
                ef('sc', lambda e, tc=tc: e.activation(e0A[:, tc:tc + 1], m0A[:, tc:tc + 1], AF.Exp),
                   waits=[(dv, dv.n)], inc=(sc, 1))
                ef('dv', lambda e, tc=tc, Lp=Lp: e.tensor_single_scalar(eqs, Lp[:, 0:8], m0A[:, tc:tc + 1], ALU.is_equal))
                ef('dv', lambda e: e.scalar_tensor_tensor(tmp8, eqs, -1000.0, iotasb, ALU.mult, ALU.add))
                ef('dv', lambda e, tc=tc: e.tensor_reduce(pack[:, tc:tc + 1], tmp8, AX.X, ALU.min))
                ef('dv', lambda e, tc=tc: e.tensor_single_scalar(pack[:, tc:tc + 1], pack[:, tc:tc + 1], 1000.0, ALU.add))
                ef('dv', lambda e, tc=tc: e.tensor_single_scalar(eqs, iotasb, pack[:, tc:tc + 1], ALU.is_equal))
                ef('dv', lambda e, Lp=Lp: e.scalar_tensor_tensor(lmt, eqs, -1e30, Lp[:, 0:8], ALU.mult, ALU.add),
                   inc=(dv, 1))
                mls[tc] = dv.n
                ef('dv', lambda e, tc=tc: e.tensor_reduce(m1A[:, tc:tc + 1], lmt, AX.X, ALU.max), inc=(dv, 1))
                ef('sc', lambda e, tc=tc: e.activation(e1A[:, tc:tc + 1], m1A[:, tc:tc + 1], AF.Exp),
                   waits=[(dv, dv.n)], inc=(sc, 1))
                ef('dv', lambda e, tc=tc: e.tensor_single_scalar(eqs, lmt, m1A[:, tc:tc + 1], ALU.is_equal))
                ef('dv', lambda e: e.scalar_tensor_tensor(tmp8, eqs, -1000.0, iotasb, ALU.mult, ALU.add))
                ef('dv', lambda e, tc=tc: e.tensor_reduce(pack[:, 8 + tc:9 + tc], tmp8, AX.X, ALU.min))
                ef('dv', lambda e, tc=tc: e.tensor_single_scalar(pack[:, 8 + tc:9 + tc], pack[:, 8 + tc:9 + tc],
                                                                 1000.0, ALU.add), inc=(dv, 1))
            msce = sc.n
            ef('dv', lambda e: e.tensor_reduce(s01[:, 0:1], e0A, AX.X, ALU.add),
               waits=[(sc, msce)], inc=(dv, 1))
            ef('dv', lambda e: e.tensor_reduce(s01[:, 1:2], e1A, AX.X, ALU.add), inc=(dv, 1))
            mS = dv.n
            ef('pe', lambda e: e.matmul(ps[2][0:1, 0:2], ones_cf, s01, start=True, stop=True),
               waits=[(dv, mS), (dA, dA.n)], inc=(pe, 1))
            ef('sc', lambda e: e.copy(gstage[:, 4:6], ps[2][0:1, 0:2]), waits=[(pe, pe.n)], inc=(sc, 1))
            ef('sy', lambda e, call=call: e.dma_start(out=gin[call], in_=gstage[:, 4:6]),
               waits=[(sc, sc.n)], inc=(dA, 16))
            mgin = dA.n
            ef('gp', lambda e, call=call: e.collective_compute(
                "AllGather", ALU.bypass, replica_groups=[[0, 1], [2, 3], [4, 5], [6, 7]],
                ins=[gin[call].opt()], outs=[gout[call].opt()]),
               waits=[(dA, mgin)], inc=(cc[call], 1))
            ef('sy', lambda e, call=call: e.dma_start(out=gstage[:, 0:2], in_=gout[call][0:1, :]),
               waits=[(cc[call], 1)], inc=(dA, 16))
            ef('sy', lambda e, call=call: e.dma_start(out=gstage[:, 2:4], in_=gout[call][1:2, :]),
               inc=(dA, 16))
            mgz = dA.n
            ef('dv', lambda e: e.tensor_tensor(gstage[:, 6:8], gstage[:, 0:2], gstage[:, 2:4], ALU.add),
               waits=[(dA, mgz)], inc=(dv, 1))
            ef('dv', lambda e: e.reciprocal(gstage[:, 6:8], gstage[:, 6:8]), inc=(dv, 1))
            mrz = dv.n
            ef('pe', lambda e: e.matmul(ps[3][:, 0:2], ones_rf, gstage[:, 6:8], start=True, stop=True),
               waits=[(dv, mrz)], inc=(pe, 1))
            ef('sc', lambda e: e.copy(rzb, ps[3][:, 0:2]), waits=[(pe, pe.n)], inc=(sc, 1))
            ef('dv', lambda e: e.tensor_single_scalar(pack[:, 16:24], e0A, rzb[:, 0:1], ALU.mult),
               waits=[(sc, sc.n)])
            ef('dv', lambda e: e.tensor_single_scalar(pack[:, 24:32], e1A, rzb[:, 1:2], ALU.mult),
               inc=(dv, 1))
            mpk = dv.n
            ef('pe', lambda e: e.transpose(ps[2][0:32, 0:128], pack[:, :], identsb[:, :]),
               waits=[(dv, mpk)], inc=(pe, 1))
            ef('sc', lambda e: e.copy(trsb[:, :], ps[2][0:32, 0:128]), waits=[(pe, pe.n)], inc=(sc, 1))
            ef('sy', lambda e: e.dma_start(out=trd, in_=trsb[:, :]), waits=[(sc, sc.n)], inc=(dA, 16))
            mtr = dA.n
            ef('sy', lambda e: e.dma_start(out=row_wf[0],
                                           in_=trd[16:24, :].rearrange("a b -> (a b)").unsqueeze(0)),
               waits=[(dA, mtr)], inc=(dA, 16))
            ef('sy', lambda e: e.dma_start(out=row_wf[1],
                                           in_=trd[24:32, :].rearrange("a b -> (a b)").unsqueeze(0)),
               inc=(dA, 16))
            mwf = dA.n
            for ts in range(2):
                for wi in range(2):
                    tgt = ps[2 + wi]
                    ef('pe', lambda e, ts=ts, wi=wi, tgt=tgt: e.matmul(
                        tgt[:, :], ones_rf, row_wf[wi][:, ts * 512:(ts + 1) * 512],
                        start=True, stop=True),
                       waits=[(dA, mwf), (sc, sc.n)], inc=(pe, 1))
                mcb = pe.n
                ef('sc', lambda e, ts=ts: e.copy(crowb(0, ts), ps[2][:, :]), waits=[(pe, mcb)], inc=(sc, 1))
                ef('sc', lambda e, ts=ts: e.copy(crowb(1, ts), ps[3][:, :]), inc=(sc, 1))
            ef('gp', lambda e: e.dma_start(out=row_idx[0],
                                           in_=trd[0:8, :].rearrange("a b -> (a b)").unsqueeze(0)),
               waits=[(dA, mtr)], inc=(dO, 16))
            ef('gp', lambda e: e.dma_start(out=row_idx[1],
                                           in_=trd[8:16, :].rearrange("a b -> (a b)").unsqueeze(0)),
               inc=(dO, 16))
            mrows = dO.n
            mmask_prev_dv = 0
            for ts in range(2):
                ef('pe', lambda e, ts=ts: e.matmul(ps[0][:, :], ones_rb,
                                                   row_idx[0][:, ts * 512:(ts + 1) * 512],
                                                   start=True, stop=True),
                   waits=[(dO, mrows), (dv, mmask_prev_dv)], inc=(pe, 1))
                ef('pe', lambda e, ts=ts: e.matmul(ps[1][:, :], ones_rb,
                                                   row_idx[1][:, ts * 512:(ts + 1) * 512],
                                                   start=True, stop=True), inc=(pe, 1))
                mb = pe.n
                ef('dv', lambda e, ts=ts: e.tensor_scalar(Mm(0, ts), ps[0][:, :], ecol, 2.0,
                                                          ALU.is_equal, ALU.mult),
                   waits=[(pe, mb)], inc=(dv, 1))
                ef('dv', lambda e, ts=ts: e.tensor_scalar(Mm(1, ts), ps[1][:, :], ecol, 2.0,
                                                          ALU.is_equal, ALU.mult), inc=(dv, 1))
                mmask_prev_dv = dv.n
            return dv.n, sc.n

        def ffn(call):
            base_pe, base_dv, base_sc = pe.n, dv.n, sc.n
            layer_norm(base_pe, base_dv, base_sc)
            mr_done = [0]
            for w_i, Ad in enumerate((upAT, gateAT)):
                wa_pe = S["lnstats"] if w_i == 0 else pe.n
                ef('gp', lambda e, Ad=Ad: e.dma_start(out=_t3(upA_sb, 128), in_=_tl(Ad)),
                   waits=[(pe, wa_pe)], inc=(dP, 16))
                mA = dP.n
                for ts in range(2):
                    tgt = psQ[2 * w_i + ts]
                    for dk in range(NDK):
                        ef('pe', lambda e, ts=ts, dk=dk, tgt=tgt: e.matmul(
                            tgt[:, :], upA_sb[:, dk * 128:(dk + 1) * 128], ytile(dk, ts),
                            start=(dk == 0), stop=(dk == NDK - 1)),
                           waits=[(dP, mA), (dv, S["y"]), (dv, base_dv)],
                           inc=(pe, 1) if dk == NDK - 1 else None)
            mr_done[0] = pe.n
            mmask_dv, mmask_sc = gating(call)

            for w_i in range(2):
                for ts in range(2):
                    tgt = psQ[2 * w_i + ts]
                    ef('dv', lambda e, ts=ts, w_i=w_i, tgt=tgt: e.tensor_tensor(
                        rho(2 * w_i, ts), tgt[:, :], Mm(0, ts), ALU.mult),
                       waits=[(pe, mr_done[0]), (dv, mmask_dv)], inc=(dv, 1))
                    ef('dv', lambda e, ts=ts, tgt=tgt: e.tensor_tensor(
                        cht[0], tgt[:, :], Mm(1, ts), ALU.mult), inc=(dv, 1))
                    ef('dv', lambda e, ts=ts, w_i=w_i: e.tensor_tensor(
                        rho(2 * w_i + 1, ts), cht[0], rho(2 * w_i, ts), ALU.subtract), inc=(dv, 1))
            mrho = dv.n

            wload = [0] * NHC
            wfree = [0] * NHC

            wload_s = [0] * NHC

            def load_hc(hc):
                wpe = max(wfree[hc - 2] if hc >= 2 else 0, base_pe)
                sem = wA[hc % 2]
                ef('gp', lambda e, hc=hc: e.dma_start(
                    out=_t3(wtA(0, hc % 2), 128), in_=_tl(wupT[:, hc * 128:(hc + 1) * 128])),
                   waits=[(pe, wpe)], inc=(sem, 16))
                ef('gp', lambda e, hc=hc: e.dma_start(
                    out=_t3(wtA(1, hc % 2), 128), in_=_tl(wgateT[:, hc * 128:(hc + 1) * 128])),
                   inc=(sem, 16))
                wload[hc] = sem.n

            def load_small(hc):
                wpe = max(wfree_q[hc - 2] if hc >= 2 else 0, base_pe)
                sem = wS[hc % 2]
                ef('gp', lambda e, hc=hc: e.dma_start(
                    out=btl(0, hc % 2), in_=upBT[:, hc * 128:(hc + 1) * 128]),
                   waits=[(pe, wpe)], inc=(sem, 16))
                ef('gp', lambda e, hc=hc: e.dma_start(
                    out=btl(1, hc % 2), in_=gateBT[:, hc * 128:(hc + 1) * 128]), inc=(sem, 16))
                ef('gp', lambda e, hc=hc: e.dma_start(
                    out=_t3(btl(2, hc % 2), 128), in_=_tl(dnAT[hc * 128:(hc + 1) * 128, :])), inc=(sem, 16))
                wload_s[hc] = sem.n

            load_hc(0)
            load_hc(1)
            load_small(0)
            load_small(1)
            slA_free = [0, 0]
            slB_free = [0, 0]
            cht_free = [0, 0]
            pend_q = []
            wfree_q = [0] * NHC
            silu0_free = [0]
            silu1_free = [0]
            bA_free = [0]
            bB_free = [0]
            mdnb = [0]
            mrho2 = [0]
            bfree = [0] * NDK
            bfree2 = [0] * 32
            psfree_dv = [0]

            for half in range(2):
                bload2 = [0] * 32
                bfree2_prev = list(bfree2) if half == 1 else [0] * 32

                def load_b(dc, pc, half=half, bload2=bload2, bfree2_prev=bfree2_prev):
                    idx = 2 * dc + pc
                    prev = bfree2[idx - 2] if idx >= 2 else (bfree2_prev[idx] if half == 1 else 0)
                    ef('gp', lambda e, dc=dc, pc=pc, half=half: e.dma_start(
                        out=_t3(wtB((2 * dc + pc) % 2), 128),
                        in_=_tl(wdownT[half * 2816 + pc * 1408:half * 2816 + (pc + 1) * 1408,
                                       dc * 128:(dc + 1) * 128])),
                       waits=[(pe, max(prev, base_pe)), (dv, base_dv)], inc=(wB[idx % 2], 16))
                    bload2[idx] = wB[idx % 2].n

                for k in range(HALF):
                    hc = half * HALF + k
                    par = hc % 2
                    if k == HALF - 3:
                        load_b(0, 0)
                        load_b(0, 1)
                    for ts in range(2):
                        pA, pB, pDu, pDg = ps[0], ps[1], ps[2], ps[3]
                        for dk in range(NDK):
                            ef('pe', lambda e, ts=ts, dk=dk, par=par: e.matmul(
                                pA[:, :], wtA(0, par)[:, dk * 128:(dk + 1) * 128], ytile(dk, ts),
                                start=(dk == 0), stop=False),
                               waits=([(wA[par], wload[hc]), (sc, slA_free[0]), (dv, mmask_dv),
                                       (dv, psfree_dv[0])]
                                      if dk == 0 else []))
                        ef('pe', lambda e, ts=ts, par=par: e.matmul(
                            pA[:, :], btl(0, par), rho(0, ts), start=False, stop=True),
                           waits=[(dv, mrho), (wS[par], wload_s[hc])], inc=(pe, 1))
                        mAup = pe.n
                        for dk in range(NDK):
                            ef('pe', lambda e, ts=ts, dk=dk, par=par: e.matmul(
                                pB[:, :], wtA(1, par)[:, dk * 128:(dk + 1) * 128], ytile(dk, ts),
                                start=(dk == 0), stop=False),
                               waits=([(sc, slA_free[1])] if dk == 0 else []))
                        ef('pe', lambda e, ts=ts, par=par: e.matmul(
                            pB[:, :], btl(1, par), rho(2, ts), start=False, stop=True),
                           inc=(pe, 1))
                        mAg = pe.n
                        ef('pe', lambda e, ts=ts, par=par: e.matmul(
                            pDu[:, :], btl(0, par), rho(1, ts), start=True, stop=True),
                           waits=[(dv, slB_free[0]), (dv, psfree_dv[0]), (sc, mmask_sc)], inc=(pe, 1))
                        mDu = pe.n
                        ef('pe', lambda e, ts=ts, par=par: e.matmul(
                            pDg[:, :], btl(1, par), rho(3, ts), start=True, stop=True),
                           waits=[(dv, slB_free[1]), (dv, psfree_dv[0]), (sc, mmask_sc)], inc=(pe, 1))
                        mDg = pe.n
                        ef('sc', lambda e: e.copy(bA, pA[:, :]),
                           waits=[(pe, mAup), (dv, bA_free[0])], inc=(sc, 1))
                        slA_free[0] = sc.n
                        ef('sc', lambda e: e.copy(bB, pB[:, :]),
                           waits=[(pe, mAg), (dv, bB_free[0])], inc=(sc, 1))
                        slA_free[1] = sc.n
                        it2 = (hc * 2 + ts) % 2
                        ch2 = cht[2] if it2 == 0 else kstg[:, 0:512]
                        ch3 = cht[3] if it2 == 0 else kstg[:, 512:1024]
                        ef('sc', lambda e: e.activation(siluX, bA, AF.Silu),
                           waits=[(dv, silu0_free[0])], inc=(sc, 1))
                        ms0 = sc.n
                        ef('dv', lambda e: e.tensor_tensor(cht[0], siluX, bB, ALU.mult),
                           waits=[(sc, ms0), (pe, cht_free[it2])], inc=(dv, 1))
                        silu0_free[0] = dv.n
                        ef('dv', lambda e: e.tensor_tensor(s1t, bA, pDu[:, :], ALU.add),
                           waits=[(pe, mDu)], inc=(dv, 1))
                        slB_free[0] = dv.n
                        bA_free[0] = dv.n
                        ef('sc', lambda e: e.activation(siluX, s1t, AF.Silu),
                           waits=[(dv, dv.n)], inc=(sc, 1))
                        ms1 = sc.n
                        ef('dv', lambda e: e.tensor_tensor(t2t, bB, pDg[:, :], ALU.add),
                           waits=[(pe, mDg)], inc=(dv, 1))
                        slB_free[1] = dv.n
                        bB_free[0] = dv.n
                        ef('dv', lambda e: e.tensor_tensor(cht[1], siluX, t2t, ALU.mult),
                           waits=[(sc, ms1)], inc=(dv, 1))
                        silu0_free[0] = dv.n
                        silu1_free[0] = dv.n
                        ef('dv', lambda e, ts=ts, ch2=ch2: e.tensor_tensor(ch2, cht[0], crowb(0, ts), ALU.mult))
                        ef('dv', lambda e, ts=ts, ch3=ch3: e.tensor_tensor(ch3, cht[1], crowb(1, ts), ALU.mult))
                        ef('dv', lambda e, k=k, ts=ts, ch2=ch2, ch3=ch3: e.tensor_tensor(
                            mixed(k, ts), ch2, ch3, ALU.add), inc=(dv, 1))
                        mch = dv.n
                        def qmm(e, hc=hc, par=par, ts=ts, ch2=ch2):
                            return e.matmul(psQ[ts][:, :], btl(2, par), ch2,
                                            start=(hc == 0), stop=(hc == NHC - 1))
                        def qmm2(e, hc=hc, par=par, ts=ts, ch3=ch3):
                            return e.matmul(psQ[2 + ts][:, :], btl(2, par), ch3,
                                            start=(hc == 0), stop=(hc == NHC - 1))
                        pend_q.append((qmm, qmm2, mch, hc, it2))
                        if len(pend_q) > 1:
                            f1, f2, mw, phc, pit2 = pend_q.pop(0)
                            ef('pe', f1, waits=[(dv, mw)])
                            ef('pe', f2, inc=(pe, 1))
                            cht_free[pit2] = pe.n
                            wfree_q[phc] = pe.n
                            if ts == 0 and hc + 1 < NHC and hc >= 1:
                                load_small(hc + 1)
                        if ts == 1:
                            wfree[hc] = pe.n
                    if hc + 2 < NHC:
                        load_hc(hc + 2)

                if half == 1:
                    while pend_q:
                        f1, f2, mw, phc, pit2 = pend_q.pop(0)
                        ef('pe', f1, waits=[(dv, mw)])
                        ef('pe', f2, inc=(pe, 1))
                        cht_free[pit2] = pe.n
                        wfree_q[phc] = pe.n
                mmixed = dv.n
                if half == 1:
                    ef('gp', lambda e: e.dma_start(out=dnB_sb, in_=dnBT),
                       waits=[(pe, pe.n)], inc=(dP, 16))
                    mdnb[0] = dP.n
                    for ts in range(2):
                        ef('dv', lambda e, ts=ts: e.tensor_tensor(cht[0], psQ[ts][:, :], Mm(0, ts), ALU.mult),
                           waits=[(pe, max(cht_free[0], cht_free[1]))], inc=(dv, 1))
                        ef('dv', lambda e, ts=ts: e.tensor_tensor(cht[1], psQ[2 + ts][:, :], Mm(1, ts), ALU.mult),
                           inc=(dv, 1))
                        ef('dv', lambda e, ts=ts: e.tensor_tensor(rho(0, ts), cht[0], cht[1], ALU.add),
                           inc=(dv, 1))
                    mrho2[0] = dv.n
                for dc in range(NDK):
                    pf = [ps[(dc % 2) * 2], ps[(dc % 2) * 2 + 1]]
                    mpc = [0, 0]
                    for k in range(HALF):
                        pc = k // 11
                        kk = k % 11
                        buf = (2 * dc + pc) % 2
                        for ts in range(2):
                            isl = (half == 0 and k == HALF - 1 and ts == 1)
                            w0 = []
                            if kk == 0 and ts == 0:
                                w0 = [(wB[(2 * dc + pc) % 2], bload2[2 * dc + pc]), (dv, mmixed),
                                      (dv, bfree[dc - 2] if dc >= 2 else 0),
                                      (sc, slA_free[0]), (sc, slA_free[1]),
                                      (dv, slB_free[0]), (dv, slB_free[1]),
                                      (dv, silu1_free[0])]
                            ef('pe', lambda e, ts=ts, k=k, kk=kk, buf=buf, pX=pf[ts],
                               stp=(half == 0 and k == HALF - 1): e.matmul(
                                pX[:, :], wtB(buf)[:, kk * 128:(kk + 1) * 128], mixed(k, ts),
                                start=(k == 0), stop=stp),
                               waits=w0,
                               inc=(pe, 1) if (kk == 10 and ts == 1) or isl else None)
                        if kk == 10 and ts == 1:
                            mpc[pc] = pe.n
                        if k == 10:
                            bfree2[2 * dc] = pe.n
                            ni = 2 * dc + 2
                            if ni < 32:
                                load_b(ni // 2, ni % 2)
                        if k == HALF - 1:
                            bfree2[2 * dc + 1] = pe.n
                            ni = 2 * dc + 3
                            if ni < 32:
                                load_b(ni // 2, ni % 2)
                    for ts in range(2):
                        if half == 1:
                            ef('pe', lambda e, dc=dc, ts=ts, pX=pf[ts]: e.matmul(
                                pX[:, :], dnB_sb[:, dc * 128:(dc + 1) * 128], rho(0, ts),
                                start=False, stop=True),
                               waits=[(dP, mdnb[0]), (dv, mrho2[0])], inc=(pe, 1))
                        mpb = pe.n
                        ef('dv', lambda e, dc=dc, ts=ts, pX=pf[ts]: e.scalar_tensor_tensor(
                            rtile(dc, ts), pX[:, :], 0.5, rtile(dc, ts), ALU.mult, ALU.add),
                           waits=[(pe, mpb)], inc=(dv, 1))
                    bfree[dc] = dv.n
                psfree_dv[0] = dv.n

        def attention():
            base_pe, base_dv, base_sc = pe.n, dv.n, sc.n
            layer_norm(base_pe, base_dv, base_sc)
            my = S["y"]
            ef('gp', lambda e: e.dma_start(out=row_bv, in_=bvd),
               waits=[(dv, base_dv)], inc=(dO, 16))
            mbv = dO.n
            for h4 in range(4):
                ef('pe', lambda e, h4=h4: e.matmul(ps[h4][:, :], ones_rb,
                                                   row_bv[:, h4 * 512:(h4 + 1) * 512],
                                                   start=True, stop=True),
                   waits=[(dO, mbv), (dv, my)], inc=(pe, 1))
            mbc2 = pe.n
            for h4 in range(4):
                ef('sc', lambda e, h4=h4: e.copy(bv_b[:, h4 * 512:(h4 + 1) * 512], ps[h4][:, :]),
                   waits=[(pe, mbc2)], inc=(sc, 1))
            mbve = sc.n
            pload = [0] * (2 * NDK)
            pfree = [0] * (2 * NDK)
            pfree_sc = [0] * (2 * NDK)
            kdma = {0: 0, 1: 0}
            kview = cin.rearrange("a (b c) -> (a b) c", b=2)

            def load_p(i):
                wpe = max(pfree[i - 2] if i >= 2 else 0, base_pe)
                Wd = wq_d if i < NDK else wk_d
                oc = i % NDK
                ef('gp', lambda e, Wd=Wd, oc=oc, i=i: e.dma_start(
                    out=_t3(wtP(i % 2), 128), in_=_tl(Wd[:, oc * 128:(oc + 1) * 128])),
                   waits=[(pe, wpe)], inc=(wP[i % 2], 16))
                pload[i] = wP[i % 2].n

            load_p(0)
            load_p(1)
            for i in range(2 * NDK):
                oc = i % NDK
                isq = i < NDK
                for ts in range(2):
                    pX = ps[(i % 2) * 2 + ts]
                    for dk in range(NDK):
                        ef('pe', lambda e, i=i, ts=ts, dk=dk, pX=pX: e.matmul(
                            pX[:, :], wtP(i % 2)[:, dk * 128:(dk + 1) * 128], ytile(dk, ts),
                            start=(dk == 0), stop=(dk == NDK - 1)),
                           waits=([(wP[i % 2], pload[i]), (dv, my), (sc, mbve),
                                   (dv, pfree_sc[i - 2] if i >= 2 else 0)] if dk == 0 else []),
                           inc=(pe, 1) if dk == NDK - 1 else None)
                    mproj = pe.n
                    bias = bq_s[:, oc:oc + 1] if isq else bk_s[:, oc:oc + 1]
                    scl = SM if isq else 1.0
                    ef('dv', lambda e, ts=ts, pX=pX, bias=bias, scl=scl: e.tensor_scalar(
                        kstg[:, ts * 512:(ts + 1) * 512], pX[:, :], bias, scl,
                        ALU.add, ALU.mult),
                       waits=[(pe, mproj), (dK[ts], kdma[ts])], inc=(dv, 1))
                    dst = (qdram[oc * 128:(oc + 1) * 128, ts * 512:(ts + 1) * 512] if isq
                           else kview[oc * 128:(oc + 1) * 128, ts * 512:(ts + 1) * 512])
                    ef('sy', lambda e, dst=dst, ts=ts: e.dma_start(
                        out=dst, in_=kstg[:, ts * 512:(ts + 1) * 512]),
                       waits=[(dv, dv.n)], inc=(dK[ts], 16))
                    kdma[ts] = dK[ts].n
                pfree[i] = pe.n
                pfree_sc[i] = dv.n
                if i + 2 < 2 * NDK:
                    load_p(i + 2)
            vsem = [wV[0], wV[1], wS[0], wS[1]]
            vw_free = [base_pe] * 4
            vload = {}
            vseq = [(tcb, dvb, dk) for tcb in range(2) for dvb in range(4) for dk in range(NDK)]

            def loadv(i):
                tcb, dvb, dk = vseq[i]
                buf = i % 4
                ef('gp', lambda e, dvb=dvb, dk=dk, buf=buf: e.dma_start(
                    out=wvt(buf), in_=wv_d[dk * 128:(dk + 1) * 128, dvb * 512:(dvb + 1) * 512]),
                   waits=[(pe, vw_free[buf]), (dv, base_dv)], inc=(vsem[buf], 16))
                vload[i] = vsem[buf].n

            for i in range(4):
                loadv(i)
            vev_free = [base_dv] * 4
            for tcb in range(2):
                for dvb in range(4):
                    for dk in range(NDK):
                        i = (tcb * 4 + dvb) * NDK + dk
                        buf = i % 4
                        for j4 in range(4):
                            tc = tcb * 4 + j4
                            w0 = [(vsem[buf], vload[i])] if j4 == 0 else []
                            if dk == 0:
                                w0 = w0 + [(dv, vev_free[j4]), (dv, my)]
                            ef('pe', lambda e, tc=tc, dk=dk, j4=j4, buf=buf: e.matmul(
                                psQ[j4][:, :], ytile(dk)[:, tc * 128:(tc + 1) * 128], wvt(buf),
                                start=(dk == 0), stop=(dk == NDK - 1)),
                               waits=w0,
                               inc=(pe, 1) if j4 == 3 else None)
                        vw_free[buf] = pe.n
                        if i + 4 < len(vseq):
                            loadv(i + 4)
                    mvp = pe.n
                    for j4 in range(4):
                        tc = tcb * 4 + j4
                        sslot = j4 % 2
                        ef('dv', lambda e, j4=j4, dvb=dvb, sslot=sslot: e.tensor_tensor(
                            kstg[:, sslot * 512:sslot * 512 + 512], psQ[j4][:, :],
                            bv_b[:, dvb * 512:(dvb + 1) * 512], ALU.add),
                           waits=[(pe, mvp), (sc, mbve), (dK[sslot], kdma[sslot])], inc=(dv, 1))
                        vev_free[j4] = dv.n
                        ef('sy', lambda e, tc=tc, dvb=dvb, sslot=sslot: e.dma_start(
                            out=cin[1024 + tc * 128:1024 + (tc + 1) * 128, dvb * 512:(dvb + 1) * 512],
                            in_=kstg[:, sslot * 512:sslot * 512 + 512]),
                           waits=[(dv, dv.n)], inc=(dK[sslot], 16))
                        kdma[sslot] = dK[sslot].n
            mcin = None
            ef('gp', lambda e: e.collective_compute(
                "AllGather", ALU.bypass, replica_groups=[[0, 1], [2, 3], [4, 5], [6, 7]],
                ins=[cin.opt()], outs=[cout.opt()]),
               waits=[(dK[0], dK[0].n), (dK[1], dK[1].n)], inc=(cc[2], 1))
            ef('sy', lambda e: e.dma_start(out=amask_sb, in_=amask_d),
               waits=[(dv, mrho_dead_dv[0])], inc=(dA, 16))
            mam = dA.n
            kvout = cout.rearrange("a (b c) -> (a b) c", b=2)
            hload = [0] * NH
            hvl = [0, 0]
            hfree_q = [base_dv, base_dv]
            hfree_k = [base_pe, base_pe]
            hfree_v = [base_pe, base_pe]
            sco_free = [base_sc, base_sc]
            pb_free = [base_pe, base_pe]
            ctx_free = [base_dv]
            l_free = [base_sc]

            def load_head(h):
                buf = h % 2
                sem = dH[buf]
                ef('sy', lambda e, h=h, buf=buf: e.dma_start(
                    out=q_t[buf], in_=qdram[h * 128:(h + 1) * 128, :]),
                   waits=[(dv, hfree_q[buf])], inc=(sem, 16))
                for j in range(2):
                    src = kvout[4096 * j + h * 128:4096 * j + (h + 1) * 128, :]
                    dst = k_t[buf].rearrange("p (g c) -> p g c", c=128)[:, j::2, :]
                    ef('sy', lambda e, src=src, dst=dst: e.dma_start(
                        out=dst, in_=src.rearrange("p (g c) -> p g c", c=128)),
                       waits=[(cc[2], 1), (pe, hfree_k[buf])], inc=(sem, 16))
                vbuf = buf
                for j in range(2):
                    src = cout[2048 * j + 1024:2048 * j + 2048, h * 128:(h + 1) * 128]
                    dst = v_t[vbuf].rearrange("p (g w) -> p g w", w=128)[:, j::2, :]
                    ef('sy', lambda e, src=src, dst=dst: e.dma_start(
                        out=dst, in_=src.rearrange("(g p) w -> p g w", p=128)),
                       waits=[(cc[2], 1), (pe, hfree_v[vbuf])], inc=(sem, 16))
                hvl[vbuf] = sem.n
                hload[h] = sem.n

            load_head(0)
            load_head(1)
            for h in range(NH):
                buf = h % 2
                vbuf = h % 2
                hp = h % 2
                mpv = 0
                pend_pv = []
                for kt in range(16):
                    s0 = 128 * (kt // 2)
                    pr = [ps[0], ps[1]] if kt % 2 == 0 else [ps[2], ps[3]]
                    segs = ([(0, s0, 512)] if s0 < 512 else []) + [(1, max(512, s0), 1024)]
                    for n_, (si, a, b_) in enumerate(segs):
                        ef('pe', lambda e, kt=kt, buf=buf, pr=pr, si=si, a=a, b_=b_: e.matmul(
                            pr[si][:, a - si * 512:b_ - si * 512],
                            k_t[buf][:, kt * 128:(kt + 1) * 128], q_t[buf][:, a:b_],
                            start=True, stop=True),
                           waits=([(dH[buf], hload[h]), (sc, sco_free[kt % 2])] if n_ == 0 else []),
                           inc=(pe, 1) if n_ == len(segs) - 1 else None)
                    msco = pe.n
                    for (si, a, b_) in segs:
                        ef('sc', lambda e, kt=kt, si=si, a=a, b_=b_, pr=pr: e.activation(
                            probsT(kt % 2)[:, a:b_], pr[si][:, a - si * 512:b_ - si * 512], AF.Exp),
                           waits=[(pe, max(msco, pb_free[kt % 2])), (dA, mam)], inc=(sc, 1))
                    msce2 = sc.n
                    sco_free[kt % 2] = msce2
                    ef('dv', lambda e, kt=kt, s0=s0: e.tensor_tensor(
                        probsT(kt % 2)[:, s0:s0 + 128], probsT(kt % 2)[:, s0:s0 + 128],
                        amask_sb[:, kt * 128:(kt + 1) * 128], ALU.mult),
                       waits=[(sc, msce2)], inc=(dv, 1))
                    mmk = dv.n

                    def mk_pv(kt=kt, segs=segs, vbuf=vbuf, hp=hp, mmk=mmk, h=h):
                        for n_, (si, a, b_) in enumerate(segs):
                            stp = (kt == (15 if si == 1 else 7))
                            ef('pe', lambda e, kt=kt, si=si, a=a, b_=b_, vbuf=vbuf, stp=stp: e.matmul(
                                psQ[si][:, a - si * 512:b_ - si * 512],
                                v_t[vbuf][:, kt * 128:(kt + 1) * 128],
                                probsT(kt % 2)[:, a:b_],
                                start=(kt == 0), stop=stp),
                               waits=([(dH[vbuf], hvl[vbuf]), (dv, mmk), (dv, ctx_free[0]),
                                       (sc, l_free[0])] if n_ == 0 else []))
                            ef('pe', lambda e, kt=kt, si=si, a=a, b_=b_, stp=stp: e.matmul(
                                psQ[2 + si][0:1, a - si * 512:b_ - si * 512], ones_cb,
                                probsT(kt % 2)[:, a:b_],
                                start=(kt == 0), stop=stp),
                               inc=(pe, 1) if n_ == len(segs) - 1 else None)
                        pb_free[kt % 2] = pe.n

                    pend_pv.append(mk_pv)
                    if len(pend_pv) > 1:
                        pend_pv.pop(0)()
                while pend_pv:
                    pend_pv.pop(0)()
                mpv = pe.n
                ef('dv', lambda e: e.reciprocal(linv_f[:, 0:512], psQ[2][0:1, :]),
                   waits=[(pe, mpv)], inc=(dv, 1))
                ef('dv', lambda e: e.reciprocal(linv_f[:, 512:1024], psQ[3][0:1, :]), inc=(dv, 1))
                ml = dv.n
                for si in range(2):
                    ef('pe', lambda e, si=si: e.matmul(psQ[2 + si][:, :], ones_rf,
                                                       linv_f[:, si * 512:(si + 1) * 512],
                                                       start=True, stop=True),
                       waits=[(dv, ml)], inc=(pe, 1))
                mlb = pe.n
                for si in range(2):
                    ef('sc', lambda e, si=si: e.copy(linv_b[:, si * 512:(si + 1) * 512], psQ[2 + si][:, :]),
                       waits=[(pe, mlb)], inc=(sc, 1))
                l_free[0] = sc.n
                for si in range(2):
                    ef('dv', lambda e, h=h, si=si: e.tensor_tensor(
                        ytile(h)[:, si * 512:(si + 1) * 512], psQ[si][:, :],
                        linv_b[:, si * 512:(si + 1) * 512], ALU.mult),
                       waits=[(sc, l_free[0])], inc=(dv, 1))
                ctx_free[0] = dv.n
                hfree_q[buf] = ctx_free[0]
                hfree_k[buf] = mpv
                hfree_v[vbuf] = mpv
                if h + 2 < NH:
                    load_head(h + 2)
            mattn = dv.n
            oload = [0] * NDK
            ofree = [0] * NDK

            def load_o(dc):
                ef('gp', lambda e, dc=dc: e.dma_start(
                    out=_t3(wtP(dc % 2), 128), in_=_tl(wo_d[:, dc * 128:(dc + 1) * 128])),
                   waits=[(pe, pfree[2 * NDK - 1]),
                          (dv, ofree[dc - 2] if dc >= 2 else 0)], inc=(wP[dc % 2], 16))
                oload[dc] = wP[dc % 2].n

            load_o(0)
            load_o(1)
            for dc in range(NDK):
                for ts in range(2):
                    pX = ps[(dc % 2) * 2 + ts]
                    for hh in range(NH):
                        ef('pe', lambda e, dc=dc, ts=ts, hh=hh, pX=pX: e.matmul(
                            pX[:, :], wtP(dc % 2)[:, hh * 128:(hh + 1) * 128], ytile(hh, ts),
                            start=(hh == 0), stop=(hh == NH - 1)),
                           waits=([(wP[dc % 2], oload[dc]), (dv, mattn), (sc, l_free[0]),
                                   (dv, ofree[dc - 2] if dc >= 2 else 0)] if hh == 0 else []),
                           inc=(pe, 1) if hh == NH - 1 else None)
                    mo = pe.n
                    ef('dv', lambda e, dc=dc, ts=ts, pX=pX: e.tensor_tensor(
                        rtile(dc, ts), pX[:, :], rtile(dc, ts), ALU.add),
                       waits=[(pe, mo)], inc=(dv, 1))
                ofree[dc] = dv.n
                if dc + 2 < NDK:
                    load_o(dc + 2)

        mrho_dead_dv = [0]
        ffn(0)
        mrho_dead_dv[0] = dv.n
        if STAGE >= 2:
            attention()
        if STAGE >= 3:
            ffn(1)

        ef('sy', lambda e: e.dma_start(out=_tl(out), in_=_t3(resid[:, :], T)),
           waits=[(dv, dv.n), (pe, pe.n), (sc, sc.n)], inc=(dOUT, 16))
        ef('sy', lambda e: e.nop(), waits=[(dOUT, dOUT.n)])

        @block.sync
        def _(e):
            P.emit('sy', e)

        @block.gpsimd
        def _(e):
            P.emit('gp', e)

        @block.tensor
        def _(e):
            P.emit('pe', e)

        @block.scalar
        def _(e):
            P.emit('sc', e)

        @block.vector
        def _(e):
            P.emit('dv', e)

    return nc


_NC_CACHE = {}


def _amask(j):
    m = np.zeros((128, 16 * 128), dtype=np.float32)
    for kt in range(16):
        m0 = kt // 2
        qt = 2 * m0 + j
        gk = kt * 128 + np.arange(128)[:, None]
        gq = qt * 128 + np.arange(128)[None, :]
        m[:, kt * 128:(kt + 1) * 128] = (gq >= gk).astype(np.float32)
    return m


def kernel(L0_attn_hidden_states, L0_attn_Wq, L0_attn_bq, L0_attn_Wk, L0_attn_bk,
           L0_attn_Wv, L0_attn_bv, L0_attn_Wo, L0_ffn_gate_w, L0_ffn_w_up,
           L0_ffn_w_gate, L0_ffn_w_down, L0_ffn_up_A, L0_ffn_up_B, L0_ffn_gate_A,
           L0_ffn_gate_B, L0_ffn_down_A, L0_ffn_down_B):
    x = np.ascontiguousarray(np.asarray(L0_attn_hidden_states, dtype=np.float32))
    B = x.shape[0]
    cfg = lambda a: np.ascontiguousarray(np.asarray(a, dtype=np.float32))

    gate_w = cfg(L0_ffn_gate_w)
    gwt = np.zeros((128, 128), dtype=np.float32)
    for dk in range(NDK):
        gwt[:, dk * 8:(dk + 1) * 8] = gate_w[:, dk * 128:(dk + 1) * 128].T
    common = {
        "wq": cfg(L0_attn_Wq), "wk": cfg(L0_attn_Wk), "wv": cfg(L0_attn_Wv),
        "wo": cfg(L0_attn_Wo),
        "bqd": cfg(np.asarray(L0_attn_bq, dtype=np.float32).reshape(NDK, 128).T),
        "bkd": cfg(np.asarray(L0_attn_bk, dtype=np.float32).reshape(NDK, 128).T),
        "bvd": cfg(np.asarray(L0_attn_bv, dtype=np.float32).reshape(1, D)),
        "wupT": cfg(np.asarray(L0_ffn_w_up).T),
        "wgateT": cfg(np.asarray(L0_ffn_w_gate).T),
        "wdownT": cfg(np.asarray(L0_ffn_w_down).T),
        "upAT": cfg(np.asarray(L0_ffn_up_A).reshape(128, D).T),
        "gateAT": cfg(np.asarray(L0_ffn_gate_A).reshape(128, D).T),
        "upBT": cfg(np.asarray(L0_ffn_up_B).transpose(0, 2, 1).reshape(128, H)),
        "gateBT": cfg(np.asarray(L0_ffn_gate_B).transpose(0, 2, 1).reshape(128, H)),
        "dnAT": cfg(np.asarray(L0_ffn_down_A).reshape(128, H).T),
        "dnBT": cfg(np.asarray(L0_ffn_down_B).transpose(0, 2, 1).reshape(128, D)),
        "gwt": gwt,
        "identf": np.eye(128, dtype=np.float32),
        "ones_r": np.ones((1, 128), dtype=np.float32),
        "ones_c": np.ones((128, 1), dtype=np.float32),
        "ecol_d": (np.arange(128, dtype=np.float32) // 16).reshape(128, 1),
        "iota8_d": np.broadcast_to(np.arange(8, dtype=np.float32), (128, 8)).copy(),
    }
    amasks = [_amask(0), _amask(1)]

    if "nc" not in _NC_CACHE:
        _NC_CACHE["nc"] = build_nc()
    nc = _NC_CACHE["nc"]

    in_maps = []
    for c in range(8):
        b, j = c // 2, c % 2
        xt = x[b].reshape(16, 128, D)[j::2].reshape(T, D).T
        m = dict(common)
        m["xs"] = np.ascontiguousarray(xt)
        m["amask_d"] = amasks[j].astype(ml_dtypes.bfloat16)
        in_maps.append(m)

    res = run_bass_kernel_spmd(nc, in_maps, core_ids=list(range(8)))

    y = np.empty_like(x)
    for c in range(8):
        b, j = c // 2, c % 2
        oc = np.asarray(res.results[c]["out"])
        yt = oc.T.reshape(NTC, 128, D)
        for i in range(NTC):
            y[b, (2 * i + j) * 128:(2 * i + j + 1) * 128, :] = yt[i]
    return y


# version 10 (final)
# speedup vs baseline: 1.5584x; 1.0581x over previous
import math
import numpy as np
import ml_dtypes
import concourse.bass as bass
import concourse.mybir as mybir
from concourse.bass_utils import run_bass_kernel_spmd
from contextlib import ExitStack

F32 = mybir.dt.float32
F32R = mybir.dt.float32r
BF16 = mybir.dt.bfloat16
AF = mybir.ActivationFunctionType
ALU = mybir.AluOpType
AX = mybir.AxisListType

D, H, NH, DH, T, E, R = 2048, 5632, 16, 128, 1024, 8, 16
NDK = D // 128
NHC = H // 128
HALF = NHC // 2
NTC = T // 128
TS = 512
SM = 1.0 / math.sqrt(DH)
STAGE = 3


class _Sem:
    __slots__ = ("name", "n", "h")
    def __init__(self, name):
        self.name, self.n, self.h = name, 0, None


class _B:
    def __init__(self):
        self.ops = []
        self.sems = []
        self._lw = {}
        self.auto_chain = {}

    def sem(self, name):
        s = _Sem(name)
        self.sems.append(s)
        return s

    def op(self, eng, fn, waits=(), inc=None):
        waits = list(waits)
        if eng in self.auto_chain:
            s = self.auto_chain[eng]
            waits.append((s, s.n))
            if inc is None:
                inc = (s, 1)
        w = []
        for (s, v) in waits:
            if v <= 0:
                continue
            k = (eng, s.name)
            if self._lw.get(k, -1) >= v:
                continue
            self._lw[k] = v
            w.append((s, v))
        if inc is not None:
            inc[0].n += inc[1]
        self.ops.append((eng, fn, w, inc))
        return inc[0].n if inc else None

    def emit(self, name, eng):
        for (e, fn, w, inc) in self.ops:
            if e != name:
                continue
            for (s, v) in w:
                eng.wait_ge(s.h, v)
            inst = fn(eng)
            if inc is not None:
                inst.then_inc(inc[0].h, inc[1])


def _tl(src2d):
    return src2d.rearrange("(a p) c -> p a c", p=128)


def _t3(dst2d, c):
    return dst2d.rearrange("p (a c) -> p a c", c=c)


def build_nc():
    nc = bass.Bass("TRN2", target_bir_lowering=False)

    xs = nc.dram_tensor("xs", [D, T], F32, kind="ExternalInput").ap()
    wq_d = nc.dram_tensor("wq", [D, D], F32, kind="ExternalInput").ap()
    wk_d = nc.dram_tensor("wk", [D, D], F32, kind="ExternalInput").ap()
    wv_d = nc.dram_tensor("wv", [D, D], F32, kind="ExternalInput").ap()
    wo_d = nc.dram_tensor("wo", [D, D], F32, kind="ExternalInput").ap()
    bqd = nc.dram_tensor("bqd", [128, NDK], F32, kind="ExternalInput").ap()
    bkd = nc.dram_tensor("bkd", [128, NDK], F32, kind="ExternalInput").ap()
    bvd = nc.dram_tensor("bvd", [1, D], F32, kind="ExternalInput").ap()
    wupT = nc.dram_tensor("wupT", [D, H], F32, kind="ExternalInput").ap()
    wgateT = nc.dram_tensor("wgateT", [D, H], F32, kind="ExternalInput").ap()
    wdownT = nc.dram_tensor("wdownT", [H, D], F32, kind="ExternalInput").ap()
    upAT = nc.dram_tensor("upAT", [D, 128], F32, kind="ExternalInput").ap()
    gateAT = nc.dram_tensor("gateAT", [D, 128], F32, kind="ExternalInput").ap()
    upBT = nc.dram_tensor("upBT", [128, H], F32, kind="ExternalInput").ap()
    gateBT = nc.dram_tensor("gateBT", [128, H], F32, kind="ExternalInput").ap()
    dnAT = nc.dram_tensor("dnAT", [H, 128], F32, kind="ExternalInput").ap()
    dnBT = nc.dram_tensor("dnBT", [128, D], F32, kind="ExternalInput").ap()
    gwt = nc.dram_tensor("gwt", [128, 128], F32, kind="ExternalInput").ap()
    identf = nc.dram_tensor("identf", [128, 128], F32, kind="ExternalInput").ap()
    ones_r = nc.dram_tensor("ones_r", [1, 128], F32, kind="ExternalInput").ap()
    ones_c = nc.dram_tensor("ones_c", [128, 1], F32, kind="ExternalInput").ap()
    ecol_d = nc.dram_tensor("ecol_d", [128, 1], F32, kind="ExternalInput").ap()
    iota8_d = nc.dram_tensor("iota8_d", [128, 8], F32, kind="ExternalInput").ap()
    amask_d = nc.dram_tensor("amask_d", [128, 16 * 128], BF16, kind="ExternalInput").ap()
    out = nc.dram_tensor("out", [D, T], F32, kind="ExternalOutput").ap()

    qdram = nc.dram_tensor("qdram", [D, T], BF16).ap()
    cin = nc.dram_tensor("cin", [2048, 2048], BF16).ap()
    coutk = nc.dram_tensor("coutk", [2048, 2048], BF16).ap()
    coutv = nc.dram_tensor("coutv", [2048, 2048], BF16).ap()
    trd = nc.dram_tensor("trd", [32, 128], F32).ap()
    gin = [nc.dram_tensor(f"gin{i}", [1, 2], F32).ap() for i in range(2)]
    gout = [nc.dram_tensor(f"gout{i}", [2, 2], F32).ap() for i in range(2)]

    P = _B()
    dA = P.sem("dA")
    dO = P.sem("dO")
    wA = [P.sem("wA0"), P.sem("wA1")]
    wB = [P.sem("wB0"), P.sem("wB1")]
    wP = [P.sem("wP0"), P.sem("wP1")]
    wV = [P.sem("wV0"), P.sem("wV1")]
    dK = [P.sem("dK0"), P.sem("dK1")]
    dH = [P.sem("dH0"), P.sem("dH1")]
    dOUT = P.sem("dOUT")
    dP = P.sem("dP")
    wS = [P.sem("wS0"), P.sem("wS1")]
    cc = [P.sem(f"cc{i}") for i in range(4)]
    pe, sc, dv = P.sem("pe"), P.sem("sc"), P.sem("dv")
    P.auto_chain = {"dv": dv, "sc": sc}

    with ExitStack() as stk:
        en = stk.enter_context
        resid = en(nc.sbuf_tensor("resid", [128, NDK * 1024], F32))
        act = en(nc.sbuf_tensor("act", [128, NDK * 1024], BF16))
        arena = en(nc.sbuf_tensor("arena", [128, HALF * 1024], BF16))
        wa = en(nc.sbuf_tensor("wa", [128, 4 * 2048], BF16))
        wb = en(nc.sbuf_tensor("wb", [128, 2 * 1408], BF16))
        wsm = en(nc.sbuf_tensor("wsm", [128, 6 * 128], BF16))
        lnc = en(nc.sbuf_tensor("lnc", [128, 2048], BF16))
        sf = en(nc.sbuf_tensor("sf", [128, 2560], F32))
        sb = en(nc.sbuf_tensor("sb", [128, 10240], BF16))
        rowz = en(nc.sbuf_tensor("rowz", [1, 4], F32))
        smalls = en(nc.sbuf_tensor("smalls", [128, 64], F32))
        pack = en(nc.sbuf_tensor("pack", [128, 32], F32))
        trsb = en(nc.sbuf_tensor("trsb", [32, 128], F32))
        consts = en(nc.sbuf_tensor("consts", [128, 48], F32))
        cbf = en(nc.sbuf_tensor("cbf", [128, 16], BF16))
        crow = en(nc.sbuf_tensor("crow", [1, 192], F32))
        gstage = en(nc.sbuf_tensor("gstage", [1, 8], F32))
        kstg = en(nc.sbuf_tensor("kstg", [128, 2 * 512], BF16))
        gwtile = en(nc.sbuf_tensor("gwtile", [128, 128], BF16))
        identsb = en(nc.sbuf_tensor("identsb", [128, 128], F32))
        ps = [en(nc.psum_tensor(f"ps{i}", [128, 512], F32)) for i in range(4)]
        psQ = [en(nc.psum_tensor(f"psQ{i}", [128, 512], F32)) for i in range(4)]
        for s in P.sems:
            s.h = en(nc.semaphore(s.name))
        block = en(nc.Block())

        def rtile(dc, ts=None):
            a = resid[:, dc * 1024:(dc + 1) * 1024]
            return a if ts is None else a[:, ts * TS:(ts + 1) * TS]

        def ytile(dc, ts=None):
            a = act[:, dc * 1024:(dc + 1) * 1024]
            return a if ts is None else a[:, ts * TS:(ts + 1) * TS]

        ecol = consts[:, 0:1]
        iotasb = consts[:, 8:16]
        bq_s = consts[:, 16:32]
        bk_s = consts[:, 32:48]
        ones_cb = cbf[:, 0:1]
        ones_cf = consts[:, 1:2]
        ones_rf = crow[:, 0:128]
        ones_rb = crow[:, 128:192].bitcast(BF16)
        m0A, m1A = smalls[:, 0:8], smalls[:, 8:16]
        e0A, e1A = smalls[:, 16:24], smalls[:, 24:32]
        eqs, tmp8, lmt = smalls[:, 32:40], smalls[:, 40:48], smalls[:, 48:56]
        s01 = smalls[:, 56:58]
        s01r = smalls[:, 58:60].bitcast(F32R)
        rzb = smalls[:, 60:62]

        row_mean = sb[0:1, 0:2048].bitcast(F32)
        row_var = sb[0:1, 2048:4096].bitcast(F32)
        rstd_hi = sb[0:1, 4096:5120]
        rstd_lo = sb[0:1, 5120:6144]
        bneg_hi = sb[0:1, 6144:7168]
        bneg_lo = sb[0:1, 7168:8192]
        w_hi = [sb[0:1, 0:1024], sb[0:1, 2048:3072]]
        w_lo = [sb[0:1, 1024:2048], sb[0:1, 3072:4096]]
        row_wf = [sb[0:1, 4096:6144].bitcast(F32), sb[0:1, 6144:8192].bitcast(F32)]
        row_idx = [sb[0:1, 8192:9216], sb[0:1, 9216:10240]]
        linv_f = sb[0:1, 4096:6144].bitcast(F32)
        linv_hi = sb[0:1, 6144:7168]
        linv_lo = sb[0:1, 7168:8192]
        row_bv = sb[0:1, 8192:10240]

        rstd_b = sf[:, 0:1024]
        bneg_b = sf[:, 1024:2048]
        bA = sf[:, 0:512]
        bB = sf[:, 512:1024]
        s1t = sf[:, 1024:1536]
        t2t = sf[:, 1536:2048]
        siluX = sf[:, 2048:2560]
        bv_b = sf[:, 0:2048]
        linv_b = sf[:, 0:1024]

        def rho(i, ts):
            o = (i * 2 + ts) * 512
            return sb[:, o:o + 512]

        def crowb(i, ts):
            o = 4096 + (i * 2 + ts) * 512
            return sb[:, o:o + 512]

        def Mm(i, ts):
            o = 6144 + (i * 2 + ts) * 512
            return sb[:, o:o + 512]

        def probsT(buf):
            return sb[:, buf * 1024:(buf + 1) * 1024]
        amask_sb = sb[:, 2048:4096]
        cht = [sb[:, 8192 + i * 512:8192 + (i + 1) * 512] for i in range(4)]

        def mixed(k, ts):
            o = k * 1024 + ts * 512
            return arena[:, o:o + 512]
        q_t = [arena[:, i * 1024:(i + 1) * 1024] for i in range(2)]
        k_t = [arena[:, 2048 + i * 2048:2048 + (i + 1) * 2048] for i in range(2)]
        v_t = [arena[:, 6144 + i * 2048:6144 + (i + 1) * 2048] for i in range(2)]

        def wtA(slot, buf):
            o = (slot * 2 + buf) * 2048
            return wa[:, o:o + 2048]

        def wtP(buf):
            return wa[:, buf * 2048:buf * 2048 + 2048]

        def wtB(buf):
            return wb[:, buf * 1408:buf * 1408 + 1408]

        def wvt(buf):
            return wb[:, buf * 640:buf * 640 + 512]

        def btl(slot, buf):
            o = (slot * 2 + buf) * 128
            return wsm[:, o:o + 128]

        upA_sb = lnc[:, 0:2048]
        dnB_sb = lnc[:, 0:2048]
        xsq = [lnc[:, 0:1024]] * 2
        xbf = [lnc[:, 1024:2048]] * 2

        ef = P.op
        S = {"y": 0}

        ef('sy', lambda e: e.dma_start(out=_t3(resid[:, :], T), in_=_tl(xs)), inc=(dA, 16))
        mx = dA.n
        ef('sy', lambda e: e.dma_start(out=identsb[:, :], in_=identf), inc=(dA, 16))
        ef('sy', lambda e: e.dma_start(out=consts[:, 0:1], in_=ecol_d), inc=(dA, 16))
        ef('sy', lambda e: e.dma_start(out=consts[:, 8:16], in_=iota8_d), inc=(dA, 16))
        ef('sy', lambda e: e.dma_start(out=consts[:, 16:32], in_=bqd), inc=(dA, 16))
        ef('sy', lambda e: e.dma_start(out=consts[:, 32:48], in_=bkd), inc=(dA, 16))
        mca = dA.n
        mx = dA.n
        ef('gp', lambda e: e.dma_start(out=ones_cb, in_=ones_c), inc=(dO, 16))
        ef('gp', lambda e: e.dma_start(out=ones_rb, in_=ones_r), inc=(dO, 16))
        ef('sy', lambda e: e.dma_start(out=ones_rf, in_=ones_r), inc=(dA, 16))
        ef('sy', lambda e: e.dma_start(out=ones_cf, in_=ones_c), inc=(dA, 16))
        ef('gp', lambda e: e.dma_start(out=gwtile[:, :], in_=gwt), inc=(dO, 16))
        mcg = dO.n

        def layer_norm(base_pe, base_dv, base_sc, prev_dc=None):
            mson = [0] * NDK
            psw = max(prev_dc[NDK - 1], prev_dc[NDK - 2]) if prev_dc is not None else base_dv
            for dc in range(NDK):
                b = dc % 2
                wpe = max(mson[dc - 1] if dc >= 1 else 0, base_pe)
                wdc = prev_dc[dc] if prev_dc is not None else base_dv
                ef('dv', lambda e, dc=dc, b=b: e.tensor_tensor(xsq[b], rtile(dc), rtile(dc), ALU.mult),
                   waits=[(dA, dA.n), (pe, wpe), (dv, wdc)], inc=(dv, 1))
                mdv = dv.n
                ef('sc', lambda e, dc=dc, b=b: e.copy(xbf[b], rtile(dc)),
                   waits=[(dA, dA.n), (pe, wpe), (dv, wdc), (sc, base_sc)], inc=(sc, 1))
                msc = sc.n
                for h2 in range(2):
                    tgt = ps[h2]
                    ef('pe', lambda e, b=b, h2=h2, tgt=tgt, dc=dc: e.matmul(
                        tgt[0:1, :], ones_cb, xsq[b][:, h2 * 512:(h2 + 1) * 512],
                        start=(dc == 0), stop=(dc == NDK - 1)),
                       waits=[(dv, mdv), (dO, mcg), (dv, psw), (sc, base_sc)])
                    tgt2 = ps[2 + h2]
                    mson[dc] = ef('pe', lambda e, b=b, h2=h2, tgt2=tgt2, dc=dc: e.matmul(
                        tgt2[0:1, :], ones_cb, xbf[b][:, h2 * 512:(h2 + 1) * 512],
                        start=(dc == 0), stop=(dc == NDK - 1)),
                       waits=[(sc, msc)], inc=(pe, 1))
            m_stats = pe.n
            S["lnstats"] = m_stats
            ef('sc', lambda e: e.activation(row_mean[:, 0:512], ps[2][0:1, :], AF.Copy, scale=1.0 / D),
               waits=[(pe, m_stats)], inc=(sc, 1))
            ef('sc', lambda e: e.activation(row_mean[:, 512:1024], ps[3][0:1, :], AF.Copy, scale=1.0 / D), inc=(sc, 1))
            ef('sc', lambda e: e.activation(row_var[:, 0:512], ps[0][0:1, :], AF.Copy, scale=1.0 / D), inc=(sc, 1))
            ef('sc', lambda e: e.activation(row_var[:, 512:1024], ps[1][0:1, :], AF.Copy, scale=1.0 / D), inc=(sc, 1))
            mrow = sc.n
            ef('dv', lambda e: e.tensor_tensor(ps[0][0:1, :], row_mean[:, 0:512], row_mean[:, 0:512], ALU.mult),
               waits=[(sc, mrow)], inc=(dv, 1))
            ef('dv', lambda e: e.tensor_tensor(ps[1][0:1, :], row_mean[:, 512:1024], row_mean[:, 512:1024], ALU.mult), inc=(dv, 1))
            ef('dv', lambda e: e.tensor_tensor(row_var[:, 0:512], row_var[:, 0:512], ps[0][0:1, :], ALU.subtract), inc=(dv, 1))
            ef('dv', lambda e: e.tensor_tensor(row_var[:, 512:1024], row_var[:, 512:1024], ps[1][0:1, :], ALU.subtract), inc=(dv, 1))
            mv = dv.n
            ef('dv', lambda e: e.tensor_single_scalar(row_var[:, :], row_var[:, :], 1e-5, ALU.add),
               waits=[(dv, mv)], inc=(dv, 1))
            ef('sc', lambda e: e.activation(row_var[:, :], row_var[:, :], AF.Sqrt),
               waits=[(dv, dv.n)], inc=(sc, 1))
            ef('dv', lambda e: e.reciprocal(row_var[:, :], row_var[:, :]), waits=[(sc, sc.n)], inc=(dv, 1))
            ef('dv', lambda e: e.scalar_tensor_tensor(row_mean[:, :], row_mean[:, :], -1.0,
                                                      row_var[:, :], ALU.mult, ALU.mult), inc=(dv, 1))
            mrs = dv.n
            for h2 in range(2):
                ef('pe', lambda e, h2=h2: e.matmul(ps[h2][:, :], ones_rf,
                                                   row_var[:, h2 * 512:(h2 + 1) * 512],
                                                   start=True, stop=True),
                   waits=[(dv, mrs), (dA, dA.n)], inc=(pe, 1))
                ef('pe', lambda e, h2=h2: e.matmul(ps[2 + h2][:, :], ones_rf,
                                                   row_mean[:, h2 * 512:(h2 + 1) * 512],
                                                   start=True, stop=True), inc=(pe, 1))
            mbc = pe.n
            ef('sc', lambda e: e.copy(rstd_b[:, 0:512], ps[0][:, :]), waits=[(pe, mbc)], inc=(sc, 1))
            ef('sc', lambda e: e.copy(rstd_b[:, 512:1024], ps[1][:, :]), inc=(sc, 1))
            ef('sc', lambda e: e.copy(bneg_b[:, 0:512], ps[2][:, :]), inc=(sc, 1))
            ef('sc', lambda e: e.copy(bneg_b[:, 512:1024], ps[3][:, :]), inc=(sc, 1))
            mbce = sc.n
            mnorm = [0] * NDK
            for dc in range(NDK):
                pa, pb = ps[2 * (dc % 2)], ps[1 + 2 * (dc % 2)]
                wv_ = max(mnorm[dc - 2] if dc >= 2 else 0, 0)
                m = 0
                for h2 in range(2):
                    tgt = pa if h2 == 0 else pb
                    ef('dv', lambda e, dc=dc, h2=h2, tgt=tgt: e.tensor_tensor(
                        tgt[:, :], rtile(dc)[:, h2 * 512:(h2 + 1) * 512],
                        rstd_b[:, h2 * 512:(h2 + 1) * 512], ALU.mult),
                       waits=[(sc, mbce), (dv, wv_)])
                    m = ef('dv', lambda e, dc=dc, h2=h2, tgt=tgt: e.tensor_tensor(
                        ytile(dc)[:, h2 * 512:(h2 + 1) * 512], tgt[:, :],
                        bneg_b[:, h2 * 512:(h2 + 1) * 512], ALU.add), inc=(dv, 1))
                mnorm[dc] = m
            S["y"] = dv.n
            return dv.n

        def gating(call):
            my = S["y"]
            mls = [0] * NTC
            for tc in range(NTC):
                Lp = ps[tc % 2]
                wdv = mls[tc - 2] if tc >= 2 else 0
                for dk in range(NDK):
                    ef('pe', lambda e, tc=tc, dk=dk, Lp=Lp: e.matmul(
                        Lp[:, 0:8], ytile(dk)[:, tc * 128:(tc + 1) * 128],
                        gwtile[:, dk * 8:(dk + 1) * 8],
                        start=(dk == 0), stop=(dk == NDK - 1)),
                       waits=[(dv, max(my, wdv)), (dO, mcg)],
                       inc=(pe, 1) if dk == NDK - 1 else None)
                mL = pe.n
                ef('dv', lambda e, tc=tc, Lp=Lp: e.tensor_reduce(m0A[:, tc:tc + 1], Lp[:, 0:8], AX.X, ALU.max),
                   waits=[(pe, mL), (dA, dA.n)], inc=(dv, 1))
                ef('sc', lambda e, tc=tc: e.activation(e0A[:, tc:tc + 1], m0A[:, tc:tc + 1], AF.Exp),
                   waits=[(dv, dv.n)], inc=(sc, 1))
                ef('dv', lambda e, tc=tc, Lp=Lp: e.tensor_single_scalar(eqs, Lp[:, 0:8], m0A[:, tc:tc + 1], ALU.is_equal))
                ef('dv', lambda e: e.scalar_tensor_tensor(tmp8, eqs, -1000.0, iotasb, ALU.mult, ALU.add))
                ef('dv', lambda e, tc=tc: e.tensor_reduce(pack[:, tc:tc + 1], tmp8, AX.X, ALU.min))
                ef('dv', lambda e, tc=tc: e.tensor_single_scalar(pack[:, tc:tc + 1], pack[:, tc:tc + 1], 1000.0, ALU.add))
                ef('dv', lambda e, tc=tc: e.tensor_single_scalar(eqs, iotasb, pack[:, tc:tc + 1], ALU.is_equal))
                ef('dv', lambda e, Lp=Lp: e.scalar_tensor_tensor(lmt, eqs, -1e30, Lp[:, 0:8], ALU.mult, ALU.add),
                   inc=(dv, 1))
                mls[tc] = dv.n
                ef('dv', lambda e, tc=tc: e.tensor_reduce(m1A[:, tc:tc + 1], lmt, AX.X, ALU.max), inc=(dv, 1))
                ef('sc', lambda e, tc=tc: e.activation(e1A[:, tc:tc + 1], m1A[:, tc:tc + 1], AF.Exp),
                   waits=[(dv, dv.n)], inc=(sc, 1))
                ef('dv', lambda e, tc=tc: e.tensor_single_scalar(eqs, lmt, m1A[:, tc:tc + 1], ALU.is_equal))
                ef('dv', lambda e: e.scalar_tensor_tensor(tmp8, eqs, -1000.0, iotasb, ALU.mult, ALU.add))
                ef('dv', lambda e, tc=tc: e.tensor_reduce(pack[:, 8 + tc:9 + tc], tmp8, AX.X, ALU.min))
                ef('dv', lambda e, tc=tc: e.tensor_single_scalar(pack[:, 8 + tc:9 + tc], pack[:, 8 + tc:9 + tc],
                                                                 1000.0, ALU.add), inc=(dv, 1))
            msce = sc.n
            ef('dv', lambda e: e.tensor_reduce(s01[:, 0:1], e0A, AX.X, ALU.add),
               waits=[(sc, msce)], inc=(dv, 1))
            ef('dv', lambda e: e.tensor_reduce(s01[:, 1:2], e1A, AX.X, ALU.add), inc=(dv, 1))
            mS = dv.n
            ef('pe', lambda e: e.matmul(ps[2][0:1, 0:2], ones_cf, s01, start=True, stop=True),
               waits=[(dv, mS), (dA, dA.n)], inc=(pe, 1))
            ef('sc', lambda e: e.copy(gstage[:, 4:6], ps[2][0:1, 0:2]), waits=[(pe, pe.n)], inc=(sc, 1))
            ef('sy', lambda e, call=call: e.dma_start(out=gin[call], in_=gstage[:, 4:6]),
               waits=[(sc, sc.n)], inc=(dA, 16))
            mgin = dA.n
            ef('gp', lambda e, call=call: e.collective_compute(
                "AllGather", ALU.bypass, replica_groups=[[0, 1], [2, 3], [4, 5], [6, 7]],
                ins=[gin[call].opt()], outs=[gout[call].opt()]),
               waits=[(dA, mgin)], inc=(cc[call], 1))
            ef('sy', lambda e, call=call: e.dma_start(out=gstage[:, 0:2], in_=gout[call][0:1, :]),
               waits=[(cc[call], 1)], inc=(dA, 16))
            ef('sy', lambda e, call=call: e.dma_start(out=gstage[:, 2:4], in_=gout[call][1:2, :]),
               inc=(dA, 16))
            mgz = dA.n
            ef('dv', lambda e: e.tensor_tensor(gstage[:, 6:8], gstage[:, 0:2], gstage[:, 2:4], ALU.add),
               waits=[(dA, mgz)], inc=(dv, 1))
            ef('dv', lambda e: e.reciprocal(gstage[:, 6:8], gstage[:, 6:8]), inc=(dv, 1))
            mrz = dv.n
            ef('pe', lambda e: e.matmul(ps[3][:, 0:2], ones_rf, gstage[:, 6:8], start=True, stop=True),
               waits=[(dv, mrz)], inc=(pe, 1))
            ef('sc', lambda e: e.copy(rzb, ps[3][:, 0:2]), waits=[(pe, pe.n)], inc=(sc, 1))
            ef('dv', lambda e: e.tensor_single_scalar(pack[:, 16:24], e0A, rzb[:, 0:1], ALU.mult),
               waits=[(sc, sc.n)])
            ef('dv', lambda e: e.tensor_single_scalar(pack[:, 24:32], e1A, rzb[:, 1:2], ALU.mult),
               inc=(dv, 1))
            mpk = dv.n
            ef('pe', lambda e: e.transpose(ps[2][0:32, 0:128], pack[:, :], identsb[:, :]),
               waits=[(dv, mpk)], inc=(pe, 1))
            ef('sc', lambda e: e.copy(trsb[:, :], ps[2][0:32, 0:128]), waits=[(pe, pe.n)], inc=(sc, 1))
            ef('sy', lambda e: e.dma_start(out=trd, in_=trsb[:, :]), waits=[(sc, sc.n)], inc=(dA, 16))
            mtr = dA.n
            ef('sy', lambda e: e.dma_start(out=row_wf[0],
                                           in_=trd[16:24, :].rearrange("a b -> (a b)").unsqueeze(0)),
               waits=[(dA, mtr)], inc=(dA, 16))
            ef('sy', lambda e: e.dma_start(out=row_wf[1],
                                           in_=trd[24:32, :].rearrange("a b -> (a b)").unsqueeze(0)),
               inc=(dA, 16))
            mwf = dA.n
            for ts in range(2):
                for wi in range(2):
                    tgt = ps[2 + wi]
                    ef('pe', lambda e, ts=ts, wi=wi, tgt=tgt: e.matmul(
                        tgt[:, :], ones_rf, row_wf[wi][:, ts * 512:(ts + 1) * 512],
                        start=True, stop=True),
                       waits=[(dA, mwf), (sc, sc.n)], inc=(pe, 1))
                mcb = pe.n
                ef('sc', lambda e, ts=ts: e.copy(crowb(0, ts), ps[2][:, :]), waits=[(pe, mcb)], inc=(sc, 1))
                ef('sc', lambda e, ts=ts: e.copy(crowb(1, ts), ps[3][:, :]), inc=(sc, 1))
            ef('gp', lambda e: e.dma_start(out=row_idx[0],
                                           in_=trd[0:8, :].rearrange("a b -> (a b)").unsqueeze(0)),
               waits=[(dA, mtr)], inc=(dO, 16))
            ef('gp', lambda e: e.dma_start(out=row_idx[1],
                                           in_=trd[8:16, :].rearrange("a b -> (a b)").unsqueeze(0)),
               inc=(dO, 16))
            mrows = dO.n
            mmask_prev_dv = 0
            for ts in range(2):
                ef('pe', lambda e, ts=ts: e.matmul(ps[0][:, :], ones_rb,
                                                   row_idx[0][:, ts * 512:(ts + 1) * 512],
                                                   start=True, stop=True),
                   waits=[(dO, mrows), (dv, mmask_prev_dv)], inc=(pe, 1))
                ef('pe', lambda e, ts=ts: e.matmul(ps[1][:, :], ones_rb,
                                                   row_idx[1][:, ts * 512:(ts + 1) * 512],
                                                   start=True, stop=True), inc=(pe, 1))
                mb = pe.n
                ef('dv', lambda e, ts=ts: e.tensor_scalar(Mm(0, ts), ps[0][:, :], ecol, 2.0,
                                                          ALU.is_equal, ALU.mult),
                   waits=[(pe, mb)], inc=(dv, 1))
                ef('dv', lambda e, ts=ts: e.tensor_scalar(Mm(1, ts), ps[1][:, :], ecol, 2.0,
                                                          ALU.is_equal, ALU.mult), inc=(dv, 1))
                mmask_prev_dv = dv.n
            return dv.n, sc.n

        def ffn(call):
            base_pe, base_dv, base_sc = pe.n, dv.n, sc.n
            layer_norm(base_pe, base_dv, base_sc, S.get("resid_dc"))
            mr_done = [0]
            for w_i, Ad in enumerate((upAT, gateAT)):
                wa_pe = S["lnstats"] if w_i == 0 else pe.n
                ef('gp', lambda e, Ad=Ad: e.dma_start(out=_t3(upA_sb, 128), in_=_tl(Ad)),
                   waits=[(pe, wa_pe)], inc=(dP, 16))
                mA = dP.n
                for ts in range(2):
                    tgt = psQ[2 * w_i + ts]
                    for dk in range(NDK):
                        ef('pe', lambda e, ts=ts, dk=dk, tgt=tgt: e.matmul(
                            tgt[:, :], upA_sb[:, dk * 128:(dk + 1) * 128], ytile(dk, ts),
                            start=(dk == 0), stop=(dk == NDK - 1)),
                           waits=[(dP, mA), (dv, S["y"]), (dv, base_dv)],
                           inc=(pe, 1) if dk == NDK - 1 else None)
            mr_done[0] = pe.n
            mmask_dv, mmask_sc = gating(call)

            for w_i in range(2):
                for ts in range(2):
                    tgt = psQ[2 * w_i + ts]
                    ef('dv', lambda e, ts=ts, w_i=w_i, tgt=tgt: e.tensor_tensor(
                        rho(2 * w_i, ts), tgt[:, :], Mm(0, ts), ALU.mult),
                       waits=[(pe, mr_done[0]), (dv, mmask_dv)], inc=(dv, 1))
                    ef('dv', lambda e, ts=ts, tgt=tgt: e.tensor_tensor(
                        cht[0], tgt[:, :], Mm(1, ts), ALU.mult), inc=(dv, 1))
                    ef('dv', lambda e, ts=ts, w_i=w_i: e.tensor_tensor(
                        rho(2 * w_i + 1, ts), cht[0], rho(2 * w_i, ts), ALU.subtract), inc=(dv, 1))
            mrho = dv.n

            wload = [0] * NHC
            wfree = [0] * NHC

            wload_s = [0] * NHC

            def load_hc(hc):
                wpe = max(wfree[hc - 2] if hc >= 2 else 0, base_pe)
                sem = wA[hc % 2]
                ef('gp', lambda e, hc=hc: e.dma_start(
                    out=_t3(wtA(0, hc % 2), 128), in_=_tl(wupT[:, hc * 128:(hc + 1) * 128])),
                   waits=[(pe, wpe)], inc=(sem, 16))
                ef('gp', lambda e, hc=hc: e.dma_start(
                    out=_t3(wtA(1, hc % 2), 128), in_=_tl(wgateT[:, hc * 128:(hc + 1) * 128])),
                   inc=(sem, 16))
                wload[hc] = sem.n

            def load_small(hc):
                wpe = max(wfree_q[hc - 2] if hc >= 2 else 0, base_pe)
                sem = wS[hc % 2]
                ef('gp', lambda e, hc=hc: e.dma_start(
                    out=btl(0, hc % 2), in_=upBT[:, hc * 128:(hc + 1) * 128]),
                   waits=[(pe, wpe)], inc=(sem, 16))
                ef('gp', lambda e, hc=hc: e.dma_start(
                    out=btl(1, hc % 2), in_=gateBT[:, hc * 128:(hc + 1) * 128]), inc=(sem, 16))
                ef('gp', lambda e, hc=hc: e.dma_start(
                    out=_t3(btl(2, hc % 2), 128), in_=_tl(dnAT[hc * 128:(hc + 1) * 128, :])), inc=(sem, 16))
                wload_s[hc] = sem.n

            load_hc(0)
            load_hc(1)
            load_small(0)
            load_small(1)
            slA_free = [0, 0]
            slB_free = [0, 0]
            cht_free = [0, 0]
            pend_q = []
            wfree_q = [0] * NHC
            silu0_free = [0]
            silu1_free = [0]
            bA_free = [0]
            bB_free = [0]
            mdnb = [0]
            mrho2 = [0]
            bfree = [0] * NDK
            bfree2 = [0] * 32
            psfree_dv = [0]

            for half in range(2):
                bload2 = [0] * 32
                bfree2_prev = list(bfree2) if half == 1 else [0] * 32

                def load_b(dc, pc, half=half, bload2=bload2, bfree2_prev=bfree2_prev):
                    idx = 2 * dc + pc
                    prev = bfree2[idx - 2] if idx >= 2 else (bfree2_prev[idx] if half == 1 else 0)
                    ef('gp', lambda e, dc=dc, pc=pc, half=half: e.dma_start(
                        out=_t3(wtB((2 * dc + pc) % 2), 128),
                        in_=_tl(wdownT[half * 2816 + pc * 1408:half * 2816 + (pc + 1) * 1408,
                                       dc * 128:(dc + 1) * 128])),
                       waits=[(pe, max(prev, base_pe)), (dv, base_dv)], inc=(wB[idx % 2], 16))
                    bload2[idx] = wB[idx % 2].n

                for k in range(HALF):
                    hc = half * HALF + k
                    par = hc % 2
                    if k == HALF - 3:
                        load_b(0, 0)
                        load_b(0, 1)
                    for ts in range(2):
                        pA, pB, pDu, pDg = ps[0], ps[1], ps[2], ps[3]
                        for dk in range(NDK):
                            ef('pe', lambda e, ts=ts, dk=dk, par=par: e.matmul(
                                pA[:, :], wtA(0, par)[:, dk * 128:(dk + 1) * 128], ytile(dk, ts),
                                start=(dk == 0), stop=False),
                               waits=([(wA[par], wload[hc]), (sc, slA_free[0]), (dv, mmask_dv),
                                       (dv, psfree_dv[0])]
                                      if dk == 0 else []))
                        ef('pe', lambda e, ts=ts, par=par: e.matmul(
                            pA[:, :], btl(0, par), rho(0, ts), start=False, stop=True),
                           waits=[(dv, mrho), (wS[par], wload_s[hc])], inc=(pe, 1))
                        mAup = pe.n
                        for dk in range(NDK):
                            ef('pe', lambda e, ts=ts, dk=dk, par=par: e.matmul(
                                pB[:, :], wtA(1, par)[:, dk * 128:(dk + 1) * 128], ytile(dk, ts),
                                start=(dk == 0), stop=False),
                               waits=([(sc, slA_free[1])] if dk == 0 else []))
                        ef('pe', lambda e, ts=ts, par=par: e.matmul(
                            pB[:, :], btl(1, par), rho(2, ts), start=False, stop=True),
                           inc=(pe, 1))
                        mAg = pe.n
                        ef('pe', lambda e, ts=ts, par=par: e.matmul(
                            pDu[:, :], btl(0, par), rho(1, ts), start=True, stop=True),
                           waits=[(dv, slB_free[0]), (dv, psfree_dv[0]), (sc, mmask_sc)], inc=(pe, 1))
                        mDu = pe.n
                        ef('pe', lambda e, ts=ts, par=par: e.matmul(
                            pDg[:, :], btl(1, par), rho(3, ts), start=True, stop=True),
                           waits=[(dv, slB_free[1]), (dv, psfree_dv[0]), (sc, mmask_sc)], inc=(pe, 1))
                        mDg = pe.n
                        ef('sc', lambda e: e.copy(bA, pA[:, :]),
                           waits=[(pe, mAup), (dv, bA_free[0])], inc=(sc, 1))
                        slA_free[0] = sc.n
                        ef('sc', lambda e: e.copy(bB, pB[:, :]),
                           waits=[(pe, mAg), (dv, bB_free[0])], inc=(sc, 1))
                        slA_free[1] = sc.n
                        it2 = (hc * 2 + ts) % 2
                        ch2 = cht[2] if it2 == 0 else kstg[:, 0:512]
                        ch3 = cht[3] if it2 == 0 else kstg[:, 512:1024]
                        ef('sc', lambda e: e.activation(siluX, bA, AF.Silu),
                           waits=[(dv, silu0_free[0])], inc=(sc, 1))
                        ms0 = sc.n
                        ef('dv', lambda e: e.tensor_tensor(cht[0], siluX, bB, ALU.mult),
                           waits=[(sc, ms0), (pe, cht_free[it2])], inc=(dv, 1))
                        silu0_free[0] = dv.n
                        ef('dv', lambda e: e.tensor_tensor(s1t, bA, pDu[:, :], ALU.add),
                           waits=[(pe, mDu)], inc=(dv, 1))
                        slB_free[0] = dv.n
                        bA_free[0] = dv.n
                        ef('sc', lambda e: e.activation(siluX, s1t, AF.Silu),
                           waits=[(dv, dv.n)], inc=(sc, 1))
                        ms1 = sc.n
                        ef('dv', lambda e: e.tensor_tensor(t2t, bB, pDg[:, :], ALU.add),
                           waits=[(pe, mDg)], inc=(dv, 1))
                        slB_free[1] = dv.n
                        bB_free[0] = dv.n
                        ef('dv', lambda e: e.tensor_tensor(cht[1], siluX, t2t, ALU.mult),
                           waits=[(sc, ms1)], inc=(dv, 1))
                        silu0_free[0] = dv.n
                        silu1_free[0] = dv.n
                        ef('dv', lambda e, ts=ts, ch2=ch2: e.tensor_tensor(ch2, cht[0], crowb(0, ts), ALU.mult))
                        ef('dv', lambda e, ts=ts, ch3=ch3: e.tensor_tensor(ch3, cht[1], crowb(1, ts), ALU.mult))
                        ef('dv', lambda e, k=k, ts=ts, ch2=ch2, ch3=ch3: e.tensor_tensor(
                            mixed(k, ts), ch2, ch3, ALU.add), inc=(dv, 1))
                        mch = dv.n
                        def qmm(e, hc=hc, par=par, ts=ts, ch2=ch2):
                            return e.matmul(psQ[ts][:, :], btl(2, par), ch2,
                                            start=(hc == 0), stop=(hc == NHC - 1))
                        def qmm2(e, hc=hc, par=par, ts=ts, ch3=ch3):
                            return e.matmul(psQ[2 + ts][:, :], btl(2, par), ch3,
                                            start=(hc == 0), stop=(hc == NHC - 1))
                        pend_q.append((qmm, qmm2, mch, hc, it2))
                        if len(pend_q) > 1:
                            f1, f2, mw, phc, pit2 = pend_q.pop(0)
                            ef('pe', f1, waits=[(dv, mw)])
                            ef('pe', f2, inc=(pe, 1))
                            cht_free[pit2] = pe.n
                            wfree_q[phc] = pe.n
                            if ts == 0 and hc + 1 < NHC and hc >= 1:
                                load_small(hc + 1)
                        if ts == 1:
                            wfree[hc] = pe.n
                    if hc + 2 < NHC:
                        load_hc(hc + 2)

                if half == 1:
                    while pend_q:
                        f1, f2, mw, phc, pit2 = pend_q.pop(0)
                        ef('pe', f1, waits=[(dv, mw)])
                        ef('pe', f2, inc=(pe, 1))
                        cht_free[pit2] = pe.n
                        wfree_q[phc] = pe.n
                mmixed = dv.n
                if half == 1:
                    ef('gp', lambda e: e.dma_start(out=dnB_sb, in_=dnBT),
                       waits=[(pe, pe.n)], inc=(dP, 16))
                    mdnb[0] = dP.n
                    for ts in range(2):
                        ef('dv', lambda e, ts=ts: e.tensor_tensor(cht[0], psQ[ts][:, :], Mm(0, ts), ALU.mult),
                           waits=[(pe, max(cht_free[0], cht_free[1]))], inc=(dv, 1))
                        ef('dv', lambda e, ts=ts: e.tensor_tensor(cht[1], psQ[2 + ts][:, :], Mm(1, ts), ALU.mult),
                           inc=(dv, 1))
                        ef('dv', lambda e, ts=ts: e.tensor_tensor(rho(0, ts), cht[0], cht[1], ALU.add),
                           inc=(dv, 1))
                    mrho2[0] = dv.n
                for dc in range(NDK):
                    pf = [ps[(dc % 2) * 2], ps[(dc % 2) * 2 + 1]]
                    mpc = [0, 0]
                    for k in range(HALF):
                        pc = k // 11
                        kk = k % 11
                        buf = (2 * dc + pc) % 2
                        for ts in range(2):
                            isl = (half == 0 and k == HALF - 1 and ts == 1)
                            w0 = []
                            if kk == 0 and ts == 0:
                                w0 = [(wB[(2 * dc + pc) % 2], bload2[2 * dc + pc]), (dv, mmixed),
                                      (dv, bfree[dc - 2] if dc >= 2 else 0),
                                      (sc, slA_free[0]), (sc, slA_free[1]),
                                      (dv, slB_free[0]), (dv, slB_free[1]),
                                      (dv, silu1_free[0])]
                            ef('pe', lambda e, ts=ts, k=k, kk=kk, buf=buf, pX=pf[ts],
                               stp=(half == 0 and k == HALF - 1): e.matmul(
                                pX[:, :], wtB(buf)[:, kk * 128:(kk + 1) * 128], mixed(k, ts),
                                start=(k == 0), stop=stp),
                               waits=w0,
                               inc=(pe, 1) if (kk == 10 and ts == 1) or isl else None)
                        if kk == 10 and ts == 1:
                            mpc[pc] = pe.n
                        if k == 10:
                            bfree2[2 * dc] = pe.n
                            ni = 2 * dc + 2
                            if ni < 32:
                                load_b(ni // 2, ni % 2)
                        if k == HALF - 1:
                            bfree2[2 * dc + 1] = pe.n
                            ni = 2 * dc + 3
                            if ni < 32:
                                load_b(ni // 2, ni % 2)
                    for ts in range(2):
                        if half == 1:
                            ef('pe', lambda e, dc=dc, ts=ts, pX=pf[ts]: e.matmul(
                                pX[:, :], dnB_sb[:, dc * 128:(dc + 1) * 128], rho(0, ts),
                                start=False, stop=True),
                               waits=[(dP, mdnb[0]), (dv, mrho2[0])], inc=(pe, 1))
                        mpb = pe.n
                        ef('dv', lambda e, dc=dc, ts=ts, pX=pf[ts]: e.scalar_tensor_tensor(
                            rtile(dc, ts), pX[:, :], 0.5, rtile(dc, ts), ALU.mult, ALU.add),
                           waits=[(pe, mpb)], inc=(dv, 1))
                    bfree[dc] = dv.n
                psfree_dv[0] = dv.n
            S["resid_dc"] = list(bfree)

        def attention():
            base_pe, base_dv, base_sc = pe.n, dv.n, sc.n
            layer_norm(base_pe, base_dv, base_sc, S.get("resid_dc"))
            my = S["y"]
            ef('gp', lambda e: e.dma_start(out=row_bv, in_=bvd),
               waits=[(dv, base_dv)], inc=(dO, 16))
            mbv = dO.n
            for h4 in range(4):
                ef('pe', lambda e, h4=h4: e.matmul(ps[h4][:, :], ones_rb,
                                                   row_bv[:, h4 * 512:(h4 + 1) * 512],
                                                   start=True, stop=True),
                   waits=[(dO, mbv), (dv, my)], inc=(pe, 1))
            mbc2 = pe.n
            for h4 in range(4):
                ef('sc', lambda e, h4=h4: e.copy(bv_b[:, h4 * 512:(h4 + 1) * 512], ps[h4][:, :]),
                   waits=[(pe, mbc2)], inc=(sc, 1))
            mbve = sc.n
            pload = [0] * (2 * NDK)
            pfree = [0] * (2 * NDK)
            pfree_sc = [0] * (2 * NDK)
            kdma = {0: 0, 1: 0}
            kview = cin.rearrange("a (b c) -> (a b) c", b=2)

            def load_p(i):
                wpe = max(pfree[i - 2] if i >= 2 else 0, base_pe)
                Wd = wk_d if i < NDK else wq_d
                oc = i % NDK
                ef('gp', lambda e, Wd=Wd, oc=oc, i=i: e.dma_start(
                    out=_t3(wtP(i % 2), 128), in_=_tl(Wd[:, oc * 128:(oc + 1) * 128])),
                   waits=[(pe, wpe)], inc=(wP[i % 2], 16))
                pload[i] = wP[i % 2].n

            load_p(0)
            load_p(1)
            for i in range(NDK):
                oc = i % NDK
                isq = i >= NDK
                for ts in range(2):
                    pX = ps[(i % 2) * 2 + ts]
                    for dk in range(NDK):
                        ef('pe', lambda e, i=i, ts=ts, dk=dk, pX=pX: e.matmul(
                            pX[:, :], wtP(i % 2)[:, dk * 128:(dk + 1) * 128], ytile(dk, ts),
                            start=(dk == 0), stop=(dk == NDK - 1)),
                           waits=([(wP[i % 2], pload[i]), (dv, my), (sc, mbve),
                                   (dv, pfree_sc[i - 2] if i >= 2 else 0)] if dk == 0 else []),
                           inc=(pe, 1) if dk == NDK - 1 else None)
                    mproj = pe.n
                    bias = bq_s[:, oc:oc + 1] if isq else bk_s[:, oc:oc + 1]
                    scl = SM if isq else 1.0
                    ef('dv', lambda e, ts=ts, pX=pX, bias=bias, scl=scl: e.tensor_scalar(
                        kstg[:, ts * 512:(ts + 1) * 512], pX[:, :], bias, scl,
                        ALU.add, ALU.mult),
                       waits=[(pe, mproj), (dK[ts], kdma[ts])], inc=(dv, 1))
                    dst = (qdram[oc * 128:(oc + 1) * 128, ts * 512:(ts + 1) * 512] if isq
                           else kview[oc * 128:(oc + 1) * 128, ts * 512:(ts + 1) * 512])
                    ef('sy', lambda e, dst=dst, ts=ts: e.dma_start(
                        out=dst, in_=kstg[:, ts * 512:(ts + 1) * 512]),
                       waits=[(dv, dv.n)], inc=(dK[ts], 16))
                    kdma[ts] = dK[ts].n
                pfree[i] = pe.n
                pfree_sc[i] = dv.n
                if i + 2 < NDK:
                    load_p(i + 2)
                if i == NDK - 1:
                    ef('gp', lambda e: e.collective_compute(
                        "AllGather", ALU.bypass, replica_groups=[[0, 1], [2, 3], [4, 5], [6, 7]],
                        ins=[cin[0:1024, :].opt()], outs=[coutk.opt()]),
                       waits=[(dK[0], dK[0].n), (dK[1], dK[1].n)], inc=(cc[2], 1))
            vsem = [wV[0], wV[1], wS[0], wS[1]]
            vw_free = [base_pe] * 4
            vload = {}
            vseq = [(tcb, dvb, dk) for tcb in range(2) for dvb in range(4) for dk in range(NDK)]

            def loadv(i):
                tcb, dvb, dk = vseq[i]
                buf = i % 4
                ef('gp', lambda e, dvb=dvb, dk=dk, buf=buf: e.dma_start(
                    out=wvt(buf), in_=wv_d[dk * 128:(dk + 1) * 128, dvb * 512:(dvb + 1) * 512]),
                   waits=[(pe, vw_free[buf]), (dv, base_dv)], inc=(vsem[buf], 16))
                vload[i] = vsem[buf].n

            for i in range(4):
                loadv(i)
            vev_free = [base_dv] * 4
            for tcb in range(2):
                for dvb in range(4):
                    for dk in range(NDK):
                        i = (tcb * 4 + dvb) * NDK + dk
                        buf = i % 4
                        for j4 in range(4):
                            tc = tcb * 4 + j4
                            w0 = [(vsem[buf], vload[i])] if j4 == 0 else []
                            if dk == 0:
                                w0 = w0 + [(dv, vev_free[j4]), (dv, my)]
                            ef('pe', lambda e, tc=tc, dk=dk, j4=j4, buf=buf: e.matmul(
                                psQ[j4][:, :], ytile(dk)[:, tc * 128:(tc + 1) * 128], wvt(buf),
                                start=(dk == 0), stop=(dk == NDK - 1)),
                               waits=w0,
                               inc=(pe, 1) if j4 == 3 else None)
                        vw_free[buf] = pe.n
                        if i + 4 < len(vseq):
                            loadv(i + 4)
                    mvp = pe.n
                    for j4 in range(4):
                        tc = tcb * 4 + j4
                        sslot = j4 % 2
                        ef('dv', lambda e, j4=j4, dvb=dvb, sslot=sslot: e.tensor_tensor(
                            kstg[:, sslot * 512:sslot * 512 + 512], psQ[j4][:, :],
                            bv_b[:, dvb * 512:(dvb + 1) * 512], ALU.add),
                           waits=[(pe, mvp), (sc, mbve), (dK[sslot], kdma[sslot])], inc=(dv, 1))
                        vev_free[j4] = dv.n
                        ef('sy', lambda e, tc=tc, dvb=dvb, sslot=sslot: e.dma_start(
                            out=cin[1024 + tc * 128:1024 + (tc + 1) * 128, dvb * 512:(dvb + 1) * 512],
                            in_=kstg[:, sslot * 512:sslot * 512 + 512]),
                           waits=[(dv, dv.n)], inc=(dK[sslot], 16))
                        kdma[sslot] = dK[sslot].n
            load_p(NDK)
            load_p(NDK + 1)
            for i in range(NDK, 2 * NDK):
                oc = i % NDK
                isq = i >= NDK
                for ts in range(2):
                    pX = ps[(i % 2) * 2 + ts]
                    for dk in range(NDK):
                        ef('pe', lambda e, i=i, ts=ts, dk=dk, pX=pX: e.matmul(
                            pX[:, :], wtP(i % 2)[:, dk * 128:(dk + 1) * 128], ytile(dk, ts),
                            start=(dk == 0), stop=(dk == NDK - 1)),
                           waits=([(wP[i % 2], pload[i]), (dv, my), (sc, mbve),
                                   (dv, pfree_sc[i - 2] if i >= 2 else 0)] if dk == 0 else []),
                           inc=(pe, 1) if dk == NDK - 1 else None)
                    mproj = pe.n
                    bias = bq_s[:, oc:oc + 1] if isq else bk_s[:, oc:oc + 1]
                    scl = SM if isq else 1.0
                    ef('dv', lambda e, ts=ts, pX=pX, bias=bias, scl=scl: e.tensor_scalar(
                        kstg[:, ts * 512:(ts + 1) * 512], pX[:, :], bias, scl,
                        ALU.add, ALU.mult),
                       waits=[(pe, mproj), (dK[ts], kdma[ts])], inc=(dv, 1))
                    dst = (qdram[oc * 128:(oc + 1) * 128, ts * 512:(ts + 1) * 512] if isq
                           else kview[oc * 128:(oc + 1) * 128, ts * 512:(ts + 1) * 512])
                    ef('sy', lambda e, dst=dst, ts=ts: e.dma_start(
                        out=dst, in_=kstg[:, ts * 512:(ts + 1) * 512]),
                       waits=[(dv, dv.n)], inc=(dK[ts], 16))
                    kdma[ts] = dK[ts].n
                pfree[i] = pe.n
                pfree_sc[i] = dv.n
                if i + 2 < 2 * NDK:
                    load_p(i + 2)
            ef('gp', lambda e: e.collective_compute(
                "AllGather", ALU.bypass, replica_groups=[[0, 1], [2, 3], [4, 5], [6, 7]],
                ins=[cin[1024:2048, :].opt()], outs=[coutv.opt()]),
               waits=[(dK[0], dK[0].n), (dK[1], dK[1].n)], inc=(cc[3], 1))
            ef('sy', lambda e: e.dma_start(out=amask_sb, in_=amask_d),
               waits=[(dv, mrho_dead_dv[0])], inc=(dA, 16))
            mam = dA.n
            kvout = coutk.rearrange("a (b c) -> (a b) c", b=2)
            hload = [0] * NH
            hvl = [0, 0]
            hfree_q = [base_dv, base_dv]
            hfree_k = [base_pe, base_pe]
            hfree_v = [base_pe, base_pe]
            sco_free = [base_sc, base_sc]
            pb_free = [base_pe, base_pe]
            ctx_free = [base_dv]
            l_free = [base_sc]

            def load_head(h):
                buf = h % 2
                sem = dH[buf]
                ef('sy', lambda e, h=h, buf=buf: e.dma_start(
                    out=q_t[buf], in_=qdram[h * 128:(h + 1) * 128, :]),
                   waits=[(dv, hfree_q[buf])], inc=(sem, 16))
                for j in range(2):
                    src = kvout[2048 * j + h * 128:2048 * j + (h + 1) * 128, :]
                    dst = k_t[buf].rearrange("p (g c) -> p g c", c=128)[:, j::2, :]
                    ef('sy', lambda e, src=src, dst=dst: e.dma_start(
                        out=dst, in_=src.rearrange("p (g c) -> p g c", c=128)),
                       waits=[(cc[2], 1), (pe, hfree_k[buf])], inc=(sem, 16))
                vbuf = buf
                for j in range(2):
                    src = coutv[1024 * j:1024 * (j + 1), h * 128:(h + 1) * 128]
                    dst = v_t[vbuf].rearrange("p (g w) -> p g w", w=128)[:, j::2, :]
                    ef('sy', lambda e, src=src, dst=dst: e.dma_start(
                        out=dst, in_=src.rearrange("(g p) w -> p g w", p=128)),
                       waits=[(cc[3], 1), (pe, hfree_v[vbuf])], inc=(sem, 16))
                hvl[vbuf] = sem.n
                hload[h] = sem.n

            load_head(0)
            load_head(1)
            for h in range(NH):
                buf = h % 2
                vbuf = h % 2
                hp = h % 2
                mpv = 0
                pend_pv = []
                for kt in range(16):
                    s0 = 128 * (kt // 2)
                    pr = [ps[0], ps[1]] if kt % 2 == 0 else [ps[2], ps[3]]
                    segs = ([(0, s0, 512)] if s0 < 512 else []) + [(1, max(512, s0), 1024)]
                    for n_, (si, a, b_) in enumerate(segs):
                        ef('pe', lambda e, kt=kt, buf=buf, pr=pr, si=si, a=a, b_=b_: e.matmul(
                            pr[si][:, a - si * 512:b_ - si * 512],
                            k_t[buf][:, kt * 128:(kt + 1) * 128], q_t[buf][:, a:b_],
                            start=True, stop=True),
                           waits=([(dH[buf], hload[h]), (sc, sco_free[kt % 2])] if n_ == 0 else []),
                           inc=(pe, 1) if n_ == len(segs) - 1 else None)
                    msco = pe.n
                    for (si, a, b_) in segs:
                        ef('sc', lambda e, kt=kt, si=si, a=a, b_=b_, pr=pr: e.activation(
                            probsT(kt % 2)[:, a:b_], pr[si][:, a - si * 512:b_ - si * 512], AF.Exp),
                           waits=[(pe, max(msco, pb_free[kt % 2])), (dA, mam)], inc=(sc, 1))
                    msce2 = sc.n
                    sco_free[kt % 2] = msce2
                    ef('dv', lambda e, kt=kt, s0=s0: e.tensor_tensor(
                        probsT(kt % 2)[:, s0:s0 + 128], probsT(kt % 2)[:, s0:s0 + 128],
                        amask_sb[:, kt * 128:(kt + 1) * 128], ALU.mult),
                       waits=[(sc, msce2)], inc=(dv, 1))
                    mmk = dv.n

                    def mk_pv(kt=kt, segs=segs, vbuf=vbuf, hp=hp, mmk=mmk, h=h):
                        for n_, (si, a, b_) in enumerate(segs):
                            stp = (kt == (15 if si == 1 else 7))
                            ef('pe', lambda e, kt=kt, si=si, a=a, b_=b_, vbuf=vbuf, stp=stp: e.matmul(
                                psQ[si][:, a - si * 512:b_ - si * 512],
                                v_t[vbuf][:, kt * 128:(kt + 1) * 128],
                                probsT(kt % 2)[:, a:b_],
                                start=(kt == 0), stop=stp),
                               waits=([(dH[vbuf], hvl[vbuf]), (dv, mmk), (dv, ctx_free[0]),
                                       (sc, l_free[0])] if n_ == 0 else []))
                            ef('pe', lambda e, kt=kt, si=si, a=a, b_=b_, stp=stp: e.matmul(
                                psQ[2 + si][0:1, a - si * 512:b_ - si * 512], ones_cb,
                                probsT(kt % 2)[:, a:b_],
                                start=(kt == 0), stop=stp),
                               inc=(pe, 1) if n_ == len(segs) - 1 else None)
                        pb_free[kt % 2] = pe.n

                    pend_pv.append(mk_pv)
                    if len(pend_pv) > 1:
                        pend_pv.pop(0)()
                while pend_pv:
                    pend_pv.pop(0)()
                mpv = pe.n
                ef('dv', lambda e: e.reciprocal(linv_f[:, 0:512], psQ[2][0:1, :]),
                   waits=[(pe, mpv)], inc=(dv, 1))
                ef('dv', lambda e: e.reciprocal(linv_f[:, 512:1024], psQ[3][0:1, :]), inc=(dv, 1))
                ml = dv.n
                for si in range(2):
                    ef('pe', lambda e, si=si: e.matmul(psQ[2 + si][:, :], ones_rf,
                                                       linv_f[:, si * 512:(si + 1) * 512],
                                                       start=True, stop=True),
                       waits=[(dv, ml)], inc=(pe, 1))
                mlb = pe.n
                for si in range(2):
                    ef('sc', lambda e, si=si: e.copy(linv_b[:, si * 512:(si + 1) * 512], psQ[2 + si][:, :]),
                       waits=[(pe, mlb)], inc=(sc, 1))
                l_free[0] = sc.n
                for si in range(2):
                    ef('dv', lambda e, h=h, si=si: e.tensor_tensor(
                        ytile(h)[:, si * 512:(si + 1) * 512], psQ[si][:, :],
                        linv_b[:, si * 512:(si + 1) * 512], ALU.mult),
                       waits=[(sc, l_free[0])], inc=(dv, 1))
                ctx_free[0] = dv.n
                hfree_q[buf] = ctx_free[0]
                hfree_k[buf] = mpv
                hfree_v[vbuf] = mpv
                if h + 2 < NH:
                    load_head(h + 2)
            mattn = dv.n
            oload = [0] * NDK
            ofree = [0] * NDK

            def load_o(dc):
                ef('gp', lambda e, dc=dc: e.dma_start(
                    out=_t3(wtP(dc % 2), 128), in_=_tl(wo_d[:, dc * 128:(dc + 1) * 128])),
                   waits=[(pe, pfree[2 * NDK - 1]),
                          (dv, ofree[dc - 2] if dc >= 2 else 0)], inc=(wP[dc % 2], 16))
                oload[dc] = wP[dc % 2].n

            load_o(0)
            load_o(1)
            for dc in range(NDK):
                for ts in range(2):
                    pX = ps[(dc % 2) * 2 + ts]
                    for hh in range(NH):
                        ef('pe', lambda e, dc=dc, ts=ts, hh=hh, pX=pX: e.matmul(
                            pX[:, :], wtP(dc % 2)[:, hh * 128:(hh + 1) * 128], ytile(hh, ts),
                            start=(hh == 0), stop=(hh == NH - 1)),
                           waits=([(wP[dc % 2], oload[dc]), (dv, mattn), (sc, l_free[0]),
                                   (dv, ofree[dc - 2] if dc >= 2 else 0)] if hh == 0 else []),
                           inc=(pe, 1) if hh == NH - 1 else None)
                    mo = pe.n
                    ef('dv', lambda e, dc=dc, ts=ts, pX=pX: e.tensor_tensor(
                        rtile(dc, ts), pX[:, :], rtile(dc, ts), ALU.add),
                       waits=[(pe, mo)], inc=(dv, 1))
                ofree[dc] = dv.n
                if dc + 2 < NDK:
                    load_o(dc + 2)
            S["resid_dc"] = list(ofree)

        mrho_dead_dv = [0]
        ffn(0)
        mrho_dead_dv[0] = dv.n
        if STAGE >= 2:
            attention()
        if STAGE >= 3:
            ffn(1)

        ef('sy', lambda e: e.dma_start(out=_tl(out), in_=_t3(resid[:, :], T)),
           waits=[(dv, dv.n), (pe, pe.n), (sc, sc.n)], inc=(dOUT, 16))
        ef('sy', lambda e: e.nop(), waits=[(dOUT, dOUT.n)])

        @block.sync
        def _(e):
            P.emit('sy', e)

        @block.gpsimd
        def _(e):
            P.emit('gp', e)

        @block.tensor
        def _(e):
            P.emit('pe', e)

        @block.scalar
        def _(e):
            P.emit('sc', e)

        @block.vector
        def _(e):
            P.emit('dv', e)

    return nc


_NC_CACHE = {}


def _amask(j):
    m = np.zeros((128, 16 * 128), dtype=np.float32)
    for kt in range(16):
        m0 = kt // 2
        qt = 2 * m0 + j
        gk = kt * 128 + np.arange(128)[:, None]
        gq = qt * 128 + np.arange(128)[None, :]
        m[:, kt * 128:(kt + 1) * 128] = (gq >= gk).astype(np.float32)
    return m


def kernel(L0_attn_hidden_states, L0_attn_Wq, L0_attn_bq, L0_attn_Wk, L0_attn_bk,
           L0_attn_Wv, L0_attn_bv, L0_attn_Wo, L0_ffn_gate_w, L0_ffn_w_up,
           L0_ffn_w_gate, L0_ffn_w_down, L0_ffn_up_A, L0_ffn_up_B, L0_ffn_gate_A,
           L0_ffn_gate_B, L0_ffn_down_A, L0_ffn_down_B):
    x = np.ascontiguousarray(np.asarray(L0_attn_hidden_states, dtype=np.float32))
    B = x.shape[0]
    cfg = lambda a: np.ascontiguousarray(np.asarray(a, dtype=np.float32))

    gate_w = cfg(L0_ffn_gate_w)
    gwt = np.zeros((128, 128), dtype=np.float32)
    for dk in range(NDK):
        gwt[:, dk * 8:(dk + 1) * 8] = gate_w[:, dk * 128:(dk + 1) * 128].T
    common = {
        "wq": cfg(L0_attn_Wq), "wk": cfg(L0_attn_Wk), "wv": cfg(L0_attn_Wv),
        "wo": cfg(L0_attn_Wo),
        "bqd": cfg(np.asarray(L0_attn_bq, dtype=np.float32).reshape(NDK, 128).T),
        "bkd": cfg(np.asarray(L0_attn_bk, dtype=np.float32).reshape(NDK, 128).T),
        "bvd": cfg(np.asarray(L0_attn_bv, dtype=np.float32).reshape(1, D)),
        "wupT": cfg(np.asarray(L0_ffn_w_up).T),
        "wgateT": cfg(np.asarray(L0_ffn_w_gate).T),
        "wdownT": cfg(np.asarray(L0_ffn_w_down).T),
        "upAT": cfg(np.asarray(L0_ffn_up_A).reshape(128, D).T),
        "gateAT": cfg(np.asarray(L0_ffn_gate_A).reshape(128, D).T),
        "upBT": cfg(np.asarray(L0_ffn_up_B).transpose(0, 2, 1).reshape(128, H)),
        "gateBT": cfg(np.asarray(L0_ffn_gate_B).transpose(0, 2, 1).reshape(128, H)),
        "dnAT": cfg(np.asarray(L0_ffn_down_A).reshape(128, H).T),
        "dnBT": cfg(np.asarray(L0_ffn_down_B).transpose(0, 2, 1).reshape(128, D)),
        "gwt": gwt,
        "identf": np.eye(128, dtype=np.float32),
        "ones_r": np.ones((1, 128), dtype=np.float32),
        "ones_c": np.ones((128, 1), dtype=np.float32),
        "ecol_d": (np.arange(128, dtype=np.float32) // 16).reshape(128, 1),
        "iota8_d": np.broadcast_to(np.arange(8, dtype=np.float32), (128, 8)).copy(),
    }
    amasks = [_amask(0), _amask(1)]

    if "nc" not in _NC_CACHE:
        _NC_CACHE["nc"] = build_nc()
    nc = _NC_CACHE["nc"]

    in_maps = []
    for c in range(8):
        b, j = c // 2, c % 2
        xt = x[b].reshape(16, 128, D)[j::2].reshape(T, D).T
        m = dict(common)
        m["xs"] = np.ascontiguousarray(xt)
        m["amask_d"] = amasks[j].astype(ml_dtypes.bfloat16)
        in_maps.append(m)

    res = run_bass_kernel_spmd(nc, in_maps, core_ids=list(range(8)))

    y = np.empty_like(x)
    for c in range(8):
        b, j = c // 2, c % 2
        oc = np.asarray(res.results[c]["out"])
        yt = oc.T.reshape(NTC, 128, D)
        for i in range(NTC):
            y[b, (2 * i + j) * 128:(2 * i + j + 1) * 128, :] = yt[i]
    return y


# version 11
# speedup vs baseline: 1.5974x; 1.0250x over previous
import math
import numpy as np
import ml_dtypes
import concourse.bass as bass
import concourse.mybir as mybir
from concourse.bass_utils import run_bass_kernel_spmd
from contextlib import ExitStack

F32 = mybir.dt.float32
F32R = mybir.dt.float32r
BF16 = mybir.dt.bfloat16
AF = mybir.ActivationFunctionType
ALU = mybir.AluOpType
AX = mybir.AxisListType

D, H, NH, DH, T, E, R = 2048, 5632, 16, 128, 1024, 8, 16
NDK = D // 128
NHC = H // 128
HALF = NHC // 2
NTC = T // 128
TS = 512
SM = 1.0 / math.sqrt(DH)
STAGE = 3


class _Sem:
    __slots__ = ("name", "n", "h")
    def __init__(self, name):
        self.name, self.n, self.h = name, 0, None


class _B:
    def __init__(self):
        self.ops = []
        self.sems = []
        self._lw = {}
        self.auto_chain = {}

    def sem(self, name):
        s = _Sem(name)
        self.sems.append(s)
        return s

    def op(self, eng, fn, waits=(), inc=None):
        waits = list(waits)
        if eng in self.auto_chain:
            s = self.auto_chain[eng]
            waits.append((s, s.n))
            if inc is None:
                inc = (s, 1)
        w = []
        for (s, v) in waits:
            if v <= 0:
                continue
            k = (eng, s.name)
            if self._lw.get(k, -1) >= v:
                continue
            self._lw[k] = v
            w.append((s, v))
        if inc is not None:
            inc[0].n += inc[1]
        self.ops.append((eng, fn, w, inc))
        return inc[0].n if inc else None

    def emit(self, name, eng):
        for (e, fn, w, inc) in self.ops:
            if e != name:
                continue
            for (s, v) in w:
                eng.wait_ge(s.h, v)
            inst = fn(eng)
            if inc is not None:
                inst.then_inc(inc[0].h, inc[1])


def _tl(src2d):
    return src2d.rearrange("(a p) c -> p a c", p=128)


def _t3(dst2d, c):
    return dst2d.rearrange("p (a c) -> p a c", c=c)


def build_nc():
    nc = bass.Bass("TRN2", target_bir_lowering=False)

    xs = nc.dram_tensor("xs", [D, T], F32, kind="ExternalInput").ap()
    wq_d = nc.dram_tensor("wq", [D, D], F32, kind="ExternalInput").ap()
    wk_d = nc.dram_tensor("wk", [D, D], F32, kind="ExternalInput").ap()
    wv_d = nc.dram_tensor("wv", [D, D], F32, kind="ExternalInput").ap()
    wo_d = nc.dram_tensor("wo", [D, D], F32, kind="ExternalInput").ap()
    bqd = nc.dram_tensor("bqd", [128, NDK], F32, kind="ExternalInput").ap()
    bkd = nc.dram_tensor("bkd", [128, NDK], F32, kind="ExternalInput").ap()
    bvd = nc.dram_tensor("bvd", [1, D], F32, kind="ExternalInput").ap()
    wupT = nc.dram_tensor("wupT", [D, H], F32, kind="ExternalInput").ap()
    wgateT = nc.dram_tensor("wgateT", [D, H], F32, kind="ExternalInput").ap()
    wdownT = nc.dram_tensor("wdownT", [H, D], F32, kind="ExternalInput").ap()
    upAT = nc.dram_tensor("upAT", [D, 128], F32, kind="ExternalInput").ap()
    gateAT = nc.dram_tensor("gateAT", [D, 128], F32, kind="ExternalInput").ap()
    upBT = nc.dram_tensor("upBT", [128, H], F32, kind="ExternalInput").ap()
    gateBT = nc.dram_tensor("gateBT", [128, H], F32, kind="ExternalInput").ap()
    dnAT = nc.dram_tensor("dnAT", [H, 128], F32, kind="ExternalInput").ap()
    dnBT = nc.dram_tensor("dnBT", [128, D], F32, kind="ExternalInput").ap()
    gwt = nc.dram_tensor("gwt", [128, 128], F32, kind="ExternalInput").ap()
    identf = nc.dram_tensor("identf", [128, 128], F32, kind="ExternalInput").ap()
    ones_r = nc.dram_tensor("ones_r", [1, 128], F32, kind="ExternalInput").ap()
    ones_c = nc.dram_tensor("ones_c", [128, 1], F32, kind="ExternalInput").ap()
    ecol_d = nc.dram_tensor("ecol_d", [128, 1], F32, kind="ExternalInput").ap()
    iota8_d = nc.dram_tensor("iota8_d", [128, 8], F32, kind="ExternalInput").ap()
    amask_d = nc.dram_tensor("amask_d", [128, 16 * 128], BF16, kind="ExternalInput").ap()
    out = nc.dram_tensor("out", [D, T], F32, kind="ExternalOutput").ap()

    qdram = nc.dram_tensor("qdram", [D, T], BF16).ap()
    cin = nc.dram_tensor("cin", [2048, 2048], BF16).ap()
    coutk = nc.dram_tensor("coutk", [2048, 2048], BF16).ap()
    coutv = nc.dram_tensor("coutv", [2048, 2048], BF16).ap()
    trd = nc.dram_tensor("trd", [32, 128], F32).ap()
    gin = [nc.dram_tensor(f"gin{i}", [1, 2], F32).ap() for i in range(2)]
    gout = [nc.dram_tensor(f"gout{i}", [2, 2], F32).ap() for i in range(2)]

    P = _B()
    dA = P.sem("dA")
    dO = P.sem("dO")
    wA = [P.sem("wA0"), P.sem("wA1")]
    wB = [P.sem("wB0"), P.sem("wB1")]
    wP = [P.sem("wP0"), P.sem("wP1")]
    wV = [P.sem("wV0"), P.sem("wV1")]
    dK = [P.sem("dK0"), P.sem("dK1")]
    dH = [P.sem("dH0"), P.sem("dH1")]
    dOUT = P.sem("dOUT")
    dP = P.sem("dP")
    wS = [P.sem("wS0"), P.sem("wS1")]
    cc = [P.sem(f"cc{i}") for i in range(4)]
    dZ = P.sem("dZ")
    pe, sc, dv = P.sem("pe"), P.sem("sc"), P.sem("dv")
    P.auto_chain = {"dv": dv, "sc": sc}

    with ExitStack() as stk:
        en = stk.enter_context
        resid = en(nc.sbuf_tensor("resid", [128, NDK * 1024], F32))
        act = en(nc.sbuf_tensor("act", [128, NDK * 1024], BF16))
        arena = en(nc.sbuf_tensor("arena", [128, HALF * 1024], BF16))
        wa = en(nc.sbuf_tensor("wa", [128, 4 * 2048], BF16))
        wb = en(nc.sbuf_tensor("wb", [128, 2 * 1408], BF16))
        wsm = en(nc.sbuf_tensor("wsm", [128, 6 * 128], BF16))
        lnc = en(nc.sbuf_tensor("lnc", [128, 2048], BF16))
        sf = en(nc.sbuf_tensor("sf", [128, 2560], F32))
        sb = en(nc.sbuf_tensor("sb", [128, 10240], BF16))
        rowz = en(nc.sbuf_tensor("rowz", [1, 4], F32))
        smalls = en(nc.sbuf_tensor("smalls", [128, 64], F32))
        pack = en(nc.sbuf_tensor("pack", [128, 32], F32))
        trsb = en(nc.sbuf_tensor("trsb", [32, 128], F32))
        consts = en(nc.sbuf_tensor("consts", [128, 48], F32))
        cbf = en(nc.sbuf_tensor("cbf", [128, 16], BF16))
        crow = en(nc.sbuf_tensor("crow", [1, 192], F32))
        gstage = en(nc.sbuf_tensor("gstage", [1, 8], F32))
        kstg = en(nc.sbuf_tensor("kstg", [128, 2 * 512], BF16))
        gwtile = en(nc.sbuf_tensor("gwtile", [128, 128], BF16))
        identsb = en(nc.sbuf_tensor("identsb", [128, 128], F32))
        ps = [en(nc.psum_tensor(f"ps{i}", [128, 512], F32)) for i in range(4)]
        psQ = [en(nc.psum_tensor(f"psQ{i}", [128, 512], F32)) for i in range(4)]
        for s in P.sems:
            s.h = en(nc.semaphore(s.name))
        block = en(nc.Block())

        def rtile(dc, ts=None):
            a = resid[:, dc * 1024:(dc + 1) * 1024]
            return a if ts is None else a[:, ts * TS:(ts + 1) * TS]

        def ytile(dc, ts=None):
            a = act[:, dc * 1024:(dc + 1) * 1024]
            return a if ts is None else a[:, ts * TS:(ts + 1) * TS]

        ecol = consts[:, 0:1]
        iotasb = consts[:, 8:16]
        bq_s = consts[:, 16:32]
        bk_s = consts[:, 32:48]
        ones_cb = cbf[:, 0:1]
        ones_cf = consts[:, 1:2]
        ones_rf = crow[:, 0:128]
        ones_rb = crow[:, 128:192].bitcast(BF16)
        m0A, m1A = smalls[:, 0:8], smalls[:, 8:16]
        e0A, e1A = smalls[:, 16:24], smalls[:, 24:32]
        eqs, tmp8, lmt = smalls[:, 32:40], smalls[:, 40:48], smalls[:, 48:56]
        s01 = smalls[:, 56:58]
        s01r = smalls[:, 58:60].bitcast(F32R)
        rzb = smalls[:, 60:62]

        row_mean = sb[0:1, 0:2048].bitcast(F32)
        row_var = sb[0:1, 2048:4096].bitcast(F32)
        rstd_hi = sb[0:1, 4096:5120]
        rstd_lo = sb[0:1, 5120:6144]
        bneg_hi = sb[0:1, 6144:7168]
        bneg_lo = sb[0:1, 7168:8192]
        row_wf = [sf[0:1, 0:1024], sf[0:1, 1024:2048]]
        row_idx = [sb[0:1, 8192:9216], sb[0:1, 9216:10240]]
        linv_f = sb[0:1, 4096:6144].bitcast(F32)
        linv_hi = sb[0:1, 6144:7168]
        linv_lo = sb[0:1, 7168:8192]
        row_bv = sb[0:1, 8192:10240]

        rstd_b = sf[:, 0:1024]
        bneg_b = sf[:, 1024:2048]
        bA = sf[:, 0:512]
        bB = sf[:, 512:1024]
        s1t = sf[:, 1024:1536]
        t2t = sf[:, 1536:2048]
        siluX = sf[:, 2048:2560]
        bv_b = sb[:, 4096:8192].bitcast(F32)
        linv_b = sf[:, 0:1024]

        def rho(i, ts):
            o = (i * 2 + ts) * 512
            return sb[:, o:o + 512]

        def crowb(i, ts):
            o = 4096 + (i * 2 + ts) * 512
            return sb[:, o:o + 512]

        def Mm(i, ts):
            o = 6144 + (i * 2 + ts) * 512
            return sb[:, o:o + 512]

        def probsT(buf):
            return sb[:, buf * 1024:(buf + 1) * 1024]
        amask_sb = sb[:, 2048:4096]
        cht = [sb[:, 8192 + i * 512:8192 + (i + 1) * 512] for i in range(4)]

        def mixed(k, ts):
            o = k * 1024 + ts * 512
            return arena[:, o:o + 512]
        q_t = [arena[:, i * 1024:(i + 1) * 1024] for i in range(2)]
        k_t = [arena[:, 2048 + i * 2048:2048 + (i + 1) * 2048] for i in range(2)]
        v_t = [arena[:, 6144 + i * 2048:6144 + (i + 1) * 2048] for i in range(2)]

        def wtA(slot, buf):
            o = (slot * 2 + buf) * 2048
            return wa[:, o:o + 2048]

        def wtP(buf):
            return wa[:, buf * 2048:buf * 2048 + 2048]

        def wtB(buf):
            return wb[:, buf * 1408:buf * 1408 + 1408]

        def wvt(buf):
            return wb[:, buf * 640:buf * 640 + 512]

        def btl(slot, buf):
            o = (slot * 2 + buf) * 128
            return wsm[:, o:o + 128]

        upA_sb = lnc[:, 0:2048]
        dnB_sb = lnc[:, 0:2048]
        xsq = [lnc[:, 0:1024]] * 2
        xbf = [lnc[:, 1024:2048]] * 2

        ef = P.op
        S = {"y": 0}

        ef('sy', lambda e: e.dma_start(out=identsb[:, :], in_=identf), inc=(dA, 16))
        ef('sy', lambda e: e.dma_start(out=consts[:, 0:1], in_=ecol_d), inc=(dA, 16))
        ef('sy', lambda e: e.dma_start(out=consts[:, 8:16], in_=iota8_d), inc=(dA, 16))
        ef('sy', lambda e: e.dma_start(out=consts[:, 16:32], in_=bqd), inc=(dA, 16))
        ef('sy', lambda e: e.dma_start(out=consts[:, 32:48], in_=bkd), inc=(dA, 16))
        ef('sy', lambda e: e.dma_start(out=ones_rf, in_=ones_r), inc=(dA, 16))
        ef('sy', lambda e: e.dma_start(out=ones_cf, in_=ones_c), inc=(dA, 16))
        xload = []
        for dc in range(NDK):
            ef('sy', lambda e, dc=dc: e.dma_start(out=rtile(dc), in_=xs[dc * 128:(dc + 1) * 128, :]),
               waits=[(dA, dA.n)], inc=(dA, 16))
            xload.append(dA.n)
        S["prev_dA"] = xload
        ef('gp', lambda e: e.dma_start(out=ones_cb, in_=ones_c), inc=(dO, 16))
        ef('gp', lambda e: e.dma_start(out=ones_rb, in_=ones_r), inc=(dO, 16))
        ef('gp', lambda e: e.dma_start(out=gwtile[:, :], in_=gwt), inc=(dO, 16))
        mcg = dO.n

        def layer_norm(base_pe, base_dv, base_sc, prev_dc=None, scratch=None, prev_dA=None):
            scratch = ps if scratch is None else scratch
            mson = [0] * NDK
            psw = max(prev_dc[NDK - 1], prev_dc[NDK - 2]) if prev_dc is not None else base_dv
            for dc in range(NDK):
                b = dc % 2
                wpe = max(mson[dc - 1] if dc >= 1 else 0, base_pe)
                wdc = prev_dc[dc] if prev_dc is not None else base_dv
                wda = prev_dA[dc] if prev_dA is not None else dA.n
                ef('dv', lambda e, dc=dc, b=b: e.tensor_tensor(xsq[b], rtile(dc), rtile(dc), ALU.mult),
                   waits=[(dA, wda), (pe, wpe), (dv, wdc)], inc=(dv, 1))
                mdv = dv.n
                ef('sc', lambda e, dc=dc, b=b: e.copy(xbf[b], rtile(dc)),
                   waits=[(dA, wda), (pe, wpe), (dv, wdc), (sc, base_sc)], inc=(sc, 1))
                msc = sc.n
                for h2 in range(2):
                    tgt = ps[h2]
                    ef('pe', lambda e, b=b, h2=h2, tgt=tgt, dc=dc: e.matmul(
                        tgt[0:1, :], ones_cb, xsq[b][:, h2 * 512:(h2 + 1) * 512],
                        start=(dc == 0), stop=(dc == NDK - 1)),
                       waits=[(dv, mdv), (dO, mcg), (dv, psw), (sc, base_sc)])
                    tgt2 = ps[2 + h2]
                    mson[dc] = ef('pe', lambda e, b=b, h2=h2, tgt2=tgt2, dc=dc: e.matmul(
                        tgt2[0:1, :], ones_cb, xbf[b][:, h2 * 512:(h2 + 1) * 512],
                        start=(dc == 0), stop=(dc == NDK - 1)),
                       waits=[(sc, msc)], inc=(pe, 1))
            m_stats = pe.n
            S["lnstats"] = m_stats
            ef('sc', lambda e: e.activation(row_mean[:, 0:512], ps[2][0:1, :], AF.Copy, scale=1.0 / D),
               waits=[(pe, m_stats)], inc=(sc, 1))
            ef('sc', lambda e: e.activation(row_mean[:, 512:1024], ps[3][0:1, :], AF.Copy, scale=1.0 / D), inc=(sc, 1))
            ef('sc', lambda e: e.activation(row_var[:, 0:512], ps[0][0:1, :], AF.Copy, scale=1.0 / D), inc=(sc, 1))
            ef('sc', lambda e: e.activation(row_var[:, 512:1024], ps[1][0:1, :], AF.Copy, scale=1.0 / D), inc=(sc, 1))
            mrow = sc.n
            ef('dv', lambda e: e.tensor_tensor(ps[0][0:1, :], row_mean[:, 0:512], row_mean[:, 0:512], ALU.mult),
               waits=[(sc, mrow)], inc=(dv, 1))
            ef('dv', lambda e: e.tensor_tensor(ps[1][0:1, :], row_mean[:, 512:1024], row_mean[:, 512:1024], ALU.mult), inc=(dv, 1))
            ef('dv', lambda e: e.tensor_tensor(row_var[:, 0:512], row_var[:, 0:512], ps[0][0:1, :], ALU.subtract), inc=(dv, 1))
            ef('dv', lambda e: e.tensor_tensor(row_var[:, 512:1024], row_var[:, 512:1024], ps[1][0:1, :], ALU.subtract), inc=(dv, 1))
            mv = dv.n
            ef('dv', lambda e: e.tensor_single_scalar(row_var[:, :], row_var[:, :], 1e-5, ALU.add),
               waits=[(dv, mv)], inc=(dv, 1))
            ef('sc', lambda e: e.activation(row_var[:, :], row_var[:, :], AF.Sqrt),
               waits=[(dv, dv.n)], inc=(sc, 1))
            ef('dv', lambda e: e.reciprocal(row_var[:, :], row_var[:, :]), waits=[(sc, sc.n)], inc=(dv, 1))
            ef('dv', lambda e: e.scalar_tensor_tensor(row_mean[:, :], row_mean[:, :], -1.0,
                                                      row_var[:, :], ALU.mult, ALU.mult), inc=(dv, 1))
            mrs = dv.n
            for h2 in range(2):
                ef('pe', lambda e, h2=h2: e.matmul(ps[h2][:, :], ones_rf,
                                                   row_var[:, h2 * 512:(h2 + 1) * 512],
                                                   start=True, stop=True),
                   waits=[(dv, mrs), (dA, dA.n)], inc=(pe, 1))
                ef('pe', lambda e, h2=h2: e.matmul(ps[2 + h2][:, :], ones_rf,
                                                   row_mean[:, h2 * 512:(h2 + 1) * 512],
                                                   start=True, stop=True), inc=(pe, 1))
            mbc = pe.n
            ef('sc', lambda e: e.copy(rstd_b[:, 0:512], ps[0][:, :]), waits=[(pe, mbc)], inc=(sc, 1))
            ef('sc', lambda e: e.copy(rstd_b[:, 512:1024], ps[1][:, :]), inc=(sc, 1))
            ef('sc', lambda e: e.copy(bneg_b[:, 0:512], ps[2][:, :]), inc=(sc, 1))
            ef('sc', lambda e: e.copy(bneg_b[:, 512:1024], ps[3][:, :]), inc=(sc, 1))
            mbce = sc.n
            S["mbce"] = mbce
            mnorm = [0] * NDK
            for dc in range(NDK):
                pa, pb = scratch[2 * (dc % 2)], scratch[1 + 2 * (dc % 2)]
                wv_ = max(mnorm[dc - 2] if dc >= 2 else 0, 0)
                m = 0
                for h2 in range(2):
                    tgt = pa if h2 == 0 else pb
                    ef('dv', lambda e, dc=dc, h2=h2, tgt=tgt: e.tensor_tensor(
                        tgt[:, :], rtile(dc)[:, h2 * 512:(h2 + 1) * 512],
                        rstd_b[:, h2 * 512:(h2 + 1) * 512], ALU.mult),
                       waits=[(sc, mbce), (dv, wv_)])
                    m = ef('dv', lambda e, dc=dc, h2=h2, tgt=tgt: e.tensor_tensor(
                        ytile(dc)[:, h2 * 512:(h2 + 1) * 512], tgt[:, :],
                        bneg_b[:, h2 * 512:(h2 + 1) * 512], ALU.add), inc=(dv, 1))
                mnorm[dc] = m
            S["y"] = dv.n
            S["y_dc"] = list(mnorm)
            return dv.n

        def gating(call):
            my = S["y"]
            mls = [0] * NTC
            for tc in range(NTC):
                Lp = ps[tc % 2]
                wdv = mls[tc - 2] if tc >= 2 else 0
                for dk in range(NDK):
                    ef('pe', lambda e, tc=tc, dk=dk, Lp=Lp: e.matmul(
                        Lp[:, 0:8], ytile(dk)[:, tc * 128:(tc + 1) * 128],
                        gwtile[:, dk * 8:(dk + 1) * 8],
                        start=(dk == 0), stop=(dk == NDK - 1)),
                       waits=[(dv, max(my, wdv)), (dO, mcg)],
                       inc=(pe, 1) if dk == NDK - 1 else None)
                mL = pe.n
                ef('dv', lambda e, tc=tc, Lp=Lp: e.tensor_reduce(m0A[:, tc:tc + 1], Lp[:, 0:8], AX.X, ALU.max),
                   waits=[(pe, mL), (dA, dA.n)], inc=(dv, 1))
                ef('sc', lambda e, tc=tc: e.activation(pack[:, 16 + tc:17 + tc], m0A[:, tc:tc + 1], AF.Exp),
                   waits=[(dv, dv.n)], inc=(sc, 1))
                ef('dv', lambda e, tc=tc, Lp=Lp: e.tensor_single_scalar(eqs, Lp[:, 0:8], m0A[:, tc:tc + 1], ALU.is_equal))
                ef('dv', lambda e: e.scalar_tensor_tensor(tmp8, eqs, -1000.0, iotasb, ALU.mult, ALU.add))
                ef('dv', lambda e, tc=tc: e.tensor_reduce(pack[:, tc:tc + 1], tmp8, AX.X, ALU.min))
                ef('dv', lambda e, tc=tc: e.tensor_single_scalar(pack[:, tc:tc + 1], pack[:, tc:tc + 1], 1000.0, ALU.add))
                ef('dv', lambda e, tc=tc: e.tensor_single_scalar(eqs, iotasb, pack[:, tc:tc + 1], ALU.is_equal))
                ef('dv', lambda e, Lp=Lp: e.scalar_tensor_tensor(lmt, eqs, -1e30, Lp[:, 0:8], ALU.mult, ALU.add),
                   inc=(dv, 1))
                mls[tc] = dv.n
                ef('dv', lambda e, tc=tc: e.tensor_reduce(m1A[:, tc:tc + 1], lmt, AX.X, ALU.max), inc=(dv, 1))
                ef('sc', lambda e, tc=tc: e.activation(pack[:, 24 + tc:25 + tc], m1A[:, tc:tc + 1], AF.Exp),
                   waits=[(dv, dv.n)], inc=(sc, 1))
                ef('dv', lambda e, tc=tc: e.tensor_single_scalar(eqs, lmt, m1A[:, tc:tc + 1], ALU.is_equal))
                ef('dv', lambda e: e.scalar_tensor_tensor(tmp8, eqs, -1000.0, iotasb, ALU.mult, ALU.add))
                ef('dv', lambda e, tc=tc: e.tensor_reduce(pack[:, 8 + tc:9 + tc], tmp8, AX.X, ALU.min))
                ef('dv', lambda e, tc=tc: e.tensor_single_scalar(pack[:, 8 + tc:9 + tc], pack[:, 8 + tc:9 + tc],
                                                                 1000.0, ALU.add), inc=(dv, 1))
            msce = sc.n
            mtop = dv.n
            ef('dv', lambda e: e.tensor_reduce(s01[:, 0:1], pack[:, 16:24], AX.X, ALU.add),
               waits=[(sc, msce)], inc=(dv, 1))
            ef('dv', lambda e: e.tensor_reduce(s01[:, 1:2], pack[:, 24:32], AX.X, ALU.add), inc=(dv, 1))
            mS = dv.n
            ef('pe', lambda e: e.matmul(ps[2][0:1, 0:2], ones_cf, s01, start=True, stop=True),
               waits=[(dv, mS), (dA, dA.n)], inc=(pe, 1))
            ef('sc', lambda e: e.copy(gstage[:, 4:6], ps[2][0:1, 0:2]), waits=[(pe, pe.n)], inc=(sc, 1))
            m_sevac = sc.n
            ef('sy', lambda e, call=call: e.dma_start(out=gin[call], in_=gstage[:, 4:6]),
               waits=[(sc, m_sevac)], inc=(dA, 16))
            mgin = dA.n
            ef('gp', lambda e, call=call: e.collective_compute(
                "AllGather", ALU.bypass, replica_groups=[[0, 1], [2, 3], [4, 5], [6, 7]],
                ins=[gin[call].opt()], outs=[gout[call].opt()]),
               waits=[(dA, mgin)], inc=(cc[call], 1))
            ef('sy', lambda e, call=call: e.dma_start(out=gstage[:, 0:2], in_=gout[call][0:1, :]),
               waits=[(cc[call], 1)], inc=(dZ, 16))
            ef('sy', lambda e, call=call: e.dma_start(out=gstage[:, 2:4], in_=gout[call][1:2, :]),
               inc=(dZ, 16))
            mgz = dZ.n
            ef('dv', lambda e: e.tensor_tensor(gstage[:, 6:8], gstage[:, 0:2], gstage[:, 2:4], ALU.add),
               waits=[(dZ, mgz)], inc=(dv, 1))
            ef('dv', lambda e: e.reciprocal(gstage[:, 6:8], gstage[:, 6:8]), inc=(dv, 1))
            mrz = dv.n
            ef('pe', lambda e: e.transpose(ps[2][0:32, 0:128], pack[:, :], identsb[:, :]),
               waits=[(dv, mtop), (sc, m_sevac)], inc=(pe, 1))
            m_trp = pe.n
            ef('sc', lambda e: e.copy(trsb[:, :], ps[2][0:32, 0:128]), waits=[(pe, m_trp)], inc=(sc, 1))
            m_trevac = sc.n
            ef('sy', lambda e: e.dma_start(out=trd, in_=trsb[:, :]), waits=[(sc, m_trevac)], inc=(dA, 16))
            mtr = dA.n
            ef('sy', lambda e: e.dma_start(out=row_wf[0],
                                           in_=trd[16:24, :].rearrange("a b -> (a b)").unsqueeze(0)),
               waits=[(dA, mtr)], inc=(dA, 16))
            ef('sy', lambda e: e.dma_start(out=row_wf[1],
                                           in_=trd[24:32, :].rearrange("a b -> (a b)").unsqueeze(0)),
               inc=(dA, 16))
            mwf = dA.n
            ef('gp', lambda e: e.dma_start(out=row_idx[0],
                                           in_=trd[0:8, :].rearrange("a b -> (a b)").unsqueeze(0)),
               waits=[(dA, mtr)], inc=(dO, 16))
            ef('gp', lambda e: e.dma_start(out=row_idx[1],
                                           in_=trd[8:16, :].rearrange("a b -> (a b)").unsqueeze(0)),
               inc=(dO, 16))
            mrows = dO.n
            mmask_prev_dv = 0
            for ts in range(2):
                ef('pe', lambda e, ts=ts: e.matmul(ps[0][:, :], ones_rb,
                                                   row_idx[0][:, ts * 512:(ts + 1) * 512],
                                                   start=True, stop=True),
                   waits=[(dO, mrows), (dv, mmask_prev_dv), (dv, mtop)], inc=(pe, 1))
                ef('pe', lambda e, ts=ts: e.matmul(ps[1][:, :], ones_rb,
                                                   row_idx[1][:, ts * 512:(ts + 1) * 512],
                                                   start=True, stop=True), inc=(pe, 1))
                mb = pe.n
                ef('dv', lambda e, ts=ts: e.tensor_scalar(Mm(0, ts), ps[0][:, :], ecol, 2.0,
                                                          ALU.is_equal, ALU.mult),
                   waits=[(pe, mb)], inc=(dv, 1))
                ef('dv', lambda e, ts=ts: e.tensor_scalar(Mm(1, ts), ps[1][:, :], ecol, 2.0,
                                                          ALU.is_equal, ALU.mult), inc=(dv, 1))
                mmask_prev_dv = dv.n
            mmask = dv.n
            for wi in range(2):
                ef('dv', lambda e, wi=wi: e.tensor_single_scalar(
                    row_wf[wi], row_wf[wi], gstage[:, 6 + wi:7 + wi], ALU.mult),
                   waits=[(dA, mwf), (dv, mrz)], inc=(dv, 1))
            mwsc = dv.n
            mcb_sc = m_trevac
            for ts in range(2):
                for wi in range(2):
                    tgt = ps[2 + wi]
                    ef('pe', lambda e, ts=ts, wi=wi, tgt=tgt: e.matmul(
                        tgt[:, :], ones_rf, row_wf[wi][:, ts * 512:(ts + 1) * 512],
                        start=True, stop=True),
                       waits=[(dv, mwsc), (sc, mcb_sc)], inc=(pe, 1))
                mcb = pe.n
                ef('sc', lambda e, ts=ts: e.copy(crowb(0, ts), ps[2][:, :]), waits=[(pe, mcb)], inc=(sc, 1))
                ef('sc', lambda e, ts=ts: e.copy(crowb(1, ts), ps[3][:, :]), inc=(sc, 1))
                mcb_sc = sc.n
            return mmask, sc.n

        def ffn(call):
            base_pe, base_dv, base_sc = pe.n, dv.n, sc.n
            layer_norm(base_pe, base_dv, base_sc, S.get("resid_dc"),
                       prev_dA=S.pop("prev_dA", None))
            mr_done = [0]
            for w_i, Ad in enumerate((upAT, gateAT)):
                wa_pe = S["lnstats"] if w_i == 0 else pe.n
                ef('gp', lambda e, Ad=Ad: e.dma_start(out=_t3(upA_sb, 128), in_=_tl(Ad)),
                   waits=[(pe, wa_pe)], inc=(dP, 16))
                mA = dP.n
                for ts in range(2):
                    tgt = psQ[2 * w_i + ts]
                    for dk in range(NDK):
                        ef('pe', lambda e, ts=ts, dk=dk, tgt=tgt: e.matmul(
                            tgt[:, :], upA_sb[:, dk * 128:(dk + 1) * 128], ytile(dk, ts),
                            start=(dk == 0), stop=(dk == NDK - 1)),
                           waits=[(dP, mA), (dv, S["y_dc"][dk]), (dv, base_dv)],
                           inc=(pe, 1) if dk == NDK - 1 else None)
            mr_done[0] = pe.n
            mmask_dv, mmask_sc = gating(call)

            for w_i in range(2):
                for ts in range(2):
                    tgt = psQ[2 * w_i + ts]
                    ef('dv', lambda e, ts=ts, w_i=w_i, tgt=tgt: e.tensor_tensor(
                        rho(2 * w_i, ts), tgt[:, :], Mm(0, ts), ALU.mult),
                       waits=[(pe, mr_done[0]), (dv, mmask_dv)], inc=(dv, 1))
                    ef('dv', lambda e, ts=ts, tgt=tgt: e.tensor_tensor(
                        cht[0], tgt[:, :], Mm(1, ts), ALU.mult), inc=(dv, 1))
                    ef('dv', lambda e, ts=ts, w_i=w_i: e.tensor_tensor(
                        rho(2 * w_i + 1, ts), cht[0], rho(2 * w_i, ts), ALU.subtract), inc=(dv, 1))
            mrho = dv.n

            wload = [0] * NHC
            wfree = [0] * NHC

            wload_s = [0] * NHC

            def load_hc(hc):
                wpe = max(wfree[hc - 2] if hc >= 2 else 0, base_pe)
                sem = wA[hc % 2]
                ef('gp', lambda e, hc=hc: e.dma_start(
                    out=_t3(wtA(0, hc % 2), 128), in_=_tl(wupT[:, hc * 128:(hc + 1) * 128])),
                   waits=[(pe, wpe)], inc=(sem, 16))
                ef('gp', lambda e, hc=hc: e.dma_start(
                    out=_t3(wtA(1, hc % 2), 128), in_=_tl(wgateT[:, hc * 128:(hc + 1) * 128])),
                   inc=(sem, 16))
                wload[hc] = sem.n

            def load_small(hc):
                wpe = max(wfree_q[hc - 2] if hc >= 2 else 0, base_pe)
                sem = wS[hc % 2]
                ef('gp', lambda e, hc=hc: e.dma_start(
                    out=btl(0, hc % 2), in_=upBT[:, hc * 128:(hc + 1) * 128]),
                   waits=[(pe, wpe)], inc=(sem, 16))
                ef('gp', lambda e, hc=hc: e.dma_start(
                    out=btl(1, hc % 2), in_=gateBT[:, hc * 128:(hc + 1) * 128]), inc=(sem, 16))
                ef('gp', lambda e, hc=hc: e.dma_start(
                    out=_t3(btl(2, hc % 2), 128), in_=_tl(dnAT[hc * 128:(hc + 1) * 128, :])), inc=(sem, 16))
                wload_s[hc] = sem.n

            load_hc(0)
            load_hc(1)
            load_small(0)
            load_small(1)
            slA_free = [0, 0]
            slB_free = [0, 0]
            cht_free = [0, 0]
            pend_q = []
            wfree_q = [0] * NHC
            silu0_free = [0]
            silu1_free = [0]
            bA_free = [0]
            bB_free = [0]
            mdnb = [0]
            mrho2 = [0]
            bfree = [0] * NDK
            bfree2 = [0] * 32
            psfree_dv = [0]

            for half in range(2):
                bload2 = [0] * 32
                bfree2_prev = list(bfree2) if half == 1 else [0] * 32

                def load_b(dc, pc, half=half, bload2=bload2, bfree2_prev=bfree2_prev):
                    idx = 2 * dc + pc
                    prev = bfree2[idx - 2] if idx >= 2 else (bfree2_prev[idx] if half == 1 else 0)
                    ef('gp', lambda e, dc=dc, pc=pc, half=half: e.dma_start(
                        out=_t3(wtB((2 * dc + pc) % 2), 128),
                        in_=_tl(wdownT[half * 2816 + pc * 1408:half * 2816 + (pc + 1) * 1408,
                                       dc * 128:(dc + 1) * 128])),
                       waits=[(pe, max(prev, base_pe)), (dv, base_dv)], inc=(wB[idx % 2], 16))
                    bload2[idx] = wB[idx % 2].n

                for k in range(HALF):
                    hc = half * HALF + k
                    par = hc % 2
                    if k == HALF - 3:
                        load_b(0, 0)
                        load_b(0, 1)
                    for ts in range(2):
                        pA, pB, pDu, pDg = ps[0], ps[1], ps[2], ps[3]
                        for dk in range(NDK):
                            ef('pe', lambda e, ts=ts, dk=dk, par=par: e.matmul(
                                pA[:, :], wtA(0, par)[:, dk * 128:(dk + 1) * 128], ytile(dk, ts),
                                start=(dk == 0), stop=False),
                               waits=([(wA[par], wload[hc]), (sc, slA_free[0]), (dv, mmask_dv),
                                       (dv, psfree_dv[0])]
                                      if dk == 0 else []))
                        ef('pe', lambda e, ts=ts, par=par: e.matmul(
                            pA[:, :], btl(0, par), rho(0, ts), start=False, stop=True),
                           waits=[(dv, mrho), (wS[par], wload_s[hc])], inc=(pe, 1))
                        mAup = pe.n
                        for dk in range(NDK):
                            ef('pe', lambda e, ts=ts, dk=dk, par=par: e.matmul(
                                pB[:, :], wtA(1, par)[:, dk * 128:(dk + 1) * 128], ytile(dk, ts),
                                start=(dk == 0), stop=False),
                               waits=([(sc, slA_free[1])] if dk == 0 else []))
                        ef('pe', lambda e, ts=ts, par=par: e.matmul(
                            pB[:, :], btl(1, par), rho(2, ts), start=False, stop=True),
                           inc=(pe, 1))
                        mAg = pe.n
                        ef('pe', lambda e, ts=ts, par=par: e.matmul(
                            pDu[:, :], btl(0, par), rho(1, ts), start=True, stop=True),
                           waits=[(dv, slB_free[0]), (dv, psfree_dv[0]), (sc, mmask_sc)], inc=(pe, 1))
                        mDu = pe.n
                        ef('pe', lambda e, ts=ts, par=par: e.matmul(
                            pDg[:, :], btl(1, par), rho(3, ts), start=True, stop=True),
                           waits=[(dv, slB_free[1]), (dv, psfree_dv[0]), (sc, mmask_sc)], inc=(pe, 1))
                        mDg = pe.n
                        ef('sc', lambda e: e.copy(bA, pA[:, :]),
                           waits=[(pe, mAup), (dv, bA_free[0])], inc=(sc, 1))
                        slA_free[0] = sc.n
                        ef('sc', lambda e: e.copy(bB, pB[:, :]),
                           waits=[(pe, mAg), (dv, bB_free[0])], inc=(sc, 1))
                        slA_free[1] = sc.n
                        it2 = (hc * 2 + ts) % 2
                        ch2 = cht[2] if it2 == 0 else kstg[:, 0:512]
                        ch3 = cht[3] if it2 == 0 else kstg[:, 512:1024]
                        ef('sc', lambda e: e.activation(siluX, bA, AF.Silu),
                           waits=[(dv, silu0_free[0])], inc=(sc, 1))
                        ms0 = sc.n
                        ef('dv', lambda e: e.tensor_tensor(cht[0], siluX, bB, ALU.mult),
                           waits=[(sc, ms0), (pe, cht_free[it2])], inc=(dv, 1))
                        silu0_free[0] = dv.n
                        ef('dv', lambda e: e.tensor_tensor(s1t, bA, pDu[:, :], ALU.add),
                           waits=[(pe, mDu)], inc=(dv, 1))
                        slB_free[0] = dv.n
                        bA_free[0] = dv.n
                        ef('sc', lambda e: e.activation(siluX, s1t, AF.Silu),
                           waits=[(dv, dv.n)], inc=(sc, 1))
                        ms1 = sc.n
                        ef('dv', lambda e: e.tensor_tensor(t2t, bB, pDg[:, :], ALU.add),
                           waits=[(pe, mDg)], inc=(dv, 1))
                        slB_free[1] = dv.n
                        bB_free[0] = dv.n
                        ef('dv', lambda e: e.tensor_tensor(cht[1], siluX, t2t, ALU.mult),
                           waits=[(sc, ms1)], inc=(dv, 1))
                        silu0_free[0] = dv.n
                        silu1_free[0] = dv.n
                        ef('dv', lambda e, ts=ts, ch2=ch2: e.tensor_tensor(ch2, cht[0], crowb(0, ts), ALU.mult))
                        ef('dv', lambda e, ts=ts, ch3=ch3: e.tensor_tensor(ch3, cht[1], crowb(1, ts), ALU.mult))
                        ef('dv', lambda e, k=k, ts=ts, ch2=ch2, ch3=ch3: e.tensor_tensor(
                            mixed(k, ts), ch2, ch3, ALU.add), inc=(dv, 1))
                        mch = dv.n
                        def qmm(e, hc=hc, par=par, ts=ts, ch2=ch2):
                            return e.matmul(psQ[ts][:, :], btl(2, par), ch2,
                                            start=(hc == 0), stop=(hc == NHC - 1))
                        def qmm2(e, hc=hc, par=par, ts=ts, ch3=ch3):
                            return e.matmul(psQ[2 + ts][:, :], btl(2, par), ch3,
                                            start=(hc == 0), stop=(hc == NHC - 1))
                        pend_q.append((qmm, qmm2, mch, hc, it2))
                        if len(pend_q) > 1:
                            f1, f2, mw, phc, pit2 = pend_q.pop(0)
                            ef('pe', f1, waits=[(dv, mw)])
                            ef('pe', f2, inc=(pe, 1))
                            cht_free[pit2] = pe.n
                            wfree_q[phc] = pe.n
                            if ts == 0 and hc + 1 < NHC and hc >= 1:
                                load_small(hc + 1)
                        if ts == 1:
                            wfree[hc] = pe.n
                    if hc + 2 < NHC:
                        load_hc(hc + 2)

                if half == 1:
                    while pend_q:
                        f1, f2, mw, phc, pit2 = pend_q.pop(0)
                        ef('pe', f1, waits=[(dv, mw)])
                        ef('pe', f2, inc=(pe, 1))
                        cht_free[pit2] = pe.n
                        wfree_q[phc] = pe.n
                    S["cht_dead_pe"] = pe.n
                    S["cht_dead_dv"] = dv.n
                mmixed = dv.n
                if half == 1:
                    ef('gp', lambda e: e.dma_start(out=dnB_sb, in_=dnBT),
                       waits=[(pe, pe.n)], inc=(dP, 16))
                    mdnb[0] = dP.n
                    for ts in range(2):
                        ef('dv', lambda e, ts=ts: e.tensor_tensor(cht[0], psQ[ts][:, :], Mm(0, ts), ALU.mult),
                           waits=[(pe, max(cht_free[0], cht_free[1]))], inc=(dv, 1))
                        ef('dv', lambda e, ts=ts: e.tensor_tensor(cht[1], psQ[2 + ts][:, :], Mm(1, ts), ALU.mult),
                           inc=(dv, 1))
                        ef('dv', lambda e, ts=ts: e.tensor_tensor(rho(0, ts), cht[0], cht[1], ALU.add),
                           inc=(dv, 1))
                    mrho2[0] = dv.n
                for dc in range(NDK):
                    pf = [ps[(dc % 2) * 2], ps[(dc % 2) * 2 + 1]]
                    mpc = [0, 0]
                    for k in range(HALF):
                        pc = k // 11
                        kk = k % 11
                        buf = (2 * dc + pc) % 2
                        for ts in range(2):
                            isl = (half == 0 and k == HALF - 1 and ts == 1)
                            w0 = []
                            if kk == 0 and ts == 0:
                                w0 = [(wB[(2 * dc + pc) % 2], bload2[2 * dc + pc]), (dv, mmixed),
                                      (dv, bfree[dc - 2] if dc >= 2 else 0),
                                      (sc, slA_free[0]), (sc, slA_free[1]),
                                      (dv, slB_free[0]), (dv, slB_free[1]),
                                      (dv, silu1_free[0])]
                            ef('pe', lambda e, ts=ts, k=k, kk=kk, buf=buf, pX=pf[ts],
                               stp=(half == 0 and k == HALF - 1): e.matmul(
                                pX[:, :], wtB(buf)[:, kk * 128:(kk + 1) * 128], mixed(k, ts),
                                start=(k == 0), stop=stp),
                               waits=w0,
                               inc=(pe, 1) if (kk == 10 and ts == 1) or isl else None)
                        if kk == 10 and ts == 1:
                            mpc[pc] = pe.n
                        if k == 10:
                            bfree2[2 * dc] = pe.n
                            ni = 2 * dc + 2
                            if ni < 32:
                                load_b(ni // 2, ni % 2)
                        if k == HALF - 1:
                            bfree2[2 * dc + 1] = pe.n
                            ni = 2 * dc + 3
                            if ni < 32:
                                load_b(ni // 2, ni % 2)
                    for ts in range(2):
                        if half == 1:
                            ef('pe', lambda e, dc=dc, ts=ts, pX=pf[ts]: e.matmul(
                                pX[:, :], dnB_sb[:, dc * 128:(dc + 1) * 128], rho(0, ts),
                                start=False, stop=True),
                               waits=[(dP, mdnb[0]), (dv, mrho2[0])], inc=(pe, 1))
                        mpb = pe.n
                        ef('dv', lambda e, dc=dc, ts=ts, pX=pf[ts]: e.scalar_tensor_tensor(
                            rtile(dc, ts), pX[:, :], 0.5, rtile(dc, ts), ALU.mult, ALU.add),
                           waits=[(pe, mpb)], inc=(dv, 1))
                    bfree[dc] = dv.n
                psfree_dv[0] = dv.n
            S["resid_dc"] = list(bfree)

        def attention():
            base_pe, base_dv, base_sc = pe.n, dv.n, sc.n
            layer_norm(base_pe, base_dv, base_sc, S.get("resid_dc"), scratch=psQ)
            my = S["y"]
            ydc = S["y_dc"]
            ef('gp', lambda e: e.dma_start(out=row_bv, in_=bvd),
               waits=[(dv, S["cht_dead_dv"]), (pe, S["cht_dead_pe"])], inc=(dO, 16))
            mbv = dO.n
            for h4 in range(4):
                ef('pe', lambda e, h4=h4: e.matmul(ps[h4][:, :], ones_rb,
                                                   row_bv[:, h4 * 512:(h4 + 1) * 512],
                                                   start=True, stop=True),
                   waits=[(dO, mbv), (sc, S["mbce"])], inc=(pe, 1))
            mbc2 = pe.n
            for h4 in range(4):
                ef('sc', lambda e, h4=h4: e.copy(bv_b[:, h4 * 512:(h4 + 1) * 512], ps[h4][:, :]),
                   waits=[(pe, mbc2), (dv, base_dv)], inc=(sc, 1))
            mbve = sc.n
            pload = [0] * (2 * NDK)
            pfree = [0] * (2 * NDK)
            pfree_sc = [0] * (2 * NDK)
            kdma = {0: 0, 1: 0}
            kview = cin.rearrange("a (b c) -> (a b) c", b=2)

            def load_p(i):
                wpe = max(pfree[i - 2] if i >= 2 else 0, base_pe)
                Wd = wk_d if i < NDK else wq_d
                oc = i % NDK
                ef('gp', lambda e, Wd=Wd, oc=oc, i=i: e.dma_start(
                    out=_t3(wtP(i % 2), 128), in_=_tl(Wd[:, oc * 128:(oc + 1) * 128])),
                   waits=[(pe, wpe)], inc=(wP[i % 2], 16))
                pload[i] = wP[i % 2].n

            load_p(0)
            load_p(1)
            for i in range(NDK):
                oc = i % NDK
                isq = i >= NDK
                for ts in range(2):
                    pX = ps[(i % 2) * 2 + ts]
                    for dk in range(NDK):
                        w0 = [(dv, ydc[dk])] if i < 2 else []
                        if dk == 0:
                            w0 = w0 + [(wP[i % 2], pload[i]), (sc, mbve),
                                       (dv, ydc[dk] if i < 2 else my),
                                       (dv, pfree_sc[i - 2] if i >= 2 else 0)]
                        ef('pe', lambda e, i=i, ts=ts, dk=dk, pX=pX: e.matmul(
                            pX[:, :], wtP(i % 2)[:, dk * 128:(dk + 1) * 128], ytile(dk, ts),
                            start=(dk == 0), stop=(dk == NDK - 1)),
                           waits=w0,
                           inc=(pe, 1) if dk == NDK - 1 else None)
                    mproj = pe.n
                    bias = bq_s[:, oc:oc + 1] if isq else bk_s[:, oc:oc + 1]
                    scl = SM if isq else 1.0
                    ef('dv', lambda e, ts=ts, pX=pX, bias=bias, scl=scl: e.tensor_scalar(
                        kstg[:, ts * 512:(ts + 1) * 512], pX[:, :], bias, scl,
                        ALU.add, ALU.mult),
                       waits=[(pe, mproj), (dK[ts], kdma[ts])], inc=(dv, 1))
                    dst = (qdram[oc * 128:(oc + 1) * 128, ts * 512:(ts + 1) * 512] if isq
                           else kview[oc * 128:(oc + 1) * 128, ts * 512:(ts + 1) * 512])
                    ef('sy', lambda e, dst=dst, ts=ts: e.dma_start(
                        out=dst, in_=kstg[:, ts * 512:(ts + 1) * 512]),
                       waits=[(dv, dv.n)], inc=(dK[ts], 16))
                    kdma[ts] = dK[ts].n
                pfree[i] = pe.n
                pfree_sc[i] = dv.n
                if i + 2 < NDK:
                    load_p(i + 2)
                if i == NDK - 1:
                    ef('gp', lambda e: e.collective_compute(
                        "AllGather", ALU.bypass, replica_groups=[[0, 1], [2, 3], [4, 5], [6, 7]],
                        ins=[cin[0:1024, :].opt()], outs=[coutk.opt()]),
                       waits=[(dK[0], dK[0].n), (dK[1], dK[1].n)], inc=(cc[2], 1))
            vsem = [wV[0], wV[1], wS[0], wS[1]]
            vw_free = [base_pe] * 4
            vload = {}
            vseq = [(tcb, dvb, dk) for tcb in range(2) for dvb in range(4) for dk in range(NDK)]

            def loadv(i):
                tcb, dvb, dk = vseq[i]
                buf = i % 4
                ef('gp', lambda e, dvb=dvb, dk=dk, buf=buf: e.dma_start(
                    out=wvt(buf), in_=wv_d[dk * 128:(dk + 1) * 128, dvb * 512:(dvb + 1) * 512]),
                   waits=[(pe, vw_free[buf]), (dv, base_dv)], inc=(vsem[buf], 16))
                vload[i] = vsem[buf].n

            for i in range(4):
                loadv(i)
            vev_free = [base_dv] * 4
            for tcb in range(2):
                for dvb in range(4):
                    for dk in range(NDK):
                        i = (tcb * 4 + dvb) * NDK + dk
                        buf = i % 4
                        for j4 in range(4):
                            tc = tcb * 4 + j4
                            w0 = [(vsem[buf], vload[i])] if j4 == 0 else []
                            if dk == 0:
                                w0 = w0 + [(dv, vev_free[j4]), (dv, my)]
                            ef('pe', lambda e, tc=tc, dk=dk, j4=j4, buf=buf: e.matmul(
                                psQ[j4][:, :], ytile(dk)[:, tc * 128:(tc + 1) * 128], wvt(buf),
                                start=(dk == 0), stop=(dk == NDK - 1)),
                               waits=w0,
                               inc=(pe, 1) if j4 == 3 else None)
                        vw_free[buf] = pe.n
                        if i + 4 < len(vseq):
                            loadv(i + 4)
                    mvp = pe.n
                    for j4 in range(4):
                        tc = tcb * 4 + j4
                        sslot = j4 % 2
                        ef('dv', lambda e, j4=j4, dvb=dvb, sslot=sslot: e.tensor_tensor(
                            kstg[:, sslot * 512:sslot * 512 + 512], psQ[j4][:, :],
                            bv_b[:, dvb * 512:(dvb + 1) * 512], ALU.add),
                           waits=[(pe, mvp), (sc, mbve), (dK[sslot], kdma[sslot])], inc=(dv, 1))
                        vev_free[j4] = dv.n
                        ef('sy', lambda e, tc=tc, dvb=dvb, sslot=sslot: e.dma_start(
                            out=cin[1024 + tc * 128:1024 + (tc + 1) * 128, dvb * 512:(dvb + 1) * 512],
                            in_=kstg[:, sslot * 512:sslot * 512 + 512]),
                           waits=[(dv, dv.n)], inc=(dK[sslot], 16))
                        kdma[sslot] = dK[sslot].n
            ef('gp', lambda e: e.collective_compute(
                "AllGather", ALU.bypass, replica_groups=[[0, 1], [2, 3], [4, 5], [6, 7]],
                ins=[cin[1024:2048, :].opt()], outs=[coutv.opt()]),
               waits=[(dK[0], dK[0].n), (dK[1], dK[1].n)], inc=(cc[3], 1))
            load_p(NDK)
            load_p(NDK + 1)
            qst = [None] * NDK
            for i in range(NDK, 2 * NDK):
                oc = i % NDK
                isq = i >= NDK
                for ts in range(2):
                    pX = ps[(i % 2) * 2 + ts]
                    for dk in range(NDK):
                        ef('pe', lambda e, i=i, ts=ts, dk=dk, pX=pX: e.matmul(
                            pX[:, :], wtP(i % 2)[:, dk * 128:(dk + 1) * 128], ytile(dk, ts),
                            start=(dk == 0), stop=(dk == NDK - 1)),
                           waits=([(wP[i % 2], pload[i]), (dv, my), (sc, mbve),
                                   (dv, pfree_sc[i - 2] if i >= 2 else 0)] if dk == 0 else []),
                           inc=(pe, 1) if dk == NDK - 1 else None)
                    mproj = pe.n
                    bias = bq_s[:, oc:oc + 1] if isq else bk_s[:, oc:oc + 1]
                    scl = SM if isq else 1.0
                    ef('dv', lambda e, ts=ts, pX=pX, bias=bias, scl=scl: e.tensor_scalar(
                        kstg[:, ts * 512:(ts + 1) * 512], pX[:, :], bias, scl,
                        ALU.add, ALU.mult),
                       waits=[(pe, mproj), (dK[ts], kdma[ts])], inc=(dv, 1))
                    dst = (qdram[oc * 128:(oc + 1) * 128, ts * 512:(ts + 1) * 512] if isq
                           else kview[oc * 128:(oc + 1) * 128, ts * 512:(ts + 1) * 512])
                    ef('sy', lambda e, dst=dst, ts=ts: e.dma_start(
                        out=dst, in_=kstg[:, ts * 512:(ts + 1) * 512]),
                       waits=[(dv, dv.n)], inc=(dK[ts], 16))
                    kdma[ts] = dK[ts].n
                pfree[i] = pe.n
                pfree_sc[i] = dv.n
                qst[oc] = (dK[0].n, dK[1].n)
                if i + 2 < 2 * NDK:
                    load_p(i + 2)
            ef('sy', lambda e: e.dma_start(out=amask_sb, in_=amask_d),
               waits=[(dv, mrho_dead_dv[0])], inc=(dA, 16))
            mam = dA.n
            kvout = coutk.rearrange("a (b c) -> (a b) c", b=2)
            hload = [0] * NH
            hvl = [0, 0]
            hfree_q = [base_dv, base_dv]
            hfree_k = [base_pe, base_pe]
            hfree_v = [base_pe, base_pe]
            sco_free = [base_sc, base_sc]
            pb_free = [base_pe, base_pe]
            ctx_free = [base_dv]
            l_free = [base_sc]

            def load_head(h):
                buf = h % 2
                sem = dH[buf]
                ef('sy', lambda e, h=h, buf=buf: e.dma_start(
                    out=q_t[buf], in_=qdram[h * 128:(h + 1) * 128, :]),
                   waits=[(dv, hfree_q[buf]), (dK[0], qst[h][0]), (dK[1], qst[h][1])],
                   inc=(sem, 16))
                for j in range(2):
                    src = kvout[2048 * j + h * 128:2048 * j + (h + 1) * 128, :]
                    dst = k_t[buf].rearrange("p (g c) -> p g c", c=128)[:, j::2, :]
                    ef('sy', lambda e, src=src, dst=dst: e.dma_start(
                        out=dst, in_=src.rearrange("p (g c) -> p g c", c=128)),
                       waits=[(cc[2], 1), (pe, hfree_k[buf])], inc=(sem, 16))
                vbuf = buf
                for j in range(2):
                    src = coutv[1024 * j:1024 * (j + 1), h * 128:(h + 1) * 128]
                    dst = v_t[vbuf].rearrange("p (g w) -> p g w", w=128)[:, j::2, :]
                    ef('sy', lambda e, src=src, dst=dst: e.dma_start(
                        out=dst, in_=src.rearrange("(g p) w -> p g w", p=128)),
                       waits=[(cc[3], 1), (pe, hfree_v[vbuf])], inc=(sem, 16))
                hvl[vbuf] = sem.n
                hload[h] = sem.n

            load_head(0)
            load_head(1)
            for h in range(NH):
                buf = h % 2
                vbuf = h % 2
                hp = h % 2
                mpv = 0
                pend_pv = []
                for kt in range(16):
                    s0 = 128 * (kt // 2)
                    pr = [ps[0], ps[1]] if kt % 2 == 0 else [ps[2], ps[3]]
                    segs = ([(0, s0, 512)] if s0 < 512 else []) + [(1, max(512, s0), 1024)]
                    for n_, (si, a, b_) in enumerate(segs):
                        ef('pe', lambda e, kt=kt, buf=buf, pr=pr, si=si, a=a, b_=b_: e.matmul(
                            pr[si][:, a - si * 512:b_ - si * 512],
                            k_t[buf][:, kt * 128:(kt + 1) * 128], q_t[buf][:, a:b_],
                            start=True, stop=True),
                           waits=([(dH[buf], hload[h]), (sc, sco_free[kt % 2])] if n_ == 0 else []),
                           inc=(pe, 1) if n_ == len(segs) - 1 else None)
                    msco = pe.n
                    for (si, a, b_) in segs:
                        ef('sc', lambda e, kt=kt, si=si, a=a, b_=b_, pr=pr: e.activation(
                            probsT(kt % 2)[:, a:b_], pr[si][:, a - si * 512:b_ - si * 512], AF.Exp),
                           waits=[(pe, max(msco, pb_free[kt % 2])), (dA, mam)], inc=(sc, 1))
                    msce2 = sc.n
                    sco_free[kt % 2] = msce2
                    ef('dv', lambda e, kt=kt, s0=s0: e.tensor_tensor(
                        probsT(kt % 2)[:, s0:s0 + 128], probsT(kt % 2)[:, s0:s0 + 128],
                        amask_sb[:, kt * 128:(kt + 1) * 128], ALU.mult),
                       waits=[(sc, msce2)], inc=(dv, 1))
                    mmk = dv.n

                    def mk_pv(kt=kt, segs=segs, vbuf=vbuf, hp=hp, mmk=mmk, h=h):
                        for n_, (si, a, b_) in enumerate(segs):
                            stp = (kt == (15 if si == 1 else 7))
                            ef('pe', lambda e, kt=kt, si=si, a=a, b_=b_, vbuf=vbuf, stp=stp: e.matmul(
                                psQ[si][:, a - si * 512:b_ - si * 512],
                                v_t[vbuf][:, kt * 128:(kt + 1) * 128],
                                probsT(kt % 2)[:, a:b_],
                                start=(kt == 0), stop=stp),
                               waits=([(dH[vbuf], hvl[vbuf]), (dv, mmk), (dv, ctx_free[0]),
                                       (sc, l_free[0])] if n_ == 0 else []))
                            ef('pe', lambda e, kt=kt, si=si, a=a, b_=b_, stp=stp: e.matmul(
                                psQ[2 + si][0:1, a - si * 512:b_ - si * 512], ones_cb,
                                probsT(kt % 2)[:, a:b_],
                                start=(kt == 0), stop=stp),
                               inc=(pe, 1) if n_ == len(segs) - 1 else None)
                        pb_free[kt % 2] = pe.n

                    pend_pv.append(mk_pv)
                    if len(pend_pv) > 1:
                        pend_pv.pop(0)()
                while pend_pv:
                    pend_pv.pop(0)()
                mpv = pe.n
                ef('dv', lambda e: e.reciprocal(linv_f[:, 0:512], psQ[2][0:1, :]),
                   waits=[(pe, mpv)], inc=(dv, 1))
                ef('dv', lambda e: e.reciprocal(linv_f[:, 512:1024], psQ[3][0:1, :]), inc=(dv, 1))
                ml = dv.n
                for si in range(2):
                    ef('pe', lambda e, si=si: e.matmul(psQ[2 + si][:, :], ones_rf,
                                                       linv_f[:, si * 512:(si + 1) * 512],
                                                       start=True, stop=True),
                       waits=[(dv, ml)], inc=(pe, 1))
                mlb = pe.n
                for si in range(2):
                    ef('sc', lambda e, si=si: e.copy(linv_b[:, si * 512:(si + 1) * 512], psQ[2 + si][:, :]),
                       waits=[(pe, mlb)], inc=(sc, 1))
                l_free[0] = sc.n
                for si in range(2):
                    ef('dv', lambda e, h=h, si=si: e.tensor_tensor(
                        ytile(h)[:, si * 512:(si + 1) * 512], psQ[si][:, :],
                        linv_b[:, si * 512:(si + 1) * 512], ALU.mult),
                       waits=[(sc, l_free[0])], inc=(dv, 1))
                ctx_free[0] = dv.n
                hfree_q[buf] = ctx_free[0]
                hfree_k[buf] = mpv
                hfree_v[vbuf] = mpv
                if h + 2 < NH:
                    load_head(h + 2)
            mattn = dv.n
            oload = [0] * NDK
            ofree = [0] * NDK

            def load_o(dc):
                ef('gp', lambda e, dc=dc: e.dma_start(
                    out=_t3(wtP(dc % 2), 128), in_=_tl(wo_d[:, dc * 128:(dc + 1) * 128])),
                   waits=[(pe, pfree[2 * NDK - 1]),
                          (dv, ofree[dc - 2] if dc >= 2 else 0)], inc=(wP[dc % 2], 16))
                oload[dc] = wP[dc % 2].n

            load_o(0)
            load_o(1)
            for dc in range(NDK):
                for ts in range(2):
                    pX = ps[(dc % 2) * 2 + ts]
                    for hh in range(NH):
                        ef('pe', lambda e, dc=dc, ts=ts, hh=hh, pX=pX: e.matmul(
                            pX[:, :], wtP(dc % 2)[:, hh * 128:(hh + 1) * 128], ytile(hh, ts),
                            start=(hh == 0), stop=(hh == NH - 1)),
                           waits=([(wP[dc % 2], oload[dc]), (dv, mattn), (sc, l_free[0]),
                                   (dv, ofree[dc - 2] if dc >= 2 else 0)] if hh == 0 else []),
                           inc=(pe, 1) if hh == NH - 1 else None)
                    mo = pe.n
                    ef('dv', lambda e, dc=dc, ts=ts, pX=pX: e.tensor_tensor(
                        rtile(dc, ts), pX[:, :], rtile(dc, ts), ALU.add),
                       waits=[(pe, mo)], inc=(dv, 1))
                ofree[dc] = dv.n
                if dc + 2 < NDK:
                    load_o(dc + 2)
            S["resid_dc"] = list(ofree)

        mrho_dead_dv = [0]
        ffn(0)
        mrho_dead_dv[0] = dv.n
        if STAGE >= 2:
            attention()
        if STAGE >= 3:
            ffn(1)

        for dc in range(NDK):
            ef('sy', lambda e, dc=dc: e.dma_start(
                out=out[dc * 128:(dc + 1) * 128, :], in_=rtile(dc)),
               waits=[(dv, S["resid_dc"][dc])], inc=(dOUT, 16))
        ef('sy', lambda e: e.nop(), waits=[(dOUT, dOUT.n), (dv, dv.n), (pe, pe.n), (sc, sc.n)])

        @block.sync
        def _(e):
            P.emit('sy', e)

        @block.gpsimd
        def _(e):
            P.emit('gp', e)

        @block.tensor
        def _(e):
            P.emit('pe', e)

        @block.scalar
        def _(e):
            P.emit('sc', e)

        @block.vector
        def _(e):
            P.emit('dv', e)

    return nc


_NC_CACHE = {}


def _amask(j):
    m = np.zeros((128, 16 * 128), dtype=np.float32)
    for kt in range(16):
        m0 = kt // 2
        qt = 2 * m0 + j
        gk = kt * 128 + np.arange(128)[:, None]
        gq = qt * 128 + np.arange(128)[None, :]
        m[:, kt * 128:(kt + 1) * 128] = (gq >= gk).astype(np.float32)
    return m


def kernel(L0_attn_hidden_states, L0_attn_Wq, L0_attn_bq, L0_attn_Wk, L0_attn_bk,
           L0_attn_Wv, L0_attn_bv, L0_attn_Wo, L0_ffn_gate_w, L0_ffn_w_up,
           L0_ffn_w_gate, L0_ffn_w_down, L0_ffn_up_A, L0_ffn_up_B, L0_ffn_gate_A,
           L0_ffn_gate_B, L0_ffn_down_A, L0_ffn_down_B):
    x = np.ascontiguousarray(np.asarray(L0_attn_hidden_states, dtype=np.float32))
    B = x.shape[0]
    cfg = lambda a: np.ascontiguousarray(np.asarray(a, dtype=np.float32))

    gate_w = cfg(L0_ffn_gate_w)
    gwt = np.zeros((128, 128), dtype=np.float32)
    for dk in range(NDK):
        gwt[:, dk * 8:(dk + 1) * 8] = gate_w[:, dk * 128:(dk + 1) * 128].T
    common = {
        "wq": cfg(L0_attn_Wq), "wk": cfg(L0_attn_Wk), "wv": cfg(L0_attn_Wv),
        "wo": cfg(L0_attn_Wo),
        "bqd": cfg(np.asarray(L0_attn_bq, dtype=np.float32).reshape(NDK, 128).T),
        "bkd": cfg(np.asarray(L0_attn_bk, dtype=np.float32).reshape(NDK, 128).T),
        "bvd": cfg(np.asarray(L0_attn_bv, dtype=np.float32).reshape(1, D)),
        "wupT": cfg(np.asarray(L0_ffn_w_up).T),
        "wgateT": cfg(np.asarray(L0_ffn_w_gate).T),
        "wdownT": cfg(np.asarray(L0_ffn_w_down).T),
        "upAT": cfg(np.asarray(L0_ffn_up_A).reshape(128, D).T),
        "gateAT": cfg(np.asarray(L0_ffn_gate_A).reshape(128, D).T),
        "upBT": cfg(np.asarray(L0_ffn_up_B).transpose(0, 2, 1).reshape(128, H)),
        "gateBT": cfg(np.asarray(L0_ffn_gate_B).transpose(0, 2, 1).reshape(128, H)),
        "dnAT": cfg(np.asarray(L0_ffn_down_A).reshape(128, H).T),
        "dnBT": cfg(np.asarray(L0_ffn_down_B).transpose(0, 2, 1).reshape(128, D)),
        "gwt": gwt,
        "identf": np.eye(128, dtype=np.float32),
        "ones_r": np.ones((1, 128), dtype=np.float32),
        "ones_c": np.ones((128, 1), dtype=np.float32),
        "ecol_d": (np.arange(128, dtype=np.float32) // 16).reshape(128, 1),
        "iota8_d": np.broadcast_to(np.arange(8, dtype=np.float32), (128, 8)).copy(),
    }
    amasks = [_amask(0), _amask(1)]

    if "nc" not in _NC_CACHE:
        _NC_CACHE["nc"] = build_nc()
    nc = _NC_CACHE["nc"]

    in_maps = []
    for c in range(8):
        b, j = c // 2, c % 2
        xt = x[b].reshape(16, 128, D)[j::2].reshape(T, D).T
        m = dict(common)
        m["xs"] = np.ascontiguousarray(xt)
        m["amask_d"] = amasks[j].astype(ml_dtypes.bfloat16)
        in_maps.append(m)

    res = run_bass_kernel_spmd(nc, in_maps, core_ids=list(range(8)))

    y = np.empty_like(x)
    for c in range(8):
        b, j = c // 2, c % 2
        oc = np.asarray(res.results[c]["out"])
        yt = oc.T.reshape(NTC, 128, D)
        for i in range(NTC):
            y[b, (2 * i + j) * 128:(2 * i + j + 1) * 128, :] = yt[i]
    return y
